# version 04
# baseline (speedup 1.0000x reference)
import numpy as np
import ml_dtypes
import concourse.bass as bass
import concourse.mybir as mybir
from concourse.bass_utils import run_bass_kernel_spmd

F32 = mybir.dt.float32
BF16 = mybir.dt.bfloat16
I32 = mybir.dt.int32
AF = mybir.ActivationFunctionType
ALU = mybir.AluOpType
AX = mybir.AxisListType
BF = ml_dtypes.bfloat16

D = 2048
I_FF = 8192
NEG = -1.0e30
SHIFT_A = 24.0
SHIFT_B = 12.0
SC_A = float(1.0 / np.sqrt(np.sqrt(128.0)))
SC_B = float(1.0 / np.sqrt(128.0))
SLOPES4 = [2.0 ** (-0.5 * (i + 1)) for i in range(4)]
KB = 1024

ENGINES = ('G', 'Y', 'T', 'V', 'S')
SEM_OF = dict(T='td', V='vd', S='sd', Y='y_x', G='g_cst')
DMA_SEMS = ('y_x', 'y_k', 'y_v', 'y_g0', 'y_g1', 'y_cv', 'y_cw',
            'g_cst', 'g_dsc', 'g_wc0', 'g_wc1', 'g_wo0', 'g_wo1',
            'g_gb0', 'g_gb1', 'g_ub0', 'g_ub1', 'g_wd0', 'g_wd1')
ALL_SEMS = ('td', 'vd', 'sd', 'cc') + DMA_SEMS


class Sched:
    def __init__(self):
        self.ops = []
        self.cnt = {s: 0 for s in ALL_SEMS}
        self.gate = []
        self.gated = set(ENGINES)
        self.tag = "init"
        self.tags = []

    def begin_phase(self, marks):
        self.gate = [m for m in marks if m is not None and m[1] > 0]
        self.gated = set()

    @staticmethod
    def _flat(waits):
        out = []
        for w in waits:
            if w is None:
                continue
            if isinstance(w, list):
                out.extend(x for x in w if x is not None and x[1] > 0)
            elif w[1] > 0:
                out.append(w)
        return out

    def emit(self, eng, builder, waits=(), sem=None, amt=None):
        waits = self._flat(waits)
        if eng not in self.gated:
            waits = self.gate + waits
            self.gated.add(eng)
        if sem is None:
            sem = SEM_OF[eng]
        if amt is None:
            amt = 16 if sem[:2] in ('y_', 'g_') else 1
        self.ops.append((eng, waits, builder, (sem, amt)))
        self.tags.append(self.tag)
        self.cnt[sem] += amt
        return (sem, self.cnt[sem])

    def emit_wait_only(self, eng, waits):
        self.ops.append((eng, self._flat(waits), None, None))
        self.tags.append(self.tag)

    def mark(self, sem):
        return (sem, self.cnt[sem])

    def barrier_marks(self):
        return [(s, v) for s, v in self.cnt.items() if v > 0 and s != 'cc']


def mmax(*marks):
    best = {}
    for m in marks:
        if m is None:
            continue
        items = m if isinstance(m, list) else [m]
        for it in items:
            if it is None:
                continue
            s, v = it
            best[s] = max(best.get(s, 0), v)
    return [(s, v) for s, v in best.items() if v > 0]


class Builder:
    def __init__(self, n_sub=4):
        nc = bass.Bass("TRN2", target_bir_lowering=False)
        self.nc = nc
        self.n_sub = n_sub
        self.S = Sched()
        di = lambda n, s, d: nc.dram_tensor(n, s, d, kind="ExternalInput")
        self.x_in = di("x_in", [8, 128, D], F32)
        self.out_t = nc.dram_tensor("out", [8, 128, D], F32, kind="ExternalOutput")
        self.a_w = dict(q=di("a_wq", [D, D], F32), k=di("a_wk", [D, D], F32),
                        v=di("a_wv", [D, D], F32), o=di("a_wo", [D, D], F32))
        self.b_w = dict(q=di("b_wq", [D, D], F32), k=di("b_wk", [D, 512], F32),
                        v=di("b_wv", [D, 512], F32), o=di("b_wo", [D, D], F32))
        self.a_b = dict(q=di("a_bq", [D], F32), k=di("a_bk", [D], F32),
                        v=di("a_bv", [D], F32), o=di("a_bo", [D], F32))
        self.g_wt = di("g_wt", [D, I_FF], I32)
        self.u_wt = di("u_wt", [D, I_FF], I32)
        self.d_wt = di("d_wt", [I_FF, D], I32)
        self.gsc_in = di("gsc", [I_FF], F32)
        self.usc_in = di("usc", [I_FF], F32)
        self.dsc_in = di("dsc_b", [128, D], F32)
        self.ident_in = di("ident", [128, 128], BF16)
        self.amask_in = di("amask", [2, 128, 128], BF16)
        self.u2_in = di("b_u2", [6, 2 * 16 * 128], BF16)
        self.v2_in = di("b_v2", [6, 2 * 512], BF16)
        self.bbias_in = di("b_bias", [128, 128], F32)
        self.kvb = {16: nc.dram_tensor("kvb_a", [2, 16, 131072], BF16),
                    4: nc.dram_tensor("kvb_b", [2, 4, 131072], BF16)}
        self.kvg = {16: nc.dram_tensor("kvg_a", [2, 2, 16, 131072], BF16),
                    4: nc.dram_tensor("kvg_b", [2, 2, 4, 131072], BF16)}
        self.gw_bf = nc.dram_tensor("gw_bf", [D * I_FF], BF16)
        self.uw_bf = nc.dram_tensor("uw_bf", [D * I_FF], BF16)
        self.dw_bf = nc.dram_tensor("dw_bf", [I_FF * D], BF16)
        self.ffn_pass = 0
        self.gu_conv = [None] * 32
        self.wd_conv = {}

        total = 200 * KB
        base0, _end = nc.bump_sbuf(total)
        base0 = ((base0 + 511) // 512) * 512
        self.cur = base0
        def sal(name, shape, dt, at=None, align=512):
            esz = 2 if dt == BF16 else 4
            nbytes = int(np.prod(shape[1:])) * esz
            if at is None:
                at = self.cur
                self.cur = at + ((nbytes + align - 1) // align) * align
            return nc.alloc_sbuf_tensor_at(name, shape, dt, offset=at)
        self.xs = [sal(f"x{r}", [128, D], F32) for r in range(8)]
        self.slot1 = sal("slot1", [128, 16 * KB], BF16)
        self.slot2 = sal("slot2", [128, 16 * KB], BF16)
        self.ident = sal("ident_sb", [128, 128], BF16)
        self.st_sum = sal("st_sum", [128, 128], F32)
        self.st_m2 = sal("st_m2", [128, 128], F32)
        self.st_istd = sal("st_istd", [128, 128], F32)
        self.st_nmi = sal("st_nmi", [128, 128], F32)
        self.st_t1 = sal("st_t1", [128, 128], F32)
        self.st_t2 = sal("st_t2", [128, 128], F32)
        self.st_t3 = sal("st_t3", [128, 128], F32)
        self.junk = sal("junk", [128, 128], BF16, align=32)
        self.bqs = sal("bqs", [128, 16], F32, align=32)
        self.bks = sal("bks", [128, 16], F32, align=32)
        self.bvs = sal("bvs", [128, 16], F32, align=32)
        self.gsc_sb = sal("gsc_sb", [128, 64], F32, align=32)
        self.usc_sb = sal("usc_sb", [128, 64], F32, align=32)
        self.ones_col = sal("ones_col", [128, 1], BF16, align=32)
        self.ones_row = sal("ones_row", [1, 128], BF16, align=32)
        self.hneg = sal("hneg", [128, 8], F32, align=32)
        self.rd_sb = sal("rd_sb", [128, 4], F32, align=32)
        self.fc_sb = sal("fc_sb", [128, 2], F32, align=32)
        self.cur = ((self.cur + 511) // 512) * 512
        A = self.cur

        self.ksb = sal("ksb", [128, 4 * D], BF16, at=A)
        self.vsb = sal("vsb", [128, 4 * 16 * 129], BF16, at=A + 16 * KB)
        p1 = A + 33 * KB
        self.pt_sb = sal("pt_sb", [128, 8 * 512], BF16, at=p1)
        self.ktmp = sal("ktmp", [128, KB], BF16, at=p1 + 8 * KB)
        self.vtmp = sal("vtmp", [128, KB], BF16, at=p1 + 10 * KB)
        self.vttmp = sal("vttmp", [128, KB], BF16, at=p1 + 12 * KB)
        self.osb = sal("osb", [128, 512], BF16, at=p1 + 14 * KB)
        self.wcol = [sal(f"wcol{i}", [128, 16 * 128], BF16, at=p1 + 15 * KB + i * 4 * KB)
                     for i in range(2)]
        self.u2_sb = sal("u2_sb", [6, 2 * 16 * 128], BF16, at=p1 + 15 * KB)
        self.v2_sb = sal("v2_sb", [6, 2 * 512], BF16, at=p1 + 23 * KB)
        self.bbias_sb = sal("bbias_sb", [128, 128], F32, at=p1 + 25 * KB)
        self.amask_sb = sal("amask_sb", [128, 2 * 128], BF16, at=p1 + 26 * KB)
        self.bo_row = sal("bo_row", [1, D], BF16, at=p1 + 27 * KB)
        attn_end = p1 + 31 * KB
        self.wocol = [sal(f"wocol{i}", [128, 16 * 512], BF16, at=A + i * 16 * KB) for i in range(2)]
        self.xsq_a = sal("xsq_a", [128, D], F32, at=A)
        self.xlntm_a = sal("xlntm_a", [128, D], BF16, at=A + 8 * KB)

        self.h_extra = sal("h_extra", [128, 16 * KB], BF16, at=A)
        f0 = A + 32 * KB
        self.gbuf = [sal(f"gbuf{i}", [128, 16 * 256], BF16, at=f0 + i * 8 * KB) for i in range(2)]
        self.ubuf = [sal(f"ubuf{i}", [128, 16 * 256], BF16, at=f0 + 16 * KB + i * 8 * KB) for i in range(2)]
        self.wdbuf = [sal(f"wdbuf{i}", [128, 8 * 512], BF16, at=f0 + i * 8 * KB) for i in range(2)]
        self.dsc_sb = sal("dsc_sb", [128, D], F32, at=f0 + 16 * KB)
        self.t1_sb = sal("t1_sb", [128, 512], F32, at=f0 + 24 * KB)
        self.t1b_sb = sal("t1b_sb", [128, 512], F32, at=f0 + 26 * KB + 512)
        f1 = f0 + 32 * KB
        self.sg_sb = sal("sg_sb", [128, 512], BF16, at=f1)
        ffn_end = f1 + KB
        self.xsq_f = sal("xsq_f", [128, D], F32, at=f0)
        self.xlntm_f = sal("xlntm_f", [128, D], BF16, at=f0 + 8 * KB)

        assert max(attn_end, ffn_end) <= base0 + total, (attn_end, ffn_end, base0 + total)
        assert base0 + total <= nc.sbuf_top, (base0, total, nc.sbuf_top)

        self.pb01 = nc.alloc_psum_tensor("pb01", [128, 1024], F32)
        self.pb23 = nc.alloc_psum_tensor("pb23", [128, 1024], F32)
        self.pbav = [nc.alloc_psum_tensor(f"pbav{i}", [128, 512], F32) for i in range(2)]
        self.pbf = nc.alloc_psum_tensor("pbf", [128, 2048], BF16)

        self.bank_free = [None] * 4
        self.av_free = [None, None]
        self.pbf_free = [None, None]
        self.wcol_free = [None, None]
        self.wocol_free = [None, None]
        self.gbuf_free = [None, None]
        self.ubuf_free = [None, None]
        self.pt_free = [None] * 8
        self.x_marks = [None] * 8
        self.xln_ready = None
        self.phase_end = []

    def pbk(self, i, c0=0, c1=512):
        t = self.pb01 if i < 2 else self.pb23
        base = (i % 2) * 512
        return t[:, base + c0: base + c1]

    def T(self, b, waits=()): return self.S.emit('T', b, waits)
    def V(self, b, waits=()): return self.S.emit('V', b, waits)
    def Sc(self, b, waits=()): return self.S.emit('S', b, waits)
    def Y(self, b, waits=(), stream='y_x'): return self.S.emit('Y', b, waits, sem=stream)
    def G(self, b, waits=(), stream='g_cst'): return self.S.emit('G', b, waits, sem=stream)

    def init_loads(self):
        for r in range(8):
            self.Y((lambda r: lambda e: e.dma_start(
                out=self.xs[r][:], in_=self.x_in[r]))(r))
        for r in range(8):
            self.x_marks[r] = self.S.mark('y_x')
        mi = self.G(lambda e: e.dma_start(out=self.ident[:], in_=self.ident_in[:, :]))
        marks = [mi]
        for dst, src in ((self.bqs, self.a_b['q']), (self.bks, self.a_b['k']),
                         (self.bvs, self.a_b['v'])):
            marks.append(self.G((lambda dst, src: lambda e: e.dma_start(
                out=dst[:], in_=bass.AP(src, 0, [[1, 128], [128, 16]]),
                allow_slow_non_contiguous=True))(dst, src)))
        marks.append(self.G(lambda e: e.dma_start(
            out=self.gsc_sb[:], in_=bass.AP(self.gsc_in, 0, [[1, 128], [128, 64]]),
            allow_slow_non_contiguous=True)))
        marks.append(self.G(lambda e: e.dma_start(
            out=self.usc_sb[:], in_=bass.AP(self.usc_in, 0, [[1, 128], [128, 64]]),
            allow_slow_non_contiguous=True)))
        marks.append(self.G(lambda e: e.dma_start(
            out=self.bo_row[:], in_=bass.AP(self.a_b['o'], 0, [[D, 1], [1, D]]))))
        mb = mmax(*marks)
        m1 = self.V(lambda e: e.tensor_scalar_mul(self.bqs[:], self.bqs[:], SC_A), waits=[mb])
        m2 = self.V(lambda e: e.tensor_scalar_mul(self.bks[:], self.bks[:], SC_A), waits=[mb])
        m3 = self.V(lambda e: e.memset(self.ones_col[:], 1.0))
        m4 = self.V(lambda e: e.memset(self.ones_row[:], 1.0))
        m5 = self.V(lambda e: e.memset(self.fc_sb[:, 0:1], -SHIFT_A))
        m6 = self.V(lambda e: e.memset(self.fc_sb[:, 1:2], -SHIFT_B))
        mz = [self.V((lambda t: lambda e: e.memset(t[:], 0.0))(t))
              for t in (self.st_sum, self.st_m2, self.st_istd,
                        self.st_nmi, self.st_t1, self.st_t2, self.st_t3)]
        self.const_ready = mmax(mb, m1, m2, m3, m4, m5, m6, *mz)
        self.ident_ready = mi
        self.phase_end = self.S.barrier_marks()

    def ln(self, for_attn):
        S = self.S
        S.begin_phase(mmax(self.phase_end))
        xsq = self.xsq_a if for_attn else self.xsq_f
        xlntm = self.xlntm_a if for_attn else self.xlntm_f
        slot = self.slot1 if for_attn else self.slot2
        red = []
        prev_v = self.const_ready
        prev_s = None
        for r in range(8):
            w = [self.x_marks[r]]
            prev_v = self.V((lambda r: lambda e: e.reduce_sum(
                self.st_sum[:, r:r + 1], self.xs[r][:], AX.X))(r), waits=w + [prev_v])
            red.append(prev_v)
            prev_s = self.Sc((lambda r, xsq: lambda e: e.activation(
                xsq[:], self.xs[r][:], AF.Square, accum_out=self.st_m2[:, r:r + 1]))(r, xsq),
                waits=w + ([prev_s] if prev_s else []))
            red.append(prev_s)
        mred = mmax(*red)
        self.V(lambda e: e.tensor_copy(self.junk[:], self.ident[:]), waits=[mred, self.ident_ready])
        mA = self.V(lambda e: e.tensor_scalar_mul(self.st_t1[:], self.st_sum[:], 1.0 / D), waits=[mred])
        mB = self.V(lambda e: e.tensor_scalar_mul(self.st_m2[:], self.st_m2[:], 1.0 / D), waits=[mred])
        mC = self.V(lambda e: e.tensor_mul(self.st_t2[:], self.st_t1[:], self.st_t1[:]), waits=[mA])
        mD = self.V(lambda e: e.tensor_sub(self.st_t3[:], self.st_m2[:], self.st_t2[:]), waits=[mB, mC])
        mD2 = self.V(lambda e: e.tensor_scalar_add(self.st_t3[:], self.st_t3[:], 1e-5), waits=[mD])
        mE = self.Sc(lambda e: e.activation(self.st_t2[:], self.st_t3[:], AF.Ln), waits=[mD2])
        mF = self.Sc(lambda e: e.activation(self.st_istd[:], self.st_t2[:], AF.Exp, scale=-0.5), waits=[mE])
        mG = self.V(lambda e: e.scalar_tensor_tensor(
            self.st_nmi[:], self.st_t1[:], -1.0, self.st_istd[:], ALU.mult, ALU.mult), waits=[mF])
        cps = []
        last_tr = [None, None]
        maff = None
        for r in range(8):
            wa = [mG] + [m for m in last_tr if m] + ([maff] if maff else [])
            maff = self.V((lambda r, xlntm: lambda e: e.tensor_scalar(
                xlntm[:], self.xs[r][:], self.st_istd[:, r:r + 1],
                self.st_nmi[:, r:r + 1], ALU.mult, ALU.add))(r, xlntm), waits=wa)
            for dt in range(16):
                half = dt % 2
                wt = [maff, self.ident_ready, self.pbf_free[half]]
                mt = self.T((lambda dt, half, xlntm: lambda e: e.transpose(
                    self.pbf[:, half * 1024:half * 1024 + 128],
                    xlntm[:, dt * 128:(dt + 1) * 128], self.ident[:]))(dt, half, xlntm), waits=wt)
                if dt >= 14:
                    last_tr[half] = mt
                mc = self.V((lambda r, dt, half, slot: lambda e: e.tensor_copy(
                    slot[:, dt * KB + r * 128: dt * KB + (r + 1) * 128],
                    self.pbf[:, half * 1024:half * 1024 + 128]))(r, dt, half, slot), waits=[mt])
                self.pbf_free[half] = mc
                cps.append(mc)
        self.xln_ready = mmax(*cps)
        self.phase_end = self.S.barrier_marks()

    def xlnT(self, for_attn, dt):
        slot = self.slot1 if for_attn else self.slot2
        return slot[:, dt * KB:(dt + 1) * KB]

    def qt_ot(self, h):
        return self.slot2[:, h * KB:(h + 1) * KB]

    def h_tile(self, it):
        if it < 32:
            return self.slot1[:, it * 512:(it + 1) * 512]
        return self.h_extra[:, (it - 32) * 512:(it - 31) * 512]

    def _proj16(self, wh, E, et, buf, dest_cb, scale, bias_col, for_attn=True):
        mload = self.G((lambda buf, wh, E, et: lambda e: e.dma_start(
            out=self.wcol[buf][:],
            in_=bass.AP(wh, et * 128, [[E, 128], [128 * E, 16], [1, 128]])))(buf, wh, E, et),
            waits=[self.wcol_free[buf]], stream=f'g_wc{buf}')
        mm = None
        evs = []
        for qc in range(2):
            bank = qc
            wt = [mload, self.xln_ready, self.bank_free[bank]]
            for dt in range(16):
                mm = self.T((lambda buf, dt, bank, qc: lambda e: e.matmul(
                    self.pbk(bank), self.wcol[buf][:, dt * 128:(dt + 1) * 128],
                    self.xlnT(True, dt)[:, qc * 512:(qc + 1) * 512],
                    start=(dt == 0), stop=(dt == 15)))(buf, dt, bank, qc),
                    waits=wt if dt == 0 else ())
            me = dest_cb(qc, bank, mm)
            self.bank_free[bank] = me
            evs.append(me)
        self.wcol_free[buf] = mm
        return evs

    def attn(self, variant):
        S = self.S
        a = variant == 'a'
        nKV = 16 if a else 4
        W = self.a_w if a else self.b_w
        kvb, kvg = self.kvb[nKV], self.kvg[nKV]
        E_kv = D if a else 512
        qscale = SC_A if a else SC_B
        kscale = SC_A if a else 1.0
        shift = SHIFT_A if a else SHIFT_B

        S.begin_phase(mmax(self.phase_end))
        mconst = self.G(lambda e: e.dma_start(
            out=self.amask_sb[:],
            in_=bass.AP(self.amask_in, 0, [[128, 128], [16384, 2], [1, 128]])))
        if not a:
            mconst = mmax(mconst, self.G(lambda e: e.dma_start(
                out=self.bbias_sb[:], in_=self.bbias_in[:, :])))
        m_ones = self.V(lambda e: e.memset(
            bass.AP(self.vsb, 128, [[4 * 16 * 129, 128], [129, 64], [1, 1]]), 1.0))

        base_tag = self.S.tag
        self.S.tag = base_tag + ":kv"
        wi = 0
        store_marks = []
        ktmp_free = None
        vttmp_free = None
        for kv in range(nKV):
            def k_dest(qc, bank, mm, kv=kv):
                if a:
                    bld = (lambda bank, qc, kv: lambda e: e.tensor_scalar(
                        self.ktmp[:, qc * 512:(qc + 1) * 512], self.pbk(bank), kscale,
                        self.bks[:, kv:kv + 1], ALU.mult, ALU.add))(bank, qc, kv)
                else:
                    bld = (lambda bank, qc: lambda e: e.tensor_scalar(
                        self.ktmp[:, qc * 512:(qc + 1) * 512], self.pbk(bank), kscale,
                        None, ALU.mult))(bank, qc)
                return self.V(bld, waits=[mm, ktmp_free, self.const_ready])
            evk = self._proj16(W['k'], E_kv, kv, wi % 2, k_dest, kscale, kv)
            wi += 1
            ms = self.Y((lambda kv: lambda e: e.dma_start(
                out=bass.AP(kvb, kv * 131072, [[1024, 128], [1, 1024]]),
                in_=self.ktmp[:]))(kv), waits=evk, stream='y_k')
            ktmp_free = ms
            store_marks.append(ms)

            def v_dest(qc, bank, mm, kv=kv):
                if a:
                    bld = (lambda bank, qc, kv: lambda e: e.tensor_scalar(
                        self.vtmp[:, qc * 512:(qc + 1) * 512], self.pbk(bank), 1.0,
                        self.bvs[:, kv:kv + 1], ALU.mult, ALU.add))(bank, qc, kv)
                else:
                    bld = (lambda bank, qc: lambda e: e.tensor_copy(
                        self.vtmp[:, qc * 512:(qc + 1) * 512], self.pbk(bank)))(bank, qc)
                return self.V(bld, waits=[mm, self.const_ready])
            evv = self._proj16(W['v'], E_kv, kv, wi % 2, v_dest, 1.0, kv)
            wi += 1
            vt = []
            for j in range(8):
                half = j % 2
                wt = [mmax(*evv), self.pbf_free[half], self.ident_ready]
                mt = self.T((lambda j, half: lambda e: e.transpose(
                    self.pbf[:, half * 1024:half * 1024 + 128],
                    self.vtmp[:, j * 128:(j + 1) * 128], self.ident[:]))(j, half), waits=wt)
                mc = self.V((lambda j, half: lambda e: e.tensor_copy(
                    self.vttmp[:, j * 128:(j + 1) * 128],
                    self.pbf[:, half * 1024:half * 1024 + 128]))(j, half),
                    waits=[mt, vttmp_free])
                self.pbf_free[half] = mc
                vt.append(mc)
            ms = self.Y((lambda kv, nKV: lambda e: e.dma_start(
                out=bass.AP(kvb, (nKV + kv) * 131072, [[128, 128], [16384, 8], [1, 128]]),
                in_=self.vttmp[:].rearrange("p (j d) -> p j d", j=8)))(kv, nKV),
                waits=[mmax(*vt)], stream='y_v')
            vttmp_free = ms
            store_marks.append(ms)

        mcc = S.emit('G', (lambda kvb, kvg: lambda e: e.collective_compute(
            "AllGather", ALU.bypass, replica_groups=[[0, 1], [2, 3], [4, 5], [6, 7]],
            ins=[kvb.ap()], outs=[kvg.ap()]))(kvb, kvg),
            waits=[mmax(*store_marks)], sem='cc', amt=1)

        self.S.tag = base_tag + ":q"
        qev = []
        for et in range(16):
            def q_dest(qc, bank, mm, et=et):
                if a:
                    bld = (lambda bank, qc, et: lambda e: e.tensor_scalar(
                        self.qt_ot(et)[:, qc * 512:(qc + 1) * 512], self.pbk(bank), qscale,
                        self.bqs[:, et:et + 1], ALU.mult, ALU.add))(bank, qc, et)
                else:
                    bld = (lambda bank, qc, et: lambda e: e.tensor_scalar(
                        self.qt_ot(et)[:, qc * 512:(qc + 1) * 512], self.pbk(bank), qscale,
                        None, ALU.mult))(bank, qc, et)
                return self.V(bld, waits=[mm, self.const_ready])
            qev += self._proj16(W['q'], D, et, wi % 2, q_dest, qscale, et)
            wi += 1
        q_ready = mmax(*qev)
        if not a:
            self.G(lambda e: e.dma_start(out=self.u2_sb[:], in_=self.u2_in[:, :]),
                   waits=[self.wcol_free[0], self.wcol_free[1]])
            self.G(lambda e: e.dma_start(out=self.v2_sb[:], in_=self.v2_in[:, :]),
                   waits=())
            alibi_ready = self.S.mark('g_cst')
            mconst = alibi_ready
            self.wcol_free = [None, None]
        else:
            alibi_ready = None

        grp_cons = {}
        ot_marks = []
        self.S.tag = base_tag + ":heads"
        for hg in range(4):
            par = hg % 2
            wl = [mcc, m_ones, grp_cons.get(hg - 1 if a else hg - 2)]
            xfers = []
            RNK = 2 * nKV * 131072
            if a:
                for r in range(2):
                    xfers.append((
                        bass.AP(self.ksb, r * 1024, [[4 * D, 128], [2048, 4], [1, 1024]]),
                        bass.AP(kvg, r * RNK + 4 * hg * 131072,
                                [[1024, 128], [131072, 4], [1, 1024]])))
                    for hi in range(4):
                        xfers.append((
                            bass.AP(self.vsb, hi * 16 * 129 + r * 8 * 129,
                                    [[4 * 16 * 129, 128], [129, 8], [1, 128]]),
                            bass.AP(kvg, r * RNK + (nKV + 4 * hg + hi) * 131072,
                                    [[128, 128], [16384, 8], [1, 128]])))
            else:
                for r in range(2):
                    xfers.append((
                        bass.AP(self.ksb, par * 2048 + r * 1024, [[4 * D, 128], [1, 1024]]),
                        bass.AP(kvg, r * RNK + hg * 131072, [[1024, 128], [1, 1024]])))
                    xfers.append((
                        bass.AP(self.vsb, par * 16 * 129 + r * 8 * 129,
                                [[4 * 16 * 129, 128], [129, 8], [1, 128]]),
                        bass.AP(kvg, r * RNK + (nKV + hg) * 131072,
                                [[128, 128], [16384, 8], [1, 128]])))
            lms = [self.Y((lambda dd, ss: lambda e: e.dma_start(out=dd, in_=ss))(dd, ss),
                          waits=wl, stream=f'y_g{hg % 2}')
                   for dd, ss in xfers]
            gl = mmax(*lms)
            cons = []
            for hi in range(4):
                h = 4 * hg + hi
                kcol = hi * 2048 if a else par * 2048
                vbase = hi * 16 * 129 if a else par * 16 * 129
                for qc in range(2):
                    issued = []
                    for g in range(4 * qc + 4):
                        issued += [g, g + 8]
                    nI = len(issued)
                    nP = nI // 2
                    gbs = [t % 8 for t in issued]
                    mex = {}
                    av_sched = [(idx, b) for idx in range(nI)
                                for b in range(max(0, gbs[idx] - 4 * qc), 4)]
                    bank_first = {}
                    bank_last = {}
                    for pos, (idx, b) in enumerate(av_sched):
                        j = b // 2
                        if j not in bank_first:
                            bank_first[j] = (idx, b)
                        bank_last[j] = (idx, b)
                    av_stop_bank = [None, None]

                    def do_scores_pair(p):
                        bp = p % 2
                        tens = self.pb01 if bp == 0 else self.pb23
                        mprevs = []
                        t = issued[2 * p]
                        gb = t % 8
                        b0 = max(0, gb - 4 * qc)
                        nq = 512 - b0 * 128
                        for k in (0, 1):
                            idx = 2 * p + k
                            t = issued[idx]
                            rk = t // 8
                            bank = 2 * bp + k
                            wt = [gl, q_ready, self.bank_free[bank]]
                            if not a:
                                wt.append(alibi_ready)
                            mm = self.T((lambda qc, t, bank, kcol, h, b0, nq: lambda e: e.matmul(
                                self.pbk(bank, b0 * 128, b0 * 128 + nq),
                                self.ksb[:, kcol + (t // 8) * 1024 + (t % 8) * 128:
                                         kcol + (t // 8) * 1024 + (t % 8) * 128 + 128],
                                self.qt_ot(h)[:, qc * 512 + b0 * 128:(qc + 1) * 512],
                                start=True, stop=a))(qc, t, bank, kcol, h, b0, nq), waits=wt)
                            if not a:
                                ucol = (rk * 16 + h) * 128
                                mm = self.T((lambda qc, bank, ucol, b0, nq: lambda e: e.matmul(
                                    self.pbk(bank, b0 * 128, b0 * 128 + nq),
                                    self.u2_sb[:, ucol: ucol + 128],
                                    self.v2_sb[:, qc * 512 + b0 * 128:(qc + 1) * 512],
                                    start=False, stop=True))(qc, bank, ucol, b0, nq), waits=())
                            cons.append(mm)
                            mprev = mm
                            b_blk = gb - 4 * qc
                            if 0 <= b_blk <= 3:
                                mprev = self.V((lambda bank, rk, b_blk: lambda e: e.tensor_add(
                                    self.pbk(bank, b_blk * 128, (b_blk + 1) * 128),
                                    self.pbk(bank, b_blk * 128, (b_blk + 1) * 128),
                                    self.amask_sb[:, rk * 128:(rk + 1) * 128]))(bank, rk, b_blk),
                                    waits=[mm, mconst])
                            mprevs.append(mprev)
                        psA = (2 * p) % 8
                        if a:
                            bias_ap = self.fc_sb[:, 0:1]
                        else:
                            bias_ap = self.bbias_sb[:, h * 8 + gb: h * 8 + gb + 1]
                        m = self.Sc((lambda tens, bp, psA, bias_ap, b0, nq: lambda e: e.activation(
                            bass.AP(self.pt_sb, psA * 512 + b0 * 128, [[4096, 128], [512, 2], [1, nq]]),
                            bass.AP(tens, b0 * 128, [[1024, 128], [512, 2], [1, nq]]),
                            AF.Exp, bias=bias_ap))(tens, bp, psA, bias_ap, b0, nq),
                            waits=mprevs + [self.pt_free[psA], self.pt_free[psA + 1],
                                            self.const_ready, mconst])
                        self.bank_free[2 * bp] = m
                        self.bank_free[2 * bp + 1] = m
                        mex[p] = m

                    def do_av_pair(p):
                        for k in (0, 1):
                            idx = 2 * p + k
                            t = issued[idx]
                            gb = t % 8
                            ps = idx % 8
                            last_av = None
                            for b in range(max(0, gb - 4 * qc), 4):
                                j = b // 2
                                col0 = (b % 2) * 256
                                first = bank_first[j] == (idx, b)
                                last = bank_last[j] == (idx, b)
                                wt = [mex[p]]
                                if first:
                                    wt.append(self.av_free[j])
                                last_av = self.T((lambda j, col0, ps, vbase, t, first, last, b:
                                                  lambda e: e.matmul(
                                    self.pbav[j][:, col0: col0 + 129],
                                    self.pt_sb[:, ps * 512 + b * 128: ps * 512 + (b + 1) * 128],
                                    self.vsb[:, vbase + t * 129: vbase + (t + 1) * 129],
                                    start=first, stop=last))(j, col0, ps, vbase, t, first, last, b),
                                    waits=wt)
                                if last:
                                    av_stop_bank[j] = last_av
                            self.pt_free[ps] = last_av
                            cons.append(last_av)

                    for p in range(nP):
                        do_scores_pair(p)
                        if p >= 1:
                            do_av_pair(p - 1)
                    do_av_pair(nP - 1)
                    for b in range(4):
                        j, col0 = b // 2, (b % 2) * 256
                        self.V((lambda b, j, col0: lambda e: e.reciprocal(
                            self.rd_sb[:, b:b + 1], self.pbav[j][:, col0 + 128: col0 + 129]))(b, j, col0),
                            waits=[av_stop_bank[j]])
                    mj = self.V(lambda e: e.tensor_copy(self.junk[:], self.ident[:]),
                                waits=[self.ident_ready])
                    mos = [None, None]
                    for b in range(4):
                        j, col0 = b // 2, (b % 2) * 256
                        mo = self.V((lambda b, j, col0: lambda e: e.tensor_scalar(
                            self.osb[:, b * 128:(b + 1) * 128], self.pbav[j][:, col0: col0 + 128],
                            self.rd_sb[:, b:b + 1], None, ALU.mult))(b, j, col0), waits=[mj])
                        mos[j] = mo
                        half = b % 2
                        mt = self.T((lambda b, half: lambda e: e.transpose(
                            self.pbf[:, half * 1024:half * 1024 + 128],
                            self.osb[:, b * 128:(b + 1) * 128], self.ident[:]))(b, half),
                            waits=[mo, self.ident_ready, self.pbf_free[half]])
                        mc = self.V((lambda h, qc, b, half: lambda e: e.tensor_copy(
                            self.qt_ot(h)[:, qc * 512 + b * 128: qc * 512 + (b + 1) * 128],
                            self.pbf[:, half * 1024:half * 1024 + 128]))(h, qc, b, half), waits=[mt])
                        self.pbf_free[half] = mc
                        ot_marks.append(mc)
                    self.av_free = [mos[0], mos[1]]
            grp_cons[hg] = mmax(*cons)
        ot_ready = mmax(*ot_marks)

        self.S.tag = base_tag + ":proj"
        Wo = W['o']
        for ech in range(4):
            wf = [self.wocol_free[ech % 2]]
            if ech < 2:
                wf.append(grp_cons[2 + ech])
                wf.append(grp_cons[3])
            mw = self.G((lambda ech, Wo: lambda e: e.dma_start(
                out=self.wocol[ech % 2][:],
                in_=bass.AP(Wo, ech * 512, [[D, 128], [128 * D, 16], [1, 512]])))(ech, Wo),
                waits=wf, stream=f'g_wo{ech % 2}')
            mmlast = None
            for qt in range(8):
                bank = qt % 4
                wt = [mw, ot_ready, self.bank_free[bank]]
                if a:
                    wt.append(self.const_ready)
                mm = None
                for dt in range(16):
                    stop = (dt == 15) and not a
                    mm = self.T((lambda ech, qt, dt, bank, stop: lambda e: e.matmul(
                        self.pbk(bank), self.qt_ot(dt)[:, qt * 128:(qt + 1) * 128],
                        self.wocol[ech % 2][:, dt * 512:(dt + 1) * 512],
                        start=(dt == 0), stop=stop))(ech, qt, dt, bank, stop),
                        waits=wt if dt == 0 else ())
                if a:
                    mm = self.T((lambda ech, bank: lambda e: e.matmul(
                        self.pbk(bank), self.ones_row[:, 0:128],
                        self.bo_row[:, ech * 512:(ech + 1) * 512],
                        start=False, stop=True))(ech, bank), waits=())
                mmlast = mm
                mx = self.V((lambda qt, ech, bank: lambda e: e.tensor_add(
                    self.xs[qt][:, ech * 512:(ech + 1) * 512],
                    self.xs[qt][:, ech * 512:(ech + 1) * 512],
                    self.pbk(bank)))(qt, ech, bank), waits=[mm, self.x_marks[qt]])
                self.bank_free[bank] = mx
                self.x_marks[qt] = mx
            self.wocol_free[ech % 2] = mmlast
        self.phase_end = self.S.barrier_marks()

    def ffn(self):
        S = self.S
        S.begin_phase(mmax(self.phase_end))
        sg_free = None
        for qc in range(2):
            hm = [None] * 64
            hsum_stop = [None] * 4

            def emit_hsum(it):
                for qt in range(4):
                    j, col0 = qt // 2, (qt % 2) * 256
                    wt2 = [hm[it]]
                    if it == 0:
                        wt2 += [self.av_free[j], self.const_ready]
                    hsum_stop[qt] = self.T((lambda it, qt, j, col0: lambda e: e.matmul(
                        self.pbav[j][:, col0: col0 + 1],
                        self.h_tile(it)[:, qt * 128:(qt + 1) * 128],
                        self.ones_col[:],
                        start=(it == 0 and qt % 2 == 0),
                        stop=(it == 63 and qt % 2 == 1)))(it, qt, j, col0), waits=wt2)

            first = (self.ffn_pass == 0 and qc == 0)
            base_tag = self.S.tag.split(":gu")[0].split(":dn")[0]
            self.S.tag = base_tag + f":gu{qc}"
            for ip in range(32):
                buf = ip % 2
                if first:
                    mgl = self.G((lambda buf, ip: lambda e: e.dma_start(
                        out=self.gbuf[buf][:],
                        in_=bass.AP(self.g_wt, ip * 256, [[I_FF, 128], [128 * I_FF, 16], [1, 256]])))(buf, ip),
                        waits=[self.gbuf_free[buf]], stream=f'g_gb{buf}')
                    mul_ = self.G((lambda buf, ip: lambda e: e.dma_start(
                        out=self.ubuf[buf][:],
                        in_=bass.AP(self.u_wt, ip * 256, [[I_FF, 128], [128 * I_FF, 16], [1, 256]])))(buf, ip),
                        waits=[self.ubuf_free[buf]], stream=f'g_ub{buf}')
                    sg1 = self.Y((lambda buf, ip: lambda e: e.dma_start(
                        out=bass.AP(self.gw_bf, ip * 524288, [[4096, 128], [1, 4096]]),
                        in_=self.gbuf[buf][:]))(buf, ip), waits=[mgl], stream='y_cv')
                    sg2 = self.Y((lambda buf, ip: lambda e: e.dma_start(
                        out=bass.AP(self.uw_bf, ip * 524288, [[4096, 128], [1, 4096]]),
                        in_=self.ubuf[buf][:]))(buf, ip), waits=[mul_], stream='y_cv')
                    self.gu_conv[ip] = (sg1, sg2)
                else:
                    cvt = self.S.mark('y_cv')
                    mgl = self.G((lambda buf, ip: lambda e: e.dma_start(
                        out=self.gbuf[buf][:],
                        in_=bass.AP(self.gw_bf, ip * 524288, [[4096, 128], [1, 4096]])))(buf, ip),
                        waits=[self.gbuf_free[buf], cvt], stream=f'g_gb{buf}')
                    mul_ = self.G((lambda buf, ip: lambda e: e.dma_start(
                        out=self.ubuf[buf][:],
                        in_=bass.AP(self.uw_bf, ip * 524288, [[4096, 128], [1, 4096]])))(buf, ip),
                        waits=[self.ubuf_free[buf], cvt], stream=f'g_ub{buf}')
                glast = ulast = None
                for k in range(2):
                    it = ip * 2 + k
                    wt = [mgl, self.xln_ready, self.bank_free[0]]
                    for dt in range(16):
                        glast = self.T((lambda buf, dt, k, qc: lambda e: e.matmul(
                            self.pbk(0),
                            self.gbuf[buf][:, dt * 256 + k * 128: dt * 256 + (k + 1) * 128],
                            self.xlnT(False, dt)[:, qc * 512:(qc + 1) * 512],
                            start=(dt == 0), stop=(dt == 15)))(buf, dt, k, qc),
                            waits=wt if dt == 0 else ())
                    wt = [mul_, self.xln_ready, self.bank_free[1]]
                    for dt in range(16):
                        ulast = self.T((lambda buf, dt, k, qc: lambda e: e.matmul(
                            self.pbk(1),
                            self.ubuf[buf][:, dt * 256 + k * 128: dt * 256 + (k + 1) * 128],
                            self.xlnT(False, dt)[:, qc * 512:(qc + 1) * 512],
                            start=(dt == 0), stop=(dt == 15)))(buf, dt, k, qc),
                            waits=wt if dt == 0 else ())
                    msg = self.Sc((lambda it: lambda e: e.activation(
                        self.sg_sb[:], self.pbk(0), AF.Silu,
                        scale=self.gsc_sb[:, it:it + 1]))(it), waits=[glast, sg_free])
                    self.bank_free[0] = msg
                    mh = self.V((lambda it: lambda e: e.scalar_tensor_tensor(
                        self.h_tile(it), self.pbk(1), self.usc_sb[:, it:it + 1],
                        self.sg_sb[:], ALU.mult, ALU.mult))(it), waits=[ulast, msg])
                    self.bank_free[1] = mh
                    sg_free = mh
                    hm[it] = mh
                    if it >= 1:
                        emit_hsum(it - 1)
                self.gbuf_free[buf] = mmax(glast, self.gu_conv[ip][0] if first else None)
                self.ubuf_free[buf] = mmax(ulast, self.gu_conv[ip][1] if first else None)
            emit_hsum(63)
            hneg_m = []
            for qt in range(4):
                j, col0 = qt // 2, (qt % 2) * 256
                hneg_m.append(self.Sc((lambda qc, qt, j, col0: lambda e: e.mul(
                    self.hneg[:, qc * 4 + qt: qc * 4 + qt + 1],
                    self.pbav[j][:, col0: col0 + 1], -128.0))(qc, qt, j, col0),
                    waits=[hsum_stop[2 * j + 1]]))
            self.av_free = [hneg_m[1], hneg_m[3]]
            mhneg = mmax(*hneg_m)
            h_ready = mmax(*hm)
            self.S.tag = base_tag + f":dn{qc}"
            mdsc = self.G(lambda e: e.dma_start(out=self.dsc_sb[:], in_=self.dsc_in[:, :]),
                          waits=[h_ready, self.ubuf_free[0]], stream='g_dsc')
            wdbuf_free = [mmax(self.gbuf_free[0], h_ready), mmax(self.gbuf_free[1], h_ready)]
            t1_free = self.ubuf_free[1]
            t1b_free = self.ubuf_free[1]
            for ech in range(4):
                stops = [None] * 4
                for ic in range(8):
                    buf = ic % 2
                    if first:
                        mwl = self.G((lambda buf, ic, ech: lambda e: e.dma_start(
                            out=self.wdbuf[buf][:],
                            in_=bass.AP(self.d_wt, (ic * 8 * 128) * D + ech * 512,
                                        [[D, 128], [128 * D, 8], [1, 512]])))(buf, ic, ech),
                            waits=[wdbuf_free[buf]], stream=f'g_wd{buf}')
                        scv = self.Y((lambda buf, ic, ech: lambda e: e.dma_start(
                            out=bass.AP(self.dw_bf, (ech * 8 + ic) * 524288,
                                        [[4096, 128], [1, 4096]]),
                            in_=self.wdbuf[buf][:]))(buf, ic, ech), waits=[mwl], stream='y_cw')
                        self.wd_conv[(ech, ic)] = scv
                    else:
                        cvw = self.S.mark('y_cw')
                        mwl = self.G((lambda buf, ic, ech: lambda e: e.dma_start(
                            out=self.wdbuf[buf][:],
                            in_=bass.AP(self.dw_bf, (ech * 8 + ic) * 524288,
                                        [[4096, 128], [1, 4096]])))(buf, ic, ech),
                            waits=[wdbuf_free[buf], cvw], stream=f'g_wd{buf}')
                    for i8 in range(8):
                        it = ic * 8 + i8
                        for qt in range(4):
                            if it == 0:
                                wt = [mwl, h_ready, mhneg, self.bank_free[qt]]
                            elif i8 == 0:
                                wt = [mwl]
                            else:
                                wt = ()
                            stops[qt] = self.T((lambda buf, it, i8, qt: lambda e: e.matmul(
                                self.pbk(qt), self.h_tile(it)[:, qt * 128:(qt + 1) * 128],
                                self.wdbuf[buf][:, i8 * 512:(i8 + 1) * 512],
                                start=(it == 0), stop=(it == 63)))(buf, it, i8, qt),
                                waits=wt)
                    wdbuf_free[buf] = mmax(stops[3], self.wd_conv.get((ech, ic)) if first else None)
                for qt in range(4):
                    rt = qc * 4 + qt
                    m1 = self.Sc((lambda qt, qc: lambda e: e.add(
                        self.t1_sb[:], self.pbk(qt),
                        self.hneg[:, qc * 4 + qt: qc * 4 + qt + 1]))(qt, qc),
                        waits=[stops[qt], mhneg, t1_free])
                    self.bank_free[qt] = m1
                    m2 = self.V((lambda ech: lambda e: e.tensor_mul(
                        self.t1b_sb[:], self.t1_sb[:],
                        self.dsc_sb[:, ech * 512:(ech + 1) * 512]))(ech),
                        waits=[m1, mdsc, t1b_free])
                    t1_free = m2
                    m3 = self.V((lambda rt, ech: lambda e: e.tensor_add(
                        self.xs[rt][:, ech * 512:(ech + 1) * 512],
                        self.xs[rt][:, ech * 512:(ech + 1) * 512],
                        self.t1b_sb[:]))(rt, ech), waits=[m2, self.x_marks[rt]])
                    t1b_free = m3
                    self.x_marks[rt] = m3
            self.gbuf_free = [wdbuf_free[0], wdbuf_free[1]]
            self.ubuf_free = [mmax(t1_free), mmax(t1_free, t1b_free)]
        self.ffn_pass += 1
        self.phase_end = self.S.barrier_marks()

    def finish(self):
        self.S.begin_phase(mmax(self.phase_end))
        for r in range(8):
            self.Y((lambda r: lambda e: e.dma_start(
                out=self.out_t[r], in_=self.xs[r][:]))(r), waits=[self.x_marks[r]])
        self.S.emit_wait_only('Y', [self.S.mark('y_x')])

    def realize(self):
        nc, S = self.nc, self.S
        import contextlib
        with contextlib.ExitStack() as st:
            sems = {s: st.enter_context(nc.semaphore(s)) for s in ALL_SEMS}
            blk = st.enter_context(nc.Block())
            attr = {'T': 'tensor', 'V': 'vector', 'S': 'scalar', 'Y': 'sync', 'G': 'gpsimd'}
            for ename in ENGINES:
                def runner(eng, ename=ename):
                    hiwater = {}
                    for (e2, waits, builder, inc) in S.ops:
                        if e2 != ename:
                            continue
                        for (s, v) in waits:
                            if hiwater.get(s, -1) >= v:
                                continue
                            hiwater[s] = v
                            eng.wait_ge(sems[s], v)
                        if builder is None:
                            continue
                        ins = builder(eng)
                        if inc is not None:
                            ins.then_inc(sems[inc[0]], inc[1])
                getattr(blk, attr[ename])(runner)
        return nc


def build(n_sub=4):
    b = Builder(n_sub)
    b.init_loads()
    steps = [('ln_a',), ('attn', 'a'), ('ln_f',), ('ffn',), ('ln_a',), ('attn', 'b'),
             ('ln_f',), ('ffn',)][: 2 * n_sub]
    for i, st in enumerate(steps):
        b.S.tag = f"{i}:{st[0]}" + (st[1] if len(st) > 1 else "")
        if st[0] == 'ln_a':
            b.ln(True)
        elif st[0] == 'ln_f':
            b.ln(False)
        elif st[0] == 'attn':
            b.attn(st[1])
        else:
            b.ffn()
    b.finish()
    return b.realize()


def host_consts(P):
    d = {}
    d["ident"] = np.eye(128, dtype=BF)
    kp = np.arange(128)[:, None]
    qp = np.arange(128)[None, :]
    am = np.zeros((2, 128, 128), np.float32)
    for rk in range(2):
        am[rk] = np.where((2 * kp + rk) <= (2 * qp + P), 0.0, NEG)
    d["amask"] = am.astype(BF)
    slopes = np.repeat([2.0 ** (-0.5 * (i + 1)) for i in range(4)], 4)[:16]

    def hilo(x):
        hi = x.astype(BF).astype(np.float32)
        lo = (x - hi).astype(BF).astype(np.float32)
        return hi, lo

    u2t = np.zeros((6, 2 * 16 * 128), np.float32)
    for rk in range(2):
        for h in range(16):
            c = (rk * 16 + h) * 128
            u1h, u1l = hilo(slopes[h] * (2.0 * np.arange(128, dtype=np.float64) + rk))
            s2h, s2l = hilo(np.full(128, slopes[h], np.float64))
            u2t[0, c:c + 128] = u1h
            u2t[1, c:c + 128] = u1l
            u2t[2, c:c + 128] = s2h
            u2t[3, c:c + 128] = s2h
            u2t[4, c:c + 128] = s2l
            u2t[5, c:c + 128] = s2l
    d["b_u2"] = u2t.astype(BF)
    v2 = np.zeros((6, 2 * 512), np.float32)
    v2[0, :] = 1.0
    v2[1, :] = 1.0
    for qc in range(2):
        q = np.arange(512)
        w2 = -(256.0 * (4 * qc + q // 128) + 2.0 * (q % 128) + P)
        wh, wl = hilo(w2.astype(np.float64))
        v2[2, qc * 512:(qc + 1) * 512] = wh
        v2[3, qc * 512:(qc + 1) * 512] = wl
        v2[4, qc * 512:(qc + 1) * 512] = wh
        v2[5, qc * 512:(qc + 1) * 512] = wl
    d["b_v2"] = v2.astype(BF)
    bb = np.zeros((128, 128), np.float32)
    for h in range(16):
        for gb in range(8):
            bb[:, h * 8 + gb] = -SHIFT_B + slopes[h] * 256.0 * gb
    d["b_bias"] = bb
    return d


def host_weights(z):
    d = {}
    d["a_wq"] = np.ascontiguousarray(z["L0_attn_a_Wq"].T)
    d["a_wk"] = np.ascontiguousarray(z["L0_attn_a_Wk"].T)
    d["a_wv"] = np.ascontiguousarray(z["L0_attn_a_Wv"].T)
    d["a_wo"] = np.ascontiguousarray(z["L0_attn_a_Wo"].T)
    d["a_bq"] = z["L0_attn_a_bq"]
    d["a_bk"] = z["L0_attn_a_bk"]
    d["a_bv"] = z["L0_attn_a_bv"]
    d["a_bo"] = z["L0_attn_a_bo"]
    d["b_wq"] = z["L0_attn_b_Wq"]
    d["b_wk"] = z["L0_attn_b_Wk"]
    d["b_wv"] = z["L0_attn_b_Wv"]
    d["b_wo"] = z["L0_attn_b_Wo"]
    d["g_wt"] = np.ascontiguousarray(z["L0_ffn_gate_w"].T)
    d["u_wt"] = np.ascontiguousarray(z["L0_ffn_up_w"].T)
    d["d_wt"] = np.ascontiguousarray(z["L0_ffn_down_w"].T)
    d["gsc"] = z["L0_ffn_gate_scale"]
    d["usc"] = z["L0_ffn_up_scale"]
    d["dsc_b"] = np.ascontiguousarray(np.broadcast_to(z["L0_ffn_down_scale"], (128, D)))
    return d


_NC_CACHE = {}


def run(inputs, n_sub=4):
    if n_sub not in _NC_CACHE:
        _NC_CACHE[n_sub] = build(n_sub)
    nc = _NC_CACHE[n_sub]
    X = inputs["L0_attn_a_X"]
    w = host_weights(inputs)
    consts = [host_consts(0), host_consts(1)]
    in_maps = []
    for c in range(8):
        b, P = c // 2, c % 2
        d = dict(w)
        d.update(consts[P])
        d["x_in"] = np.ascontiguousarray(X[b].reshape(8, 128, 2, D)[:, :, P])
        in_maps.append(d)
    res = run_bass_kernel_spmd(nc, in_maps, core_ids=list(range(8)))
    out = np.empty((4, 2048, D), np.float32)
    ov = out.reshape(4, 8, 128, 2, D)
    for c in range(8):
        b, P = c // 2, c % 2
        ov[b, :, :, P] = res.results[c]["out"]
    return out


def kernel(**inputs):
    return run(inputs, n_sub=4)


# version 06
# speedup vs baseline: 1.0673x; 1.0673x over previous
import numpy as np
import ml_dtypes
import concourse.bass as bass
import concourse.mybir as mybir
from concourse.bass_utils import run_bass_kernel_spmd

F32 = mybir.dt.float32
BF16 = mybir.dt.bfloat16
I32 = mybir.dt.int32
AF = mybir.ActivationFunctionType
ALU = mybir.AluOpType
AX = mybir.AxisListType
BF = ml_dtypes.bfloat16

D = 2048
I_FF = 8192
NEG = -1.0e30
SHIFT_A = 24.0
SHIFT_B = 12.0
SC_A = float(1.0 / np.sqrt(np.sqrt(128.0)))
SC_B = float(1.0 / np.sqrt(128.0))
SLOPES4 = [2.0 ** (-0.5 * (i + 1)) for i in range(4)]
KB = 1024

ENGINES = ('G', 'Y', 'T', 'V', 'S')
SEM_OF = dict(T='td', V='vd', S='sd', Y='y_x', G='g_cst')
DMA_SEMS = ('y_x', 'y_k', 'y_v', 'y_g0', 'y_g1', 'y_cv', 'y_cw',
            'y_s0', 'y_s1', 'y_s2', 'y_s3',
            'g_cst', 'g_dsc', 'g_wc0', 'g_wc1', 'g_wo0', 'g_wo1',
            'g_gb0', 'g_gb1', 'g_ub0', 'g_ub1', 'g_wd0', 'g_wd1')
ALL_SEMS = ('td', 'vd', 'sd', 'cc') + DMA_SEMS


class Sched:
    def __init__(self):
        self.ops = []
        self.cnt = {s: 0 for s in ALL_SEMS}
        self.gate = []
        self.gated = set(ENGINES)
        self.tag = "init"
        self.tags = []

    def begin_phase(self, marks):
        self.gate = [m for m in marks if m is not None and m[1] > 0]
        self.gated = set()

    @staticmethod
    def _flat(waits):
        out = []
        for w in waits:
            if w is None:
                continue
            if isinstance(w, list):
                out.extend(x for x in w if x is not None and x[1] > 0)
            elif w[1] > 0:
                out.append(w)
        return out

    def emit(self, eng, builder, waits=(), sem=None, amt=None):
        waits = self._flat(waits)
        if eng not in self.gated:
            waits = self.gate + waits
            self.gated.add(eng)
        if sem is None:
            sem = SEM_OF[eng]
        if amt is None:
            amt = 16 if sem[:2] in ('y_', 'g_') else 1
        self.ops.append((eng, waits, builder, (sem, amt)))
        self.tags.append(self.tag)
        self.cnt[sem] += amt
        return (sem, self.cnt[sem])

    def emit_wait_only(self, eng, waits):
        self.ops.append((eng, self._flat(waits), None, None))
        self.tags.append(self.tag)

    def mark(self, sem):
        return (sem, self.cnt[sem])

    def barrier_marks(self):
        return [(s, v) for s, v in self.cnt.items() if v > 0 and s != 'cc']


def mmax(*marks):
    best = {}
    for m in marks:
        if m is None:
            continue
        items = m if isinstance(m, list) else [m]
        for it in items:
            if it is None:
                continue
            s, v = it
            best[s] = max(best.get(s, 0), v)
    return [(s, v) for s, v in best.items() if v > 0]


class Builder:
    def __init__(self, n_sub=4):
        nc = bass.Bass("TRN2", target_bir_lowering=False)
        self.nc = nc
        self.n_sub = n_sub
        self.S = Sched()
        di = lambda n, s, d: nc.dram_tensor(n, s, d, kind="ExternalInput")
        self.x_in = di("x_in", [8, 128, D], F32)
        self.out_t = nc.dram_tensor("out", [8, 128, D], F32, kind="ExternalOutput")
        self.a_w = dict(q=di("a_wq", [D, D], F32), k=di("a_wk", [D, D], F32),
                        v=di("a_wv", [D, D], F32), o=di("a_wo", [D, D], F32))
        self.b_w = dict(q=di("b_wq", [D, D], F32), k=di("b_wk", [D, 512], F32),
                        v=di("b_wv", [D, 512], F32), o=di("b_wo", [D, D], F32))
        self.a_b = dict(q=di("a_bq", [D], F32), k=di("a_bk", [D], F32),
                        v=di("a_bv", [D], F32), o=di("a_bo", [D], F32))
        self.g_wt = di("g_wt", [D, I_FF], I32)
        self.u_wt = di("u_wt", [D, I_FF], I32)
        self.d_wt = di("d_wt", [I_FF, D], I32)
        self.gsc_in = di("gsc", [I_FF], F32)
        self.usc_in = di("usc", [I_FF], F32)
        self.dsc_in = di("dsc_b", [128, D], F32)
        self.ident_in = di("ident", [128, 128], BF16)
        self.amask_in = di("amask", [2, 128, 128], BF16)
        self.u2_in = di("b_u2", [6, 2 * 16 * 128], BF16)
        self.v2_in = di("b_v2", [6, 2 * 512], BF16)
        self.bbias_in = di("b_bias", [128, 128], F32)
        self.kvb = {16: nc.dram_tensor("kvb_a", [2, 16, 131072], BF16),
                    4: nc.dram_tensor("kvb_b", [2, 4, 131072], BF16)}
        self.kvg = {16: nc.dram_tensor("kvg_a", [2, 2, 16, 131072], BF16),
                    4: nc.dram_tensor("kvg_b", [2, 2, 4, 131072], BF16)}
        self.gw_bf = nc.dram_tensor("gw_bf", [D * I_FF], BF16)
        self.uw_bf = nc.dram_tensor("uw_bf", [D * I_FF], BF16)
        self.dw_bf = nc.dram_tensor("dw_bf", [I_FF * D], BF16)
        self.ffn_pass = 0
        self.gu_conv = [None] * 32
        self.wd_conv = {}

        total = 200 * KB
        base0, _end = nc.bump_sbuf(total)
        base0 = ((base0 + 511) // 512) * 512
        self.cur = base0
        def sal(name, shape, dt, at=None, align=512):
            esz = 2 if dt == BF16 else 4
            nbytes = int(np.prod(shape[1:])) * esz
            if at is None:
                at = self.cur
                self.cur = at + ((nbytes + align - 1) // align) * align
            return nc.alloc_sbuf_tensor_at(name, shape, dt, offset=at)
        self.xs = [sal(f"x{r}", [128, D], F32) for r in range(8)]
        self.slot1 = sal("slot1", [128, 16 * KB], BF16)
        self.slot2 = sal("slot2", [128, 16 * KB], BF16)
        self.ident = sal("ident_sb", [128, 128], BF16)
        self.st_sum = sal("st_sum", [128, 128], F32)
        self.st_m2 = sal("st_m2", [128, 128], F32)
        self.st_istd = sal("st_istd", [128, 128], F32)
        self.st_nmi = sal("st_nmi", [128, 128], F32)
        self.st_t1 = sal("st_t1", [128, 128], F32)
        self.st_t2 = sal("st_t2", [128, 128], F32)
        self.st_t3 = sal("st_t3", [128, 128], F32)
        self.junk = sal("junk", [128, 128], BF16, align=32)
        self.bqs = sal("bqs", [128, 16], F32, align=32)
        self.bks = sal("bks", [128, 16], F32, align=32)
        self.bvs = sal("bvs", [128, 16], F32, align=32)
        self.gsc_sb = sal("gsc_sb", [128, 64], F32, align=32)
        self.usc_sb = sal("usc_sb", [128, 64], F32, align=32)
        self.ones_col = sal("ones_col", [128, 1], BF16, align=32)
        self.ones_row = sal("ones_row", [1, 128], BF16, align=32)
        self.hneg = sal("hneg", [128, 8], F32, align=32)
        self.rd_sb = sal("rd_sb", [128, 4], F32, align=32)
        self.fc_sb = sal("fc_sb", [128, 2], F32, align=32)
        self.cur = ((self.cur + 511) // 512) * 512
        A = self.cur

        self.ksb = sal("ksb", [128, 4 * D], BF16, at=A)
        self.vsb = sal("vsb", [128, 4 * 16 * 129], BF16, at=A + 16 * KB)
        p1 = A + 33 * KB
        self.pt_sb = sal("pt_sb", [128, 8 * 512], BF16, at=p1)
        self.ktmp = sal("ktmp", [128, KB], BF16, at=p1 + 8 * KB)
        self.vtmp = sal("vtmp", [128, KB], BF16, at=p1 + 10 * KB)
        self.vttmp = sal("vttmp", [128, KB], BF16, at=p1 + 12 * KB)
        self.osb = sal("osb", [128, 512], BF16, at=p1 + 14 * KB)
        self.wcol = [sal(f"wcol{i}", [128, 16 * 128], BF16, at=p1 + 15 * KB + i * 4 * KB)
                     for i in range(2)]
        self.u2_sb = sal("u2_sb", [6, 2 * 16 * 128], BF16, at=p1 + 15 * KB)
        self.v2_sb = sal("v2_sb", [6, 2 * 512], BF16, at=p1 + 23 * KB)
        self.bbias_sb = sal("bbias_sb", [128, 128], F32, at=p1 + 25 * KB)
        self.amask_sb = sal("amask_sb", [128, 2 * 128], BF16, at=p1 + 26 * KB)
        self.bo_row = sal("bo_row", [1, D], BF16, at=p1 + 27 * KB)
        attn_end = p1 + 31 * KB
        self.wocol = [sal(f"wocol{i}", [128, 16 * 512], BF16, at=A + i * 16 * KB) for i in range(2)]
        self.xsq_a = sal("xsq_a", [128, D], F32, at=A)
        self.xlntm_a = sal("xlntm_a", [128, D], BF16, at=A + 8 * KB)

        self.h_extra = sal("h_extra", [128, 16 * KB], BF16, at=A)
        f0 = A + 32 * KB
        self.gbuf = [sal(f"gbuf{i}", [128, 16 * 256], BF16, at=f0 + i * 8 * KB) for i in range(2)]
        self.ubuf = [sal(f"ubuf{i}", [128, 16 * 256], BF16, at=f0 + 16 * KB + i * 8 * KB) for i in range(2)]
        self.wdbuf = [sal(f"wdbuf{i}", [128, 8 * 512], BF16, at=f0 + i * 8 * KB) for i in range(2)]
        self.dsc_sb = sal("dsc_sb", [128, D], F32, at=f0 + 16 * KB)
        self.t1_sb = sal("t1_sb", [128, 512], F32, at=f0 + 24 * KB)
        self.t1b_sb = sal("t1b_sb", [128, 512], F32, at=f0 + 26 * KB + 512)
        f1 = f0 + 32 * KB
        self.sg_sb = sal("sg_sb", [128, 512], BF16, at=f1)
        ffn_end = f1 + KB
        self.xsq_f = sal("xsq_f", [128, D], F32, at=f0)
        self.xlntm_f = sal("xlntm_f", [128, D], BF16, at=f0 + 8 * KB)

        assert max(attn_end, ffn_end) <= base0 + total, (attn_end, ffn_end, base0 + total)
        assert base0 + total <= nc.sbuf_top, (base0, total, nc.sbuf_top)

        self.pb01 = nc.alloc_psum_tensor("pb01", [128, 1024], F32)
        self.pb23 = nc.alloc_psum_tensor("pb23", [128, 1024], F32)
        self.pbav = [nc.alloc_psum_tensor(f"pbav{i}", [128, 512], F32) for i in range(2)]
        self.pbf = nc.alloc_psum_tensor("pbf", [128, 2048], BF16)

        self.bank_free = [None] * 4
        self.av_free = [None, None]
        self.pbf_free = [None, None]
        self.wcol_free = [None, None]
        self.wocol_free = [None, None]
        self.gbuf_free = [None, None]
        self.ubuf_free = [None, None]
        self.pt_free = [None] * 8
        self.x_marks = [None] * 8
        self.xln_ready = None
        self.phase_end = []

    def pbk(self, i, c0=0, c1=512):
        t = self.pb01 if i < 2 else self.pb23
        base = (i % 2) * 512
        return t[:, base + c0: base + c1]

    def T(self, b, waits=()): return self.S.emit('T', b, waits)
    def V(self, b, waits=()): return self.S.emit('V', b, waits)
    def Sc(self, b, waits=()): return self.S.emit('S', b, waits)
    def Y(self, b, waits=(), stream='y_x'): return self.S.emit('Y', b, waits, sem=stream)
    def G(self, b, waits=(), stream='g_cst'): return self.S.emit('G', b, waits, sem=stream)

    def init_loads(self):
        for r in range(8):
            self.Y((lambda r: lambda e: e.dma_start(
                out=self.xs[r][:], in_=self.x_in[r]))(r))
        for r in range(8):
            self.x_marks[r] = self.S.mark('y_x')
        mi = self.G(lambda e: e.dma_start(out=self.ident[:], in_=self.ident_in[:, :]))
        marks = [mi]
        for dst, src in ((self.bqs, self.a_b['q']), (self.bks, self.a_b['k']),
                         (self.bvs, self.a_b['v'])):
            marks.append(self.G((lambda dst, src: lambda e: e.dma_start(
                out=dst[:], in_=bass.AP(src, 0, [[1, 128], [128, 16]]),
                allow_slow_non_contiguous=True))(dst, src)))
        marks.append(self.G(lambda e: e.dma_start(
            out=self.gsc_sb[:], in_=bass.AP(self.gsc_in, 0, [[1, 128], [128, 64]]),
            allow_slow_non_contiguous=True)))
        marks.append(self.G(lambda e: e.dma_start(
            out=self.usc_sb[:], in_=bass.AP(self.usc_in, 0, [[1, 128], [128, 64]]),
            allow_slow_non_contiguous=True)))
        marks.append(self.G(lambda e: e.dma_start(
            out=self.bo_row[:], in_=bass.AP(self.a_b['o'], 0, [[D, 1], [1, D]]))))
        mb = mmax(*marks)
        m1 = self.V(lambda e: e.tensor_scalar_mul(self.bqs[:], self.bqs[:], SC_A), waits=[mb])
        m2 = self.V(lambda e: e.tensor_scalar_mul(self.bks[:], self.bks[:], SC_A), waits=[mb])
        m3 = self.V(lambda e: e.memset(self.ones_col[:], 1.0))
        m4 = self.V(lambda e: e.memset(self.ones_row[:], 1.0))
        m5 = self.V(lambda e: e.memset(self.fc_sb[:, 0:1], -SHIFT_A))
        m6 = self.V(lambda e: e.memset(self.fc_sb[:, 1:2], -SHIFT_B))
        mz = [self.V((lambda t: lambda e: e.memset(t[:], 0.0))(t))
              for t in (self.st_sum, self.st_m2, self.st_istd,
                        self.st_nmi, self.st_t1, self.st_t2, self.st_t3)]
        self.const_ready = mmax(mb, m1, m2, m3, m4, m5, m6, *mz)
        self.ident_ready = mi
        self.phase_end = self.S.barrier_marks()

    def ln(self, for_attn):
        S = self.S
        S.begin_phase(mmax(self.phase_end))
        xsq = self.xsq_a if for_attn else self.xsq_f
        xlntm = self.xlntm_a if for_attn else self.xlntm_f
        slot = self.slot1 if for_attn else self.slot2
        red = []
        prev_v = self.const_ready
        prev_s = None
        for r in range(8):
            w = [self.x_marks[r]]
            prev_v = self.V((lambda r: lambda e: e.reduce_sum(
                self.st_sum[:, r:r + 1], self.xs[r][:], AX.X))(r), waits=w + [prev_v])
            red.append(prev_v)
            prev_s = self.Sc((lambda r, xsq: lambda e: e.activation(
                xsq[:], self.xs[r][:], AF.Square, accum_out=self.st_m2[:, r:r + 1]))(r, xsq),
                waits=w + ([prev_s] if prev_s else []))
            red.append(prev_s)
        mred = mmax(*red)
        self.V(lambda e: e.tensor_copy(self.junk[:], self.ident[:]), waits=[mred, self.ident_ready])
        mA = self.V(lambda e: e.tensor_scalar_mul(self.st_t1[:], self.st_sum[:], 1.0 / D), waits=[mred])
        mB = self.V(lambda e: e.tensor_scalar_mul(self.st_m2[:], self.st_m2[:], 1.0 / D), waits=[mred])
        mC = self.V(lambda e: e.tensor_mul(self.st_t2[:], self.st_t1[:], self.st_t1[:]), waits=[mA])
        mD = self.V(lambda e: e.tensor_sub(self.st_t3[:], self.st_m2[:], self.st_t2[:]), waits=[mB, mC])
        mD2 = self.V(lambda e: e.tensor_scalar_add(self.st_t3[:], self.st_t3[:], 1e-5), waits=[mD])
        mE = self.Sc(lambda e: e.activation(self.st_t2[:], self.st_t3[:], AF.Ln), waits=[mD2])
        mF = self.Sc(lambda e: e.activation(self.st_istd[:], self.st_t2[:], AF.Exp, scale=-0.5), waits=[mE])
        mG = self.V(lambda e: e.scalar_tensor_tensor(
            self.st_nmi[:], self.st_t1[:], -1.0, self.st_istd[:], ALU.mult, ALU.mult), waits=[mF])
        cps = []
        last_tr = [None, None]
        maff = None
        for r in range(8):
            wa = [mG] + [m for m in last_tr if m] + ([maff] if maff else [])
            maff = self.V((lambda r, xlntm: lambda e: e.tensor_scalar(
                xlntm[:], self.xs[r][:], self.st_istd[:, r:r + 1],
                self.st_nmi[:, r:r + 1], ALU.mult, ALU.add))(r, xlntm), waits=wa)
            for dt in range(16):
                half = dt % 2
                wt = [maff, self.ident_ready, self.pbf_free[half]]
                mt = self.T((lambda dt, half, xlntm: lambda e: e.transpose(
                    self.pbf[:, half * 1024:half * 1024 + 128],
                    xlntm[:, dt * 128:(dt + 1) * 128], self.ident[:]))(dt, half, xlntm), waits=wt)
                if dt >= 14:
                    last_tr[half] = mt
                mc = self.V((lambda r, dt, half, slot: lambda e: e.tensor_copy(
                    slot[:, dt * KB + r * 128: dt * KB + (r + 1) * 128],
                    self.pbf[:, half * 1024:half * 1024 + 128]))(r, dt, half, slot), waits=[mt])
                self.pbf_free[half] = mc
                cps.append(mc)
        self.xln_ready = mmax(*cps)
        self.phase_end = self.S.barrier_marks()

    def xlnT(self, for_attn, dt):
        slot = self.slot1 if for_attn else self.slot2
        return slot[:, dt * KB:(dt + 1) * KB]

    def qt_ot(self, h):
        return self.slot2[:, h * KB:(h + 1) * KB]

    def h_tile(self, it):
        if it < 32:
            return self.slot1[:, it * 512:(it + 1) * 512]
        return self.h_extra[:, (it - 32) * 512:(it - 31) * 512]

    def _proj16(self, wh, E, et, buf, dest_cb, scale, bias_col, for_attn=True):
        mload = self.G((lambda buf, wh, E, et: lambda e: e.dma_start(
            out=self.wcol[buf][:],
            in_=bass.AP(wh, et * 128, [[E, 128], [128 * E, 16], [1, 128]])))(buf, wh, E, et),
            waits=[self.wcol_free[buf]], stream=f'g_wc{buf}')
        mm = None
        evs = []
        for qc in range(2):
            bank = qc
            wt = [mload, self.xln_ready, self.bank_free[bank]]
            for dt in range(16):
                mm = self.T((lambda buf, dt, bank, qc: lambda e: e.matmul(
                    self.pbk(bank), self.wcol[buf][:, dt * 128:(dt + 1) * 128],
                    self.xlnT(True, dt)[:, qc * 512:(qc + 1) * 512],
                    start=(dt == 0), stop=(dt == 15)))(buf, dt, bank, qc),
                    waits=wt if dt == 0 else ())
            me = dest_cb(qc, bank, mm)
            self.bank_free[bank] = me
            evs.append(me)
        self.wcol_free[buf] = mm
        return evs

    def attn(self, variant):
        S = self.S
        a = variant == 'a'
        nKV = 16 if a else 4
        W = self.a_w if a else self.b_w
        kvb, kvg = self.kvb[nKV], self.kvg[nKV]
        E_kv = D if a else 512
        qscale = SC_A if a else SC_B
        kscale = SC_A if a else 1.0
        shift = SHIFT_A if a else SHIFT_B

        S.begin_phase(mmax(self.phase_end))
        mconst = self.G(lambda e: e.dma_start(
            out=self.amask_sb[:],
            in_=bass.AP(self.amask_in, 0, [[128, 128], [16384, 2], [1, 128]])))
        if not a:
            mconst = mmax(mconst, self.G(lambda e: e.dma_start(
                out=self.bbias_sb[:], in_=self.bbias_in[:, :])))
        m_ones = self.V(lambda e: e.memset(
            bass.AP(self.vsb, 128, [[4 * 16 * 129, 128], [129, 64], [1, 1]]), 1.0))

        base_tag = self.S.tag
        self.S.tag = base_tag + ":kv"
        wi = 0
        store_marks = []
        ktmp_free = None
        vttmp_free = None
        for kv in range(nKV):
            def k_dest(qc, bank, mm, kv=kv):
                if a:
                    bld = (lambda bank, qc, kv: lambda e: e.tensor_scalar(
                        self.ktmp[:, qc * 512:(qc + 1) * 512], self.pbk(bank), kscale,
                        self.bks[:, kv:kv + 1], ALU.mult, ALU.add))(bank, qc, kv)
                else:
                    bld = (lambda bank, qc: lambda e: e.tensor_scalar(
                        self.ktmp[:, qc * 512:(qc + 1) * 512], self.pbk(bank), kscale,
                        None, ALU.mult))(bank, qc)
                return self.V(bld, waits=[mm, ktmp_free, self.const_ready])
            evk = self._proj16(W['k'], E_kv, kv, wi % 2, k_dest, kscale, kv)
            wi += 1
            ms = self.Y((lambda kv: lambda e: e.dma_start(
                out=bass.AP(kvb, kv * 131072, [[1024, 128], [1, 1024]]),
                in_=self.ktmp[:]))(kv), waits=evk, stream='y_k')
            ktmp_free = ms
            store_marks.append(ms)

            def v_dest(qc, bank, mm, kv=kv):
                if a:
                    bld = (lambda bank, qc, kv: lambda e: e.tensor_scalar(
                        self.vtmp[:, qc * 512:(qc + 1) * 512], self.pbk(bank), 1.0,
                        self.bvs[:, kv:kv + 1], ALU.mult, ALU.add))(bank, qc, kv)
                else:
                    bld = (lambda bank, qc: lambda e: e.tensor_copy(
                        self.vtmp[:, qc * 512:(qc + 1) * 512], self.pbk(bank)))(bank, qc)
                return self.V(bld, waits=[mm, self.const_ready])
            evv = self._proj16(W['v'], E_kv, kv, wi % 2, v_dest, 1.0, kv)
            wi += 1
            vt = []
            for j in range(8):
                half = j % 2
                wt = [mmax(*evv), self.pbf_free[half], self.ident_ready]
                mt = self.T((lambda j, half: lambda e: e.transpose(
                    self.pbf[:, half * 1024:half * 1024 + 128],
                    self.vtmp[:, j * 128:(j + 1) * 128], self.ident[:]))(j, half), waits=wt)
                mc = self.V((lambda j, half: lambda e: e.tensor_copy(
                    self.vttmp[:, j * 128:(j + 1) * 128],
                    self.pbf[:, half * 1024:half * 1024 + 128]))(j, half),
                    waits=[mt, vttmp_free])
                self.pbf_free[half] = mc
                vt.append(mc)
            ms = self.Y((lambda kv, nKV: lambda e: e.dma_start(
                out=bass.AP(kvb, (nKV + kv) * 131072, [[128, 128], [16384, 8], [1, 128]]),
                in_=self.vttmp[:].rearrange("p (j d) -> p j d", j=8)))(kv, nKV),
                waits=[mmax(*vt)], stream='y_v')
            vttmp_free = ms
            store_marks.append(ms)

        mcc = S.emit('G', (lambda kvb, kvg: lambda e: e.collective_compute(
            "AllGather", ALU.bypass, replica_groups=[[0, 1], [2, 3], [4, 5], [6, 7]],
            ins=[kvb.ap()], outs=[kvg.ap()]))(kvb, kvg),
            waits=[mmax(*store_marks)], sem='cc', amt=1)

        self.S.tag = base_tag + ":q"
        qev = []
        for et in range(16):
            def q_dest(qc, bank, mm, et=et):
                if a:
                    bld = (lambda bank, qc, et: lambda e: e.tensor_scalar(
                        self.qt_ot(et)[:, qc * 512:(qc + 1) * 512], self.pbk(bank), qscale,
                        self.bqs[:, et:et + 1], ALU.mult, ALU.add))(bank, qc, et)
                else:
                    bld = (lambda bank, qc, et: lambda e: e.tensor_scalar(
                        self.qt_ot(et)[:, qc * 512:(qc + 1) * 512], self.pbk(bank), qscale,
                        None, ALU.mult))(bank, qc, et)
                return self.V(bld, waits=[mm, self.const_ready])
            qev += self._proj16(W['q'], D, et, wi % 2, q_dest, qscale, et)
            wi += 1
        q_ready = mmax(*qev)
        if not a:
            self.G(lambda e: e.dma_start(out=self.u2_sb[:], in_=self.u2_in[:, :]),
                   waits=[self.wcol_free[0], self.wcol_free[1]])
            self.G(lambda e: e.dma_start(out=self.v2_sb[:], in_=self.v2_in[:, :]),
                   waits=())
            alibi_ready = self.S.mark('g_cst')
            mconst = alibi_ready
            self.wcol_free = [None, None]
        else:
            alibi_ready = None

        grp_cons = {}
        slot_cons = {}
        ot_marks = []
        self.S.tag = base_tag + ":heads"
        for hg in range(4):
            par = hg % 2
            RNK = 2 * nKV * 131072
            gl_slot = [None] * 4
            if a:
                for s in range(4):
                    wl = [mcc, m_ones, slot_cons.get((hg - 1, s))]
                    ms1 = self.Y((lambda s, hg: lambda e: e.dma_start(
                        out=bass.AP(self.ksb, s * 2048, [[4 * D, 128], [1024, 2], [1, 1024]]),
                        in_=bass.AP(kvg, 0 * RNK + (4 * hg + s) * 131072,
                                    [[1024, 128], [RNK, 2], [1, 1024]])))(s, hg),
                        waits=wl, stream=f'y_s{s}')
                    ms2 = [self.Y((lambda s, hg, r: lambda e: e.dma_start(
                        out=bass.AP(self.vsb, s * 16 * 129 + r * 8 * 129,
                                    [[4 * 16 * 129, 128], [129, 8], [1, 128]]),
                        in_=bass.AP(kvg, r * RNK + (nKV + 4 * hg + s) * 131072,
                                    [[128, 128], [16384, 8], [1, 128]])))(s, hg, r),
                        waits=wl, stream=f'y_s{s}') for r in range(2)]
                    gl_slot[s] = mmax(ms1, *ms2)
            else:
                wl = [mcc, m_ones, grp_cons.get(hg - 2)]
                lms = []
                for r in range(2):
                    lms.append(self.Y((lambda par, r, hg: lambda e: e.dma_start(
                        out=bass.AP(self.ksb, par * 2048 + r * 1024, [[4 * D, 128], [1, 1024]]),
                        in_=bass.AP(kvg, r * RNK + hg * 131072, [[1024, 128], [1, 1024]])))(par, r, hg),
                        waits=wl, stream=f'y_g{hg % 2}'))
                    lms.append(self.Y((lambda par, r, hg: lambda e: e.dma_start(
                        out=bass.AP(self.vsb, par * 16 * 129 + r * 8 * 129,
                                    [[4 * 16 * 129, 128], [129, 8], [1, 128]]),
                        in_=bass.AP(kvg, r * RNK + (nKV + hg) * 131072,
                                    [[128, 128], [16384, 8], [1, 128]])))(par, r, hg),
                        waits=wl, stream=f'y_g{hg % 2}'))
                gl_slot = [mmax(*lms)] * 4
            cons = []
            for hi in range(4):
                h = 4 * hg + hi
                kcol = hi * 2048 if a else par * 2048
                vbase = hi * 16 * 129 if a else par * 16 * 129
                gl = gl_slot[hi]
                head_cons = []
                for qc in range(2):
                    issued = []
                    for g in range(4 * qc + 4):
                        issued += [g, g + 8]
                    nI = len(issued)
                    nP = nI // 2
                    gbs = [t % 8 for t in issued]
                    mex = {}
                    av_sched = [(idx, b) for idx in range(nI)
                                for b in range(max(0, gbs[idx] - 4 * qc), 4)]
                    bank_first = {}
                    bank_last = {}
                    for pos, (idx, b) in enumerate(av_sched):
                        j = b // 2
                        if j not in bank_first:
                            bank_first[j] = (idx, b)
                        bank_last[j] = (idx, b)
                    av_stop_bank = [None, None]

                    def do_scores_pair(p):
                        bp = p % 2
                        tens = self.pb01 if bp == 0 else self.pb23
                        mprevs = []
                        t = issued[2 * p]
                        gb = t % 8
                        b0 = max(0, gb - 4 * qc)
                        nq = 512 - b0 * 128
                        for k in (0, 1):
                            idx = 2 * p + k
                            t = issued[idx]
                            rk = t // 8
                            bank = 2 * bp + k
                            wt = [gl, q_ready, self.bank_free[bank]]
                            if not a:
                                wt.append(alibi_ready)
                            mm = self.T((lambda qc, t, bank, kcol, h, b0, nq: lambda e: e.matmul(
                                self.pbk(bank, b0 * 128, b0 * 128 + nq),
                                self.ksb[:, kcol + (t // 8) * 1024 + (t % 8) * 128:
                                         kcol + (t // 8) * 1024 + (t % 8) * 128 + 128],
                                self.qt_ot(h)[:, qc * 512 + b0 * 128:(qc + 1) * 512],
                                start=True, stop=a))(qc, t, bank, kcol, h, b0, nq), waits=wt)
                            if not a:
                                ucol = (rk * 16 + h) * 128
                                mm = self.T((lambda qc, bank, ucol, b0, nq: lambda e: e.matmul(
                                    self.pbk(bank, b0 * 128, b0 * 128 + nq),
                                    self.u2_sb[:, ucol: ucol + 128],
                                    self.v2_sb[:, qc * 512 + b0 * 128:(qc + 1) * 512],
                                    start=False, stop=True))(qc, bank, ucol, b0, nq), waits=())
                            head_cons.append(mm)
                            mprev = mm
                            b_blk = gb - 4 * qc
                            if 0 <= b_blk <= 3:
                                mprev = self.V((lambda bank, rk, b_blk: lambda e: e.tensor_add(
                                    self.pbk(bank, b_blk * 128, (b_blk + 1) * 128),
                                    self.pbk(bank, b_blk * 128, (b_blk + 1) * 128),
                                    self.amask_sb[:, rk * 128:(rk + 1) * 128]))(bank, rk, b_blk),
                                    waits=[mm, mconst])
                            mprevs.append(mprev)
                        psA = (2 * p) % 8
                        if a:
                            bias_ap = self.fc_sb[:, 0:1]
                        else:
                            bias_ap = self.bbias_sb[:, h * 8 + gb: h * 8 + gb + 1]
                        m = self.Sc((lambda tens, bp, psA, bias_ap, b0, nq: lambda e: e.activation(
                            bass.AP(self.pt_sb, psA * 512 + b0 * 128, [[4096, 128], [512, 2], [1, nq]]),
                            bass.AP(tens, b0 * 128, [[1024, 128], [512, 2], [1, nq]]),
                            AF.Exp, bias=bias_ap))(tens, bp, psA, bias_ap, b0, nq),
                            waits=mprevs + [self.pt_free[psA], self.pt_free[psA + 1],
                                            self.const_ready, mconst])
                        self.bank_free[2 * bp] = m
                        self.bank_free[2 * bp + 1] = m
                        mex[p] = m

                    def do_av_pair(p):
                        for k in (0, 1):
                            idx = 2 * p + k
                            t = issued[idx]
                            gb = t % 8
                            ps = idx % 8
                            last_av = None
                            for b in range(max(0, gb - 4 * qc), 4):
                                j = b // 2
                                col0 = (b % 2) * 256
                                first = bank_first[j] == (idx, b)
                                last = bank_last[j] == (idx, b)
                                wt = [mex[p]]
                                if first:
                                    wt.append(self.av_free[j])
                                last_av = self.T((lambda j, col0, ps, vbase, t, first, last, b:
                                                  lambda e: e.matmul(
                                    self.pbav[j][:, col0: col0 + 129],
                                    self.pt_sb[:, ps * 512 + b * 128: ps * 512 + (b + 1) * 128],
                                    self.vsb[:, vbase + t * 129: vbase + (t + 1) * 129],
                                    start=first, stop=last))(j, col0, ps, vbase, t, first, last, b),
                                    waits=wt)
                                if last:
                                    av_stop_bank[j] = last_av
                            self.pt_free[ps] = last_av
                            head_cons.append(last_av)

                    for p in range(nP):
                        do_scores_pair(p)
                        if p >= 2:
                            do_av_pair(p - 2)
                    if nP >= 2:
                        do_av_pair(nP - 2)
                    do_av_pair(nP - 1)
                    for b in range(4):
                        j, col0 = b // 2, (b % 2) * 256
                        self.V((lambda b, j, col0: lambda e: e.reciprocal(
                            self.rd_sb[:, b:b + 1], self.pbav[j][:, col0 + 128: col0 + 129]))(b, j, col0),
                            waits=[av_stop_bank[j]])
                    mj = self.V(lambda e: e.tensor_copy(self.junk[:], self.ident[:]),
                                waits=[self.ident_ready])
                    mos = [None, None]
                    for b in range(4):
                        j, col0 = b // 2, (b % 2) * 256
                        mo = self.V((lambda b, j, col0: lambda e: e.tensor_scalar(
                            self.osb[:, b * 128:(b + 1) * 128], self.pbav[j][:, col0: col0 + 128],
                            self.rd_sb[:, b:b + 1], None, ALU.mult))(b, j, col0), waits=[mj])
                        mos[j] = mo
                        half = b % 2
                        mt = self.T((lambda b, half: lambda e: e.transpose(
                            self.pbf[:, half * 1024:half * 1024 + 128],
                            self.osb[:, b * 128:(b + 1) * 128], self.ident[:]))(b, half),
                            waits=[mo, self.ident_ready, self.pbf_free[half]])
                        mc = self.V((lambda h, qc, b, half: lambda e: e.tensor_copy(
                            self.qt_ot(h)[:, qc * 512 + b * 128: qc * 512 + (b + 1) * 128],
                            self.pbf[:, half * 1024:half * 1024 + 128]))(h, qc, b, half), waits=[mt])
                        self.pbf_free[half] = mc
                        ot_marks.append(mc)
                    self.av_free = [mos[0], mos[1]]
                slot_cons[(hg, hi)] = mmax(*head_cons)
                cons += head_cons
            grp_cons[hg] = mmax(*cons)
        ot_ready = mmax(*ot_marks)

        self.S.tag = base_tag + ":proj"
        Wo = W['o']
        for ech in range(4):
            wf = [self.wocol_free[ech % 2]]
            if ech < 2:
                wf.append(grp_cons[2 + ech])
                wf.append(grp_cons[3])
            mw = self.G((lambda ech, Wo: lambda e: e.dma_start(
                out=self.wocol[ech % 2][:],
                in_=bass.AP(Wo, ech * 512, [[D, 128], [128 * D, 16], [1, 512]])))(ech, Wo),
                waits=wf, stream=f'g_wo{ech % 2}')
            mmlast = None
            for qt in range(8):
                bank = qt % 4
                wt = [mw, ot_ready, self.bank_free[bank]]
                if a:
                    wt.append(self.const_ready)
                mm = None
                for dt in range(16):
                    stop = (dt == 15) and not a
                    mm = self.T((lambda ech, qt, dt, bank, stop: lambda e: e.matmul(
                        self.pbk(bank), self.qt_ot(dt)[:, qt * 128:(qt + 1) * 128],
                        self.wocol[ech % 2][:, dt * 512:(dt + 1) * 512],
                        start=(dt == 0), stop=stop))(ech, qt, dt, bank, stop),
                        waits=wt if dt == 0 else ())
                if a:
                    mm = self.T((lambda ech, bank: lambda e: e.matmul(
                        self.pbk(bank), self.ones_row[:, 0:128],
                        self.bo_row[:, ech * 512:(ech + 1) * 512],
                        start=False, stop=True))(ech, bank), waits=())
                mmlast = mm
                mx = self.V((lambda qt, ech, bank: lambda e: e.tensor_add(
                    self.xs[qt][:, ech * 512:(ech + 1) * 512],
                    self.xs[qt][:, ech * 512:(ech + 1) * 512],
                    self.pbk(bank)))(qt, ech, bank), waits=[mm, self.x_marks[qt]])
                self.bank_free[bank] = mx
                self.x_marks[qt] = mx
            self.wocol_free[ech % 2] = mmlast
        self.phase_end = self.S.barrier_marks()

    def ffn(self):
        S = self.S
        S.begin_phase(mmax(self.phase_end))
        sg_free = None
        for qc in range(2):
            hm = [None] * 64
            hsum_stop = [None] * 4

            def emit_hsum(it):
                for qt in range(4):
                    j, col0 = qt // 2, (qt % 2) * 256
                    wt2 = [hm[it]]
                    if it == 0:
                        wt2 += [self.av_free[j], self.const_ready]
                    hsum_stop[qt] = self.T((lambda it, qt, j, col0: lambda e: e.matmul(
                        self.pbav[j][:, col0: col0 + 1],
                        self.h_tile(it)[:, qt * 128:(qt + 1) * 128],
                        self.ones_col[:],
                        start=(it == 0 and qt % 2 == 0),
                        stop=(it == 63 and qt % 2 == 1)))(it, qt, j, col0), waits=wt2)

            first = (self.ffn_pass == 0 and qc == 0)
            base_tag = self.S.tag.split(":gu")[0].split(":dn")[0]
            self.S.tag = base_tag + f":gu{qc}"
            for ip in range(32):
                buf = ip % 2
                if first:
                    mgl = self.G((lambda buf, ip: lambda e: e.dma_start(
                        out=self.gbuf[buf][:],
                        in_=bass.AP(self.g_wt, ip * 256, [[I_FF, 128], [128 * I_FF, 16], [1, 256]])))(buf, ip),
                        waits=[self.gbuf_free[buf]], stream=f'g_gb{buf}')
                    mul_ = self.G((lambda buf, ip: lambda e: e.dma_start(
                        out=self.ubuf[buf][:],
                        in_=bass.AP(self.u_wt, ip * 256, [[I_FF, 128], [128 * I_FF, 16], [1, 256]])))(buf, ip),
                        waits=[self.ubuf_free[buf]], stream=f'g_ub{buf}')
                    sg1 = self.Y((lambda buf, ip: lambda e: e.dma_start(
                        out=bass.AP(self.gw_bf, ip * 524288, [[4096, 128], [1, 4096]]),
                        in_=self.gbuf[buf][:]))(buf, ip), waits=[mgl], stream='y_cv')
                    sg2 = self.Y((lambda buf, ip: lambda e: e.dma_start(
                        out=bass.AP(self.uw_bf, ip * 524288, [[4096, 128], [1, 4096]]),
                        in_=self.ubuf[buf][:]))(buf, ip), waits=[mul_], stream='y_cv')
                    self.gu_conv[ip] = (sg1, sg2)
                else:
                    cvt = self.S.mark('y_cv')
                    mgl = self.G((lambda buf, ip: lambda e: e.dma_start(
                        out=self.gbuf[buf][:],
                        in_=bass.AP(self.gw_bf, ip * 524288, [[4096, 128], [1, 4096]])))(buf, ip),
                        waits=[self.gbuf_free[buf], cvt], stream=f'g_gb{buf}')
                    mul_ = self.G((lambda buf, ip: lambda e: e.dma_start(
                        out=self.ubuf[buf][:],
                        in_=bass.AP(self.uw_bf, ip * 524288, [[4096, 128], [1, 4096]])))(buf, ip),
                        waits=[self.ubuf_free[buf], cvt], stream=f'g_ub{buf}')
                glast = ulast = None
                for k in range(2):
                    it = ip * 2 + k
                    bg = (it % 2) * 2
                    bu = bg + 1
                    wt = [mgl, self.xln_ready, self.bank_free[bg]]
                    for dt in range(16):
                        glast = self.T((lambda buf, dt, k, qc, bg: lambda e: e.matmul(
                            self.pbk(bg),
                            self.gbuf[buf][:, dt * 256 + k * 128: dt * 256 + (k + 1) * 128],
                            self.xlnT(False, dt)[:, qc * 512:(qc + 1) * 512],
                            start=(dt == 0), stop=(dt == 15)))(buf, dt, k, qc, bg),
                            waits=wt if dt == 0 else ())
                    wt = [mul_, self.xln_ready, self.bank_free[bu]]
                    for dt in range(16):
                        ulast = self.T((lambda buf, dt, k, qc, bu: lambda e: e.matmul(
                            self.pbk(bu),
                            self.ubuf[buf][:, dt * 256 + k * 128: dt * 256 + (k + 1) * 128],
                            self.xlnT(False, dt)[:, qc * 512:(qc + 1) * 512],
                            start=(dt == 0), stop=(dt == 15)))(buf, dt, k, qc, bu),
                            waits=wt if dt == 0 else ())
                    msg = self.Sc((lambda it, bg: lambda e: e.activation(
                        self.sg_sb[:], self.pbk(bg), AF.Silu,
                        scale=self.gsc_sb[:, it:it + 1]))(it, bg), waits=[glast, sg_free])
                    self.bank_free[bg] = msg
                    mh = self.V((lambda it, bu: lambda e: e.scalar_tensor_tensor(
                        self.h_tile(it), self.pbk(bu), self.usc_sb[:, it:it + 1],
                        self.sg_sb[:], ALU.mult, ALU.mult))(it, bu), waits=[ulast, msg])
                    self.bank_free[bu] = mh
                    sg_free = mh
                    hm[it] = mh
                    if it >= 1:
                        emit_hsum(it - 1)
                self.gbuf_free[buf] = mmax(glast, self.gu_conv[ip][0] if first else None)
                self.ubuf_free[buf] = mmax(ulast, self.gu_conv[ip][1] if first else None)
            emit_hsum(63)
            hneg_m = []
            for qt in range(4):
                j, col0 = qt // 2, (qt % 2) * 256
                hneg_m.append(self.Sc((lambda qc, qt, j, col0: lambda e: e.mul(
                    self.hneg[:, qc * 4 + qt: qc * 4 + qt + 1],
                    self.pbav[j][:, col0: col0 + 1], -128.0))(qc, qt, j, col0),
                    waits=[hsum_stop[2 * j + 1]]))
            self.av_free = [hneg_m[1], hneg_m[3]]
            mhneg = mmax(*hneg_m)
            h_ready = mmax(*hm)
            self.S.tag = base_tag + f":dn{qc}"
            mdsc = self.G(lambda e: e.dma_start(out=self.dsc_sb[:], in_=self.dsc_in[:, :]),
                          waits=[h_ready, self.ubuf_free[0]], stream='g_dsc')
            wdbuf_free = [mmax(self.gbuf_free[0], h_ready), mmax(self.gbuf_free[1], h_ready)]
            t1_free = self.ubuf_free[1]
            t1b_free = self.ubuf_free[1]
            for ech in range(4):
                stops = [None] * 4
                for ic in range(8):
                    buf = ic % 2
                    if first:
                        mwl = self.G((lambda buf, ic, ech: lambda e: e.dma_start(
                            out=self.wdbuf[buf][:],
                            in_=bass.AP(self.d_wt, (ic * 8 * 128) * D + ech * 512,
                                        [[D, 128], [128 * D, 8], [1, 512]])))(buf, ic, ech),
                            waits=[wdbuf_free[buf]], stream=f'g_wd{buf}')
                        scv = self.Y((lambda buf, ic, ech: lambda e: e.dma_start(
                            out=bass.AP(self.dw_bf, (ech * 8 + ic) * 524288,
                                        [[4096, 128], [1, 4096]]),
                            in_=self.wdbuf[buf][:]))(buf, ic, ech), waits=[mwl], stream='y_cw')
                        self.wd_conv[(ech, ic)] = scv
                    else:
                        cvw = self.S.mark('y_cw')
                        mwl = self.G((lambda buf, ic, ech: lambda e: e.dma_start(
                            out=self.wdbuf[buf][:],
                            in_=bass.AP(self.dw_bf, (ech * 8 + ic) * 524288,
                                        [[4096, 128], [1, 4096]])))(buf, ic, ech),
                            waits=[wdbuf_free[buf], cvw], stream=f'g_wd{buf}')
                    for i8 in range(8):
                        it = ic * 8 + i8
                        for qt in range(4):
                            if it == 0:
                                wt = [mwl, h_ready, mhneg, self.bank_free[qt]]
                            elif i8 == 0:
                                wt = [mwl]
                            else:
                                wt = ()
                            stops[qt] = self.T((lambda buf, it, i8, qt: lambda e: e.matmul(
                                self.pbk(qt), self.h_tile(it)[:, qt * 128:(qt + 1) * 128],
                                self.wdbuf[buf][:, i8 * 512:(i8 + 1) * 512],
                                start=(it == 0), stop=(it == 63)))(buf, it, i8, qt),
                                waits=wt)
                    wdbuf_free[buf] = mmax(stops[3], self.wd_conv.get((ech, ic)) if first else None)
                for qt in range(4):
                    rt = qc * 4 + qt
                    m1 = self.Sc((lambda qt, qc: lambda e: e.add(
                        self.t1_sb[:], self.pbk(qt),
                        self.hneg[:, qc * 4 + qt: qc * 4 + qt + 1]))(qt, qc),
                        waits=[stops[qt], mhneg, t1_free])
                    self.bank_free[qt] = m1
                    m2 = self.V((lambda ech: lambda e: e.tensor_mul(
                        self.t1b_sb[:], self.t1_sb[:],
                        self.dsc_sb[:, ech * 512:(ech + 1) * 512]))(ech),
                        waits=[m1, mdsc, t1b_free])
                    t1_free = m2
                    m3 = self.V((lambda rt, ech: lambda e: e.tensor_add(
                        self.xs[rt][:, ech * 512:(ech + 1) * 512],
                        self.xs[rt][:, ech * 512:(ech + 1) * 512],
                        self.t1b_sb[:]))(rt, ech), waits=[m2, self.x_marks[rt]])
                    t1b_free = m3
                    self.x_marks[rt] = m3
            self.gbuf_free = [wdbuf_free[0], wdbuf_free[1]]
            self.ubuf_free = [mmax(t1_free), mmax(t1_free, t1b_free)]
        self.ffn_pass += 1
        self.phase_end = self.S.barrier_marks()

    def finish(self):
        self.S.begin_phase(mmax(self.phase_end))
        for r in range(8):
            self.Y((lambda r: lambda e: e.dma_start(
                out=self.out_t[r], in_=self.xs[r][:]))(r), waits=[self.x_marks[r]])
        self.S.emit_wait_only('Y', [self.S.mark('y_x')])

    def realize(self):
        nc, S = self.nc, self.S
        import contextlib
        with contextlib.ExitStack() as st:
            sems = {s: st.enter_context(nc.semaphore(s)) for s in ALL_SEMS}
            blk = st.enter_context(nc.Block())
            attr = {'T': 'tensor', 'V': 'vector', 'S': 'scalar', 'Y': 'sync', 'G': 'gpsimd'}
            for ename in ENGINES:
                def runner(eng, ename=ename):
                    hiwater = {}
                    for (e2, waits, builder, inc) in S.ops:
                        if e2 != ename:
                            continue
                        for (s, v) in waits:
                            if hiwater.get(s, -1) >= v:
                                continue
                            hiwater[s] = v
                            eng.wait_ge(sems[s], v)
                        if builder is None:
                            continue
                        ins = builder(eng)
                        if inc is not None:
                            ins.then_inc(sems[inc[0]], inc[1])
                getattr(blk, attr[ename])(runner)
        return nc


def build(n_sub=4):
    b = Builder(n_sub)
    b.init_loads()
    steps = [('ln_a',), ('attn', 'a'), ('ln_f',), ('ffn',), ('ln_a',), ('attn', 'b'),
             ('ln_f',), ('ffn',)][: 2 * n_sub]
    for i, st in enumerate(steps):
        b.S.tag = f"{i}:{st[0]}" + (st[1] if len(st) > 1 else "")
        if st[0] == 'ln_a':
            b.ln(True)
        elif st[0] == 'ln_f':
            b.ln(False)
        elif st[0] == 'attn':
            b.attn(st[1])
        else:
            b.ffn()
    b.finish()
    return b.realize()


def host_consts(P):
    d = {}
    d["ident"] = np.eye(128, dtype=BF)
    kp = np.arange(128)[:, None]
    qp = np.arange(128)[None, :]
    am = np.zeros((2, 128, 128), np.float32)
    for rk in range(2):
        am[rk] = np.where((2 * kp + rk) <= (2 * qp + P), 0.0, NEG)
    d["amask"] = am.astype(BF)
    slopes = np.repeat([2.0 ** (-0.5 * (i + 1)) for i in range(4)], 4)[:16]

    def hilo(x):
        hi = x.astype(BF).astype(np.float32)
        lo = (x - hi).astype(BF).astype(np.float32)
        return hi, lo

    u2t = np.zeros((6, 2 * 16 * 128), np.float32)
    for rk in range(2):
        for h in range(16):
            c = (rk * 16 + h) * 128
            u1h, u1l = hilo(slopes[h] * (2.0 * np.arange(128, dtype=np.float64) + rk))
            s2h, s2l = hilo(np.full(128, slopes[h], np.float64))
            u2t[0, c:c + 128] = u1h
            u2t[1, c:c + 128] = u1l
            u2t[2, c:c + 128] = s2h
            u2t[3, c:c + 128] = s2h
            u2t[4, c:c + 128] = s2l
            u2t[5, c:c + 128] = s2l
    d["b_u2"] = u2t.astype(BF)
    v2 = np.zeros((6, 2 * 512), np.float32)
    v2[0, :] = 1.0
    v2[1, :] = 1.0
    for qc in range(2):
        q = np.arange(512)
        w2 = -(256.0 * (4 * qc + q // 128) + 2.0 * (q % 128) + P)
        wh, wl = hilo(w2.astype(np.float64))
        v2[2, qc * 512:(qc + 1) * 512] = wh
        v2[3, qc * 512:(qc + 1) * 512] = wl
        v2[4, qc * 512:(qc + 1) * 512] = wh
        v2[5, qc * 512:(qc + 1) * 512] = wl
    d["b_v2"] = v2.astype(BF)
    bb = np.zeros((128, 128), np.float32)
    for h in range(16):
        for gb in range(8):
            bb[:, h * 8 + gb] = -SHIFT_B + slopes[h] * 256.0 * gb
    d["b_bias"] = bb
    return d


def host_weights(z):
    d = {}
    d["a_wq"] = np.ascontiguousarray(z["L0_attn_a_Wq"].T)
    d["a_wk"] = np.ascontiguousarray(z["L0_attn_a_Wk"].T)
    d["a_wv"] = np.ascontiguousarray(z["L0_attn_a_Wv"].T)
    d["a_wo"] = np.ascontiguousarray(z["L0_attn_a_Wo"].T)
    d["a_bq"] = z["L0_attn_a_bq"]
    d["a_bk"] = z["L0_attn_a_bk"]
    d["a_bv"] = z["L0_attn_a_bv"]
    d["a_bo"] = z["L0_attn_a_bo"]
    d["b_wq"] = z["L0_attn_b_Wq"]
    d["b_wk"] = z["L0_attn_b_Wk"]
    d["b_wv"] = z["L0_attn_b_Wv"]
    d["b_wo"] = z["L0_attn_b_Wo"]
    d["g_wt"] = np.ascontiguousarray(z["L0_ffn_gate_w"].T)
    d["u_wt"] = np.ascontiguousarray(z["L0_ffn_up_w"].T)
    d["d_wt"] = np.ascontiguousarray(z["L0_ffn_down_w"].T)
    d["gsc"] = z["L0_ffn_gate_scale"]
    d["usc"] = z["L0_ffn_up_scale"]
    d["dsc_b"] = np.ascontiguousarray(np.broadcast_to(z["L0_ffn_down_scale"], (128, D)))
    return d


_NC_CACHE = {}


def run(inputs, n_sub=4):
    if n_sub not in _NC_CACHE:
        _NC_CACHE[n_sub] = build(n_sub)
    nc = _NC_CACHE[n_sub]
    X = inputs["L0_attn_a_X"]
    w = host_weights(inputs)
    consts = [host_consts(0), host_consts(1)]
    in_maps = []
    for c in range(8):
        b, P = c // 2, c % 2
        d = dict(w)
        d.update(consts[P])
        d["x_in"] = np.ascontiguousarray(X[b].reshape(8, 128, 2, D)[:, :, P])
        in_maps.append(d)
    res = run_bass_kernel_spmd(nc, in_maps, core_ids=list(range(8)))
    out = np.empty((4, 2048, D), np.float32)
    ov = out.reshape(4, 8, 128, 2, D)
    for c in range(8):
        b, P = c // 2, c % 2
        ov[b, :, :, P] = res.results[c]["out"]
    return out


def kernel(**inputs):
    return run(inputs, n_sub=4)


# version 07
# speedup vs baseline: 1.0897x; 1.0210x over previous
import numpy as np
import ml_dtypes
import concourse.bass as bass
import concourse.mybir as mybir
from concourse.bass_utils import run_bass_kernel_spmd

F32 = mybir.dt.float32
BF16 = mybir.dt.bfloat16
I32 = mybir.dt.int32
AF = mybir.ActivationFunctionType
ALU = mybir.AluOpType
AX = mybir.AxisListType
BF = ml_dtypes.bfloat16

D = 2048
I_FF = 8192
NEG = -1.0e30
SHIFT_A = 24.0
SHIFT_B = 12.0
SC_A = float(1.0 / np.sqrt(np.sqrt(128.0)))
SC_B = float(1.0 / np.sqrt(128.0))
SLOPES4 = [2.0 ** (-0.5 * (i + 1)) for i in range(4)]
KB = 1024

ENGINES = ('G', 'Y', 'T', 'V', 'S')
SEM_OF = dict(T='td', V='vd', S='sd', Y='y_x', G='g_cst')
DMA_SEMS = ('y_x', 'y_k', 'y_v', 'y_g0', 'y_g1',
            'y_cg0', 'y_cg1', 'y_cu0', 'y_cu1', 'y_cd0', 'y_cd1',
            'y_s0', 'y_s1', 'y_s2', 'y_s3',
            'g_cst', 'g_dsc', 'g_wc0', 'g_wc1', 'g_wo0', 'g_wo1',
            'g_gb0', 'g_gb1', 'g_ub0', 'g_ub1', 'g_wd0', 'g_wd1')
ALL_SEMS = ('td', 'vd', 'sd', 'cc') + DMA_SEMS


class Sched:
    def __init__(self):
        self.ops = []
        self.cnt = {s: 0 for s in ALL_SEMS}
        self.gate = []
        self.gated = set(ENGINES)
        self.tag = "init"
        self.tags = []

    def begin_phase(self, marks):
        self.gate = [m for m in marks if m is not None and m[1] > 0]
        self.gated = set()

    @staticmethod
    def _flat(waits):
        out = []
        for w in waits:
            if w is None:
                continue
            if isinstance(w, list):
                out.extend(x for x in w if x is not None and x[1] > 0)
            elif w[1] > 0:
                out.append(w)
        return out

    def emit(self, eng, builder, waits=(), sem=None, amt=None):
        waits = self._flat(waits)
        if eng not in self.gated:
            waits = self.gate + waits
            self.gated.add(eng)
        if sem is None:
            sem = SEM_OF[eng]
        if amt is None:
            amt = 16 if sem[:2] in ('y_', 'g_') else 1
        self.ops.append((eng, waits, builder, (sem, amt)))
        self.tags.append(self.tag)
        self.cnt[sem] += amt
        return (sem, self.cnt[sem])

    def emit_wait_only(self, eng, waits):
        self.ops.append((eng, self._flat(waits), None, None))
        self.tags.append(self.tag)

    def mark(self, sem):
        return (sem, self.cnt[sem])

    def barrier_marks(self):
        return [(s, v) for s, v in self.cnt.items() if v > 0 and s != 'cc']


def mmax(*marks):
    best = {}
    for m in marks:
        if m is None:
            continue
        items = m if isinstance(m, list) else [m]
        for it in items:
            if it is None:
                continue
            s, v = it
            best[s] = max(best.get(s, 0), v)
    return [(s, v) for s, v in best.items() if v > 0]


class Builder:
    def __init__(self, n_sub=4):
        nc = bass.Bass("TRN2", target_bir_lowering=False)
        self.nc = nc
        self.n_sub = n_sub
        self.S = Sched()
        di = lambda n, s, d: nc.dram_tensor(n, s, d, kind="ExternalInput")
        self.x_in = di("x_in", [8, 128, D], F32)
        self.out_t = nc.dram_tensor("out", [8, 128, D], F32, kind="ExternalOutput")
        self.a_w = dict(q=di("a_wq", [D, D], F32), k=di("a_wk", [D, D], F32),
                        v=di("a_wv", [D, D], F32), o=di("a_wo", [D, D], F32))
        self.b_w = dict(q=di("b_wq", [D, D], F32), k=di("b_wk", [D, 512], F32),
                        v=di("b_wv", [D, 512], F32), o=di("b_wo", [D, D], F32))
        self.a_b = dict(q=di("a_bq", [D], F32), k=di("a_bk", [D], F32),
                        v=di("a_bv", [D], F32), o=di("a_bo", [D], F32))
        self.g_wt = di("g_wt", [D, I_FF], I32)
        self.u_wt = di("u_wt", [D, I_FF], I32)
        self.d_wt = di("d_wt", [I_FF, D], I32)
        self.gsc_in = di("gsc", [I_FF], F32)
        self.usc_in = di("usc", [I_FF], F32)
        self.dsc_in = di("dsc_b", [128, D], F32)
        self.ident_in = di("ident", [128, 128], BF16)
        self.amask_in = di("amask", [2, 128, 128], BF16)
        self.u2_in = di("b_u2", [6, 2 * 16 * 128], BF16)
        self.v2_in = di("b_v2", [6, 2 * 512], BF16)
        self.bbias_in = di("b_bias", [128, 128], F32)
        self.kvb = {16: nc.dram_tensor("kvb_a", [2, 16, 131072], BF16),
                    4: nc.dram_tensor("kvb_b", [2, 4, 131072], BF16)}
        self.kvg = {16: nc.dram_tensor("kvg_a", [2, 2, 16, 131072], BF16),
                    4: nc.dram_tensor("kvg_b", [2, 2, 4, 131072], BF16)}
        self.gw_bf = nc.dram_tensor("gw_bf", [D * I_FF], BF16)
        self.uw_bf = nc.dram_tensor("uw_bf", [D * I_FF], BF16)
        self.dw_bf = nc.dram_tensor("dw_bf", [I_FF * D], BF16)
        self.ffn_pass = 0
        self.gu_conv = [None] * 32
        self.wd_conv = {}

        total = 200 * KB
        base0, _end = nc.bump_sbuf(total)
        base0 = ((base0 + 511) // 512) * 512
        self.cur = base0
        def sal(name, shape, dt, at=None, align=512):
            esz = 2 if dt == BF16 else 4
            nbytes = int(np.prod(shape[1:])) * esz
            if at is None:
                at = self.cur
                self.cur = at + ((nbytes + align - 1) // align) * align
            return nc.alloc_sbuf_tensor_at(name, shape, dt, offset=at)
        self.xs = [sal(f"x{r}", [128, D], F32) for r in range(8)]
        self.slot1 = sal("slot1", [128, 16 * KB], BF16)
        self.slot2 = sal("slot2", [128, 16 * KB], BF16)
        self.ident = sal("ident_sb", [128, 128], BF16)
        self.st_sum = sal("st_sum", [128, 128], F32)
        self.st_m2 = sal("st_m2", [128, 128], F32)
        self.st_istd = sal("st_istd", [128, 128], F32)
        self.st_nmi = sal("st_nmi", [128, 128], F32)
        self.st_t1 = sal("st_t1", [128, 128], F32)
        self.st_t2 = sal("st_t2", [128, 128], F32)
        self.st_t3 = sal("st_t3", [128, 128], F32)
        self.junk = sal("junk", [128, 128], BF16, align=32)
        self.bqs = sal("bqs", [128, 16], F32, align=32)
        self.bks = sal("bks", [128, 16], F32, align=32)
        self.bvs = sal("bvs", [128, 16], F32, align=32)
        self.gsc_sb = sal("gsc_sb", [128, 64], F32, align=32)
        self.usc_sb = sal("usc_sb", [128, 64], F32, align=32)
        self.ones_col = sal("ones_col", [128, 1], BF16, align=32)
        self.ones_row = sal("ones_row", [1, 128], BF16, align=32)
        self.hneg = sal("hneg", [128, 8], F32, align=32)
        self.rd_sb = sal("rd_sb", [128, 4], F32, align=32)
        self.fc_sb = sal("fc_sb", [128, 2], F32, align=32)
        self.cur = ((self.cur + 511) // 512) * 512
        A = self.cur

        self.ksb = sal("ksb", [128, 4 * D], BF16, at=A)
        self.vsb = sal("vsb", [128, 4 * 16 * 129], BF16, at=A + 16 * KB)
        p1 = A + 33 * KB
        self.pt_sb = sal("pt_sb", [128, 8 * 512], BF16, at=p1)
        self.ktmp = sal("ktmp", [128, KB], BF16, at=p1 + 8 * KB)
        self.vtmp = sal("vtmp", [128, KB], BF16, at=p1 + 10 * KB)
        self.vttmp = sal("vttmp", [128, KB], BF16, at=p1 + 12 * KB)
        self.osb = sal("osb", [128, 512], BF16, at=p1 + 14 * KB)
        self.wcol = [sal(f"wcol{i}", [128, 16 * 128], BF16, at=p1 + 15 * KB + i * 4 * KB)
                     for i in range(2)]
        self.u2_sb = sal("u2_sb", [6, 2 * 16 * 128], BF16, at=p1 + 15 * KB)
        self.v2_sb = sal("v2_sb", [6, 2 * 512], BF16, at=p1 + 23 * KB)
        self.bbias_sb = sal("bbias_sb", [128, 128], F32, at=p1 + 25 * KB)
        self.amask_sb = sal("amask_sb", [128, 2 * 128], BF16, at=p1 + 26 * KB)
        self.bo_row = sal("bo_row", [1, D], BF16, at=p1 + 27 * KB)
        attn_end = p1 + 31 * KB
        self.wocol = [sal(f"wocol{i}", [128, 16 * 512], BF16, at=A + i * 16 * KB) for i in range(2)]
        self.xsq_a = sal("xsq_a", [128, D], F32, at=A)
        self.xlntm_a = sal("xlntm_a", [128, D], BF16, at=A + 8 * KB)

        self.h_extra = sal("h_extra", [128, 16 * KB], BF16, at=A)
        f0 = A + 32 * KB
        self.gbuf = [sal(f"gbuf{i}", [128, 16 * 256], BF16, at=f0 + i * 8 * KB) for i in range(2)]
        self.ubuf = [sal(f"ubuf{i}", [128, 16 * 256], BF16, at=f0 + 16 * KB + i * 8 * KB) for i in range(2)]
        self.wdbuf = [sal(f"wdbuf{i}", [128, 8 * 512], BF16, at=f0 + i * 8 * KB) for i in range(2)]
        self.dsc_sb = sal("dsc_sb", [128, D], F32, at=f0 + 16 * KB)
        self.t1_sb = sal("t1_sb", [128, 512], F32, at=f0 + 24 * KB)
        self.t1b_sb = sal("t1b_sb", [128, 512], F32, at=f0 + 26 * KB + 512)
        f1 = f0 + 32 * KB
        self.sg_sb = sal("sg_sb", [128, 512], BF16, at=f1)
        ffn_end = f1 + KB
        self.xsq_f = sal("xsq_f", [128, D], F32, at=f0)
        self.xlntm_f = sal("xlntm_f", [128, D], BF16, at=f0 + 8 * KB)

        assert max(attn_end, ffn_end) <= base0 + total, (attn_end, ffn_end, base0 + total)
        assert base0 + total <= nc.sbuf_top, (base0, total, nc.sbuf_top)

        self.pb01 = nc.alloc_psum_tensor("pb01", [128, 1024], F32)
        self.pb23 = nc.alloc_psum_tensor("pb23", [128, 1024], F32)
        self.pbav = [nc.alloc_psum_tensor(f"pbav{i}", [128, 512], F32) for i in range(2)]
        self.pbf = nc.alloc_psum_tensor("pbf", [128, 2048], BF16)

        self.bank_free = [None] * 4
        self.av_free = [None, None]
        self.pbf_free = [None, None]
        self.wcol_free = [None, None]
        self.wocol_free = [None, None]
        self.gbuf_free = [None, None]
        self.ubuf_free = [None, None]
        self.pt_free = [None] * 8
        self.x_marks = [None] * 8
        self.xln_ready = None
        self.phase_end = []

    def pbk(self, i, c0=0, c1=512):
        t = self.pb01 if i < 2 else self.pb23
        base = (i % 2) * 512
        return t[:, base + c0: base + c1]

    def T(self, b, waits=()): return self.S.emit('T', b, waits)
    def V(self, b, waits=()): return self.S.emit('V', b, waits)
    def Sc(self, b, waits=()): return self.S.emit('S', b, waits)
    def Y(self, b, waits=(), stream='y_x'): return self.S.emit('Y', b, waits, sem=stream)
    def G(self, b, waits=(), stream='g_cst'): return self.S.emit('G', b, waits, sem=stream)

    def init_loads(self):
        for r in range(8):
            self.Y((lambda r: lambda e: e.dma_start(
                out=self.xs[r][:], in_=self.x_in[r]))(r))
        for r in range(8):
            self.x_marks[r] = self.S.mark('y_x')
        mi = self.G(lambda e: e.dma_start(out=self.ident[:], in_=self.ident_in[:, :]))
        marks = [mi]
        for dst, src in ((self.bqs, self.a_b['q']), (self.bks, self.a_b['k']),
                         (self.bvs, self.a_b['v'])):
            marks.append(self.G((lambda dst, src: lambda e: e.dma_start(
                out=dst[:], in_=bass.AP(src, 0, [[1, 128], [128, 16]]),
                allow_slow_non_contiguous=True))(dst, src)))
        marks.append(self.G(lambda e: e.dma_start(
            out=self.gsc_sb[:], in_=bass.AP(self.gsc_in, 0, [[1, 128], [128, 64]]),
            allow_slow_non_contiguous=True)))
        marks.append(self.G(lambda e: e.dma_start(
            out=self.usc_sb[:], in_=bass.AP(self.usc_in, 0, [[1, 128], [128, 64]]),
            allow_slow_non_contiguous=True)))
        marks.append(self.G(lambda e: e.dma_start(
            out=self.bo_row[:], in_=bass.AP(self.a_b['o'], 0, [[D, 1], [1, D]]))))
        mb = mmax(*marks)
        m1 = self.V(lambda e: e.tensor_scalar_mul(self.bqs[:], self.bqs[:], SC_A), waits=[mb])
        m2 = self.V(lambda e: e.tensor_scalar_mul(self.bks[:], self.bks[:], SC_A), waits=[mb])
        m3 = self.V(lambda e: e.memset(self.ones_col[:], 1.0))
        m4 = self.V(lambda e: e.memset(self.ones_row[:], 1.0))
        m5 = self.V(lambda e: e.memset(self.fc_sb[:, 0:1], -SHIFT_A))
        m6 = self.V(lambda e: e.memset(self.fc_sb[:, 1:2], -SHIFT_B))
        mz = [self.V((lambda t: lambda e: e.memset(t[:], 0.0))(t))
              for t in (self.st_sum, self.st_m2, self.st_istd,
                        self.st_nmi, self.st_t1, self.st_t2, self.st_t3)]
        self.const_ready = mmax(mb, m1, m2, m3, m4, m5, m6, *mz)
        self.ident_ready = mi
        self.phase_end = self.S.barrier_marks()

    def ln(self, for_attn):
        S = self.S
        S.begin_phase(mmax(self.phase_end))
        xsq = self.xsq_a if for_attn else self.xsq_f
        xlntm = self.xlntm_a if for_attn else self.xlntm_f
        slot = self.slot1 if for_attn else self.slot2
        red = []
        prev_v = self.const_ready
        prev_s = None
        for r in range(8):
            w = [self.x_marks[r]]
            prev_v = self.V((lambda r: lambda e: e.reduce_sum(
                self.st_sum[:, r:r + 1], self.xs[r][:], AX.X))(r), waits=w + [prev_v])
            red.append(prev_v)
            prev_s = self.Sc((lambda r, xsq: lambda e: e.activation(
                xsq[:], self.xs[r][:], AF.Square, accum_out=self.st_m2[:, r:r + 1]))(r, xsq),
                waits=w + ([prev_s] if prev_s else []))
            red.append(prev_s)
        mred = mmax(*red)
        self.V(lambda e: e.tensor_copy(self.junk[:], self.ident[:]), waits=[mred, self.ident_ready])
        mA = self.V(lambda e: e.tensor_scalar_mul(self.st_t1[:], self.st_sum[:], 1.0 / D), waits=[mred])
        mB = self.V(lambda e: e.tensor_scalar_mul(self.st_m2[:], self.st_m2[:], 1.0 / D), waits=[mred])
        mC = self.V(lambda e: e.tensor_mul(self.st_t2[:], self.st_t1[:], self.st_t1[:]), waits=[mA])
        mD = self.V(lambda e: e.tensor_sub(self.st_t3[:], self.st_m2[:], self.st_t2[:]), waits=[mB, mC])
        mD2 = self.V(lambda e: e.tensor_scalar_add(self.st_t3[:], self.st_t3[:], 1e-5), waits=[mD])
        mE = self.Sc(lambda e: e.activation(self.st_t2[:], self.st_t3[:], AF.Ln), waits=[mD2])
        mF = self.Sc(lambda e: e.activation(self.st_istd[:], self.st_t2[:], AF.Exp, scale=-0.5), waits=[mE])
        mG = self.V(lambda e: e.scalar_tensor_tensor(
            self.st_nmi[:], self.st_t1[:], -1.0, self.st_istd[:], ALU.mult, ALU.mult), waits=[mF])
        cps = []
        last_tr = [None, None]
        maff = None
        for r in range(8):
            wa = [mG] + [m for m in last_tr if m] + ([maff] if maff else [])
            maff = self.V((lambda r, xlntm: lambda e: e.tensor_scalar(
                xlntm[:], self.xs[r][:], self.st_istd[:, r:r + 1],
                self.st_nmi[:, r:r + 1], ALU.mult, ALU.add))(r, xlntm), waits=wa)
            for dt in range(16):
                half = dt % 2
                wt = [maff, self.ident_ready, self.pbf_free[half]]
                mt = self.T((lambda dt, half, xlntm: lambda e: e.transpose(
                    self.pbf[:, half * 1024:half * 1024 + 128],
                    xlntm[:, dt * 128:(dt + 1) * 128], self.ident[:]))(dt, half, xlntm), waits=wt)
                if dt >= 14:
                    last_tr[half] = mt
                mc = self.V((lambda r, dt, half, slot: lambda e: e.tensor_copy(
                    slot[:, dt * KB + r * 128: dt * KB + (r + 1) * 128],
                    self.pbf[:, half * 1024:half * 1024 + 128]))(r, dt, half, slot), waits=[mt])
                self.pbf_free[half] = mc
                cps.append(mc)
        self.xln_ready = mmax(*cps)
        self.phase_end = self.S.barrier_marks()

    def xlnT(self, for_attn, dt):
        slot = self.slot1 if for_attn else self.slot2
        return slot[:, dt * KB:(dt + 1) * KB]

    def qt_ot(self, h):
        return self.slot2[:, h * KB:(h + 1) * KB]

    def h_tile(self, it):
        if it < 32:
            return self.slot1[:, it * 512:(it + 1) * 512]
        return self.h_extra[:, (it - 32) * 512:(it - 31) * 512]

    def _proj16(self, wh, E, et, buf, dest_cb, scale, bias_col, for_attn=True):
        mload = self.G((lambda buf, wh, E, et: lambda e: e.dma_start(
            out=self.wcol[buf][:],
            in_=bass.AP(wh, et * 128, [[E, 128], [128 * E, 16], [1, 128]])))(buf, wh, E, et),
            waits=[self.wcol_free[buf]], stream=f'g_wc{buf}')
        mm = None
        evs = []
        for qc in range(2):
            bank = qc
            wt = [mload, self.xln_ready, self.bank_free[bank]]
            for dt in range(16):
                mm = self.T((lambda buf, dt, bank, qc: lambda e: e.matmul(
                    self.pbk(bank), self.wcol[buf][:, dt * 128:(dt + 1) * 128],
                    self.xlnT(True, dt)[:, qc * 512:(qc + 1) * 512],
                    start=(dt == 0), stop=(dt == 15)))(buf, dt, bank, qc),
                    waits=wt if dt == 0 else ())
            me = dest_cb(qc, bank, mm)
            self.bank_free[bank] = me
            evs.append(me)
        self.wcol_free[buf] = mm
        return evs

    def attn(self, variant):
        S = self.S
        a = variant == 'a'
        nKV = 16 if a else 4
        W = self.a_w if a else self.b_w
        kvb, kvg = self.kvb[nKV], self.kvg[nKV]
        E_kv = D if a else 512
        qscale = SC_A if a else SC_B
        kscale = SC_A if a else 1.0
        shift = SHIFT_A if a else SHIFT_B

        S.begin_phase(mmax(self.phase_end))
        mconst = self.G(lambda e: e.dma_start(
            out=self.amask_sb[:],
            in_=bass.AP(self.amask_in, 0, [[128, 128], [16384, 2], [1, 128]])))
        if not a:
            mconst = mmax(mconst, self.G(lambda e: e.dma_start(
                out=self.bbias_sb[:], in_=self.bbias_in[:, :])))
        m_ones = self.V(lambda e: e.memset(
            bass.AP(self.vsb, 128, [[4 * 16 * 129, 128], [129, 64], [1, 1]]), 1.0))

        base_tag = self.S.tag
        self.S.tag = base_tag + ":kv"
        wi = 0
        store_marks = []
        ktmp_free = None
        vttmp_free = None
        for kv in range(nKV):
            def k_dest(qc, bank, mm, kv=kv):
                if a:
                    bld = (lambda bank, qc, kv: lambda e: e.tensor_scalar(
                        self.ktmp[:, qc * 512:(qc + 1) * 512], self.pbk(bank), kscale,
                        self.bks[:, kv:kv + 1], ALU.mult, ALU.add))(bank, qc, kv)
                else:
                    bld = (lambda bank, qc: lambda e: e.tensor_scalar(
                        self.ktmp[:, qc * 512:(qc + 1) * 512], self.pbk(bank), kscale,
                        None, ALU.mult))(bank, qc)
                return self.V(bld, waits=[mm, ktmp_free, self.const_ready])
            evk = self._proj16(W['k'], E_kv, kv, wi % 2, k_dest, kscale, kv)
            wi += 1
            ms = self.Y((lambda kv: lambda e: e.dma_start(
                out=bass.AP(kvb, kv * 131072, [[1024, 128], [1, 1024]]),
                in_=self.ktmp[:]))(kv), waits=evk, stream='y_k')
            ktmp_free = ms
            store_marks.append(ms)

            def v_dest(qc, bank, mm, kv=kv):
                if a:
                    bld = (lambda bank, qc, kv: lambda e: e.tensor_scalar(
                        self.vtmp[:, qc * 512:(qc + 1) * 512], self.pbk(bank), 1.0,
                        self.bvs[:, kv:kv + 1], ALU.mult, ALU.add))(bank, qc, kv)
                else:
                    bld = (lambda bank, qc: lambda e: e.tensor_copy(
                        self.vtmp[:, qc * 512:(qc + 1) * 512], self.pbk(bank)))(bank, qc)
                return self.V(bld, waits=[mm, self.const_ready])
            evv = self._proj16(W['v'], E_kv, kv, wi % 2, v_dest, 1.0, kv)
            wi += 1
            vt = []
            for j in range(8):
                half = j % 2
                wt = [mmax(*evv), self.pbf_free[half], self.ident_ready]
                mt = self.T((lambda j, half: lambda e: e.transpose(
                    self.pbf[:, half * 1024:half * 1024 + 128],
                    self.vtmp[:, j * 128:(j + 1) * 128], self.ident[:]))(j, half), waits=wt)
                mc = self.V((lambda j, half: lambda e: e.tensor_copy(
                    self.vttmp[:, j * 128:(j + 1) * 128],
                    self.pbf[:, half * 1024:half * 1024 + 128]))(j, half),
                    waits=[mt, vttmp_free])
                self.pbf_free[half] = mc
                vt.append(mc)
            ms = self.Y((lambda kv, nKV: lambda e: e.dma_start(
                out=bass.AP(kvb, (nKV + kv) * 131072, [[128, 128], [16384, 8], [1, 128]]),
                in_=self.vttmp[:].rearrange("p (j d) -> p j d", j=8)))(kv, nKV),
                waits=[mmax(*vt)], stream='y_v')
            vttmp_free = ms
            store_marks.append(ms)

        mcc = S.emit('G', (lambda kvb, kvg: lambda e: e.collective_compute(
            "AllGather", ALU.bypass, replica_groups=[[0, 1], [2, 3], [4, 5], [6, 7]],
            ins=[kvb.ap()], outs=[kvg.ap()]))(kvb, kvg),
            waits=[mmax(*store_marks)], sem='cc', amt=1)

        self.S.tag = base_tag + ":q"
        qev = []
        for et in range(16):
            def q_dest(qc, bank, mm, et=et):
                if a:
                    bld = (lambda bank, qc, et: lambda e: e.tensor_scalar(
                        self.qt_ot(et)[:, qc * 512:(qc + 1) * 512], self.pbk(bank), qscale,
                        self.bqs[:, et:et + 1], ALU.mult, ALU.add))(bank, qc, et)
                else:
                    bld = (lambda bank, qc, et: lambda e: e.tensor_scalar(
                        self.qt_ot(et)[:, qc * 512:(qc + 1) * 512], self.pbk(bank), qscale,
                        None, ALU.mult))(bank, qc, et)
                return self.V(bld, waits=[mm, self.const_ready])
            qev += self._proj16(W['q'], D, et, wi % 2, q_dest, qscale, et)
            wi += 1
        q_ready = mmax(*qev)
        if not a:
            self.G(lambda e: e.dma_start(out=self.u2_sb[:], in_=self.u2_in[:, :]),
                   waits=[self.wcol_free[0], self.wcol_free[1]])
            self.G(lambda e: e.dma_start(out=self.v2_sb[:], in_=self.v2_in[:, :]),
                   waits=())
            alibi_ready = self.S.mark('g_cst')
            mconst = alibi_ready
            self.wcol_free = [None, None]
        else:
            alibi_ready = None

        grp_cons = {}
        slot_cons = {}
        ot_marks = []
        self.S.tag = base_tag + ":heads"
        for hg in range(4):
            par = hg % 2
            RNK = 2 * nKV * 131072
            gl_slot = [None] * 4
            if a:
                for s in range(4):
                    wl = [mcc, m_ones, slot_cons.get((hg - 1, s))]
                    ms1 = self.Y((lambda s, hg: lambda e: e.dma_start(
                        out=bass.AP(self.ksb, s * 2048, [[4 * D, 128], [1024, 2], [1, 1024]]),
                        in_=bass.AP(kvg, 0 * RNK + (4 * hg + s) * 131072,
                                    [[1024, 128], [RNK, 2], [1, 1024]])))(s, hg),
                        waits=wl, stream=f'y_s{s}')
                    ms2 = [self.Y((lambda s, hg, r: lambda e: e.dma_start(
                        out=bass.AP(self.vsb, s * 16 * 129 + r * 8 * 129,
                                    [[4 * 16 * 129, 128], [129, 8], [1, 128]]),
                        in_=bass.AP(kvg, r * RNK + (nKV + 4 * hg + s) * 131072,
                                    [[128, 128], [16384, 8], [1, 128]])))(s, hg, r),
                        waits=wl, stream=f'y_s{s}') for r in range(2)]
                    gl_slot[s] = mmax(ms1, *ms2)
            else:
                wl = [mcc, m_ones, grp_cons.get(hg - 2)]
                lms = []
                for r in range(2):
                    lms.append(self.Y((lambda par, r, hg: lambda e: e.dma_start(
                        out=bass.AP(self.ksb, par * 2048 + r * 1024, [[4 * D, 128], [1, 1024]]),
                        in_=bass.AP(kvg, r * RNK + hg * 131072, [[1024, 128], [1, 1024]])))(par, r, hg),
                        waits=wl, stream=f'y_g{hg % 2}'))
                    lms.append(self.Y((lambda par, r, hg: lambda e: e.dma_start(
                        out=bass.AP(self.vsb, par * 16 * 129 + r * 8 * 129,
                                    [[4 * 16 * 129, 128], [129, 8], [1, 128]]),
                        in_=bass.AP(kvg, r * RNK + (nKV + hg) * 131072,
                                    [[128, 128], [16384, 8], [1, 128]])))(par, r, hg),
                        waits=wl, stream=f'y_g{hg % 2}'))
                gl_slot = [mmax(*lms)] * 4
            cons = []
            for hi in range(4):
                h = 4 * hg + hi
                kcol = hi * 2048 if a else par * 2048
                vbase = hi * 16 * 129 if a else par * 16 * 129
                gl = gl_slot[hi]
                head_cons = []
                for qc in range(2):
                    issued = []
                    for g in range(4 * qc + 4):
                        issued += [g, g + 8]
                    nI = len(issued)
                    nP = nI // 2
                    gbs = [t % 8 for t in issued]
                    mex = {}
                    av_sched = [(idx, b) for idx in range(nI)
                                for b in range(max(0, gbs[idx] - 4 * qc), 4)]
                    bank_first = {}
                    bank_last = {}
                    for pos, (idx, b) in enumerate(av_sched):
                        j = b // 2
                        if j not in bank_first:
                            bank_first[j] = (idx, b)
                        bank_last[j] = (idx, b)
                    av_stop_bank = [None, None]

                    def do_scores_pair(p):
                        bp = p % 2
                        tens = self.pb01 if bp == 0 else self.pb23
                        mprevs = []
                        t = issued[2 * p]
                        gb = t % 8
                        b0 = max(0, gb - 4 * qc)
                        nq = 512 - b0 * 128
                        for k in (0, 1):
                            idx = 2 * p + k
                            t = issued[idx]
                            rk = t // 8
                            bank = 2 * bp + k
                            wt = [gl, q_ready, self.bank_free[bank]]
                            if not a:
                                wt.append(alibi_ready)
                            mm = self.T((lambda qc, t, bank, kcol, h, b0, nq: lambda e: e.matmul(
                                self.pbk(bank, b0 * 128, b0 * 128 + nq),
                                self.ksb[:, kcol + (t // 8) * 1024 + (t % 8) * 128:
                                         kcol + (t // 8) * 1024 + (t % 8) * 128 + 128],
                                self.qt_ot(h)[:, qc * 512 + b0 * 128:(qc + 1) * 512],
                                start=True, stop=a))(qc, t, bank, kcol, h, b0, nq), waits=wt)
                            if not a:
                                ucol = (rk * 16 + h) * 128
                                mm = self.T((lambda qc, bank, ucol, b0, nq: lambda e: e.matmul(
                                    self.pbk(bank, b0 * 128, b0 * 128 + nq),
                                    self.u2_sb[:, ucol: ucol + 128],
                                    self.v2_sb[:, qc * 512 + b0 * 128:(qc + 1) * 512],
                                    start=False, stop=True))(qc, bank, ucol, b0, nq), waits=())
                            head_cons.append(mm)
                            mprev = mm
                            b_blk = gb - 4 * qc
                            if 0 <= b_blk <= 3:
                                mprev = self.V((lambda bank, rk, b_blk: lambda e: e.tensor_add(
                                    self.pbk(bank, b_blk * 128, (b_blk + 1) * 128),
                                    self.pbk(bank, b_blk * 128, (b_blk + 1) * 128),
                                    self.amask_sb[:, rk * 128:(rk + 1) * 128]))(bank, rk, b_blk),
                                    waits=[mm, mconst])
                            mprevs.append(mprev)
                        psA = (2 * p) % 8
                        if a:
                            bias_ap = self.fc_sb[:, 0:1]
                        else:
                            bias_ap = self.bbias_sb[:, h * 8 + gb: h * 8 + gb + 1]
                        m = self.Sc((lambda tens, bp, psA, bias_ap, b0, nq: lambda e: e.activation(
                            bass.AP(self.pt_sb, psA * 512 + b0 * 128, [[4096, 128], [512, 2], [1, nq]]),
                            bass.AP(tens, b0 * 128, [[1024, 128], [512, 2], [1, nq]]),
                            AF.Exp, bias=bias_ap))(tens, bp, psA, bias_ap, b0, nq),
                            waits=mprevs + [self.pt_free[psA], self.pt_free[psA + 1],
                                            self.const_ready, mconst])
                        self.bank_free[2 * bp] = m
                        self.bank_free[2 * bp + 1] = m
                        mex[p] = m

                    def do_av_pair(p):
                        for k in (0, 1):
                            idx = 2 * p + k
                            t = issued[idx]
                            gb = t % 8
                            ps = idx % 8
                            last_av = None
                            for b in range(max(0, gb - 4 * qc), 4):
                                j = b // 2
                                col0 = (b % 2) * 256
                                first = bank_first[j] == (idx, b)
                                last = bank_last[j] == (idx, b)
                                wt = [mex[p]]
                                if first:
                                    wt.append(self.av_free[j])
                                last_av = self.T((lambda j, col0, ps, vbase, t, first, last, b:
                                                  lambda e: e.matmul(
                                    self.pbav[j][:, col0: col0 + 129],
                                    self.pt_sb[:, ps * 512 + b * 128: ps * 512 + (b + 1) * 128],
                                    self.vsb[:, vbase + t * 129: vbase + (t + 1) * 129],
                                    start=first, stop=last))(j, col0, ps, vbase, t, first, last, b),
                                    waits=wt)
                                if last:
                                    av_stop_bank[j] = last_av
                            self.pt_free[ps] = last_av
                            head_cons.append(last_av)

                    for p in range(nP):
                        do_scores_pair(p)
                        if p >= 2:
                            do_av_pair(p - 2)
                    if nP >= 2:
                        do_av_pair(nP - 2)
                    do_av_pair(nP - 1)
                    for b in range(4):
                        j, col0 = b // 2, (b % 2) * 256
                        self.V((lambda b, j, col0: lambda e: e.reciprocal(
                            self.rd_sb[:, b:b + 1], self.pbav[j][:, col0 + 128: col0 + 129]))(b, j, col0),
                            waits=[av_stop_bank[j]])
                    mj = self.V(lambda e: e.tensor_copy(self.junk[:], self.ident[:]),
                                waits=[self.ident_ready])
                    mos = [None, None]
                    for b in range(4):
                        j, col0 = b // 2, (b % 2) * 256
                        mo = self.V((lambda b, j, col0: lambda e: e.tensor_scalar(
                            self.osb[:, b * 128:(b + 1) * 128], self.pbav[j][:, col0: col0 + 128],
                            self.rd_sb[:, b:b + 1], None, ALU.mult))(b, j, col0), waits=[mj])
                        mos[j] = mo
                        half = b % 2
                        mt = self.T((lambda b, half: lambda e: e.transpose(
                            self.pbf[:, half * 1024:half * 1024 + 128],
                            self.osb[:, b * 128:(b + 1) * 128], self.ident[:]))(b, half),
                            waits=[mo, self.ident_ready, self.pbf_free[half]])
                        mc = self.V((lambda h, qc, b, half: lambda e: e.tensor_copy(
                            self.qt_ot(h)[:, qc * 512 + b * 128: qc * 512 + (b + 1) * 128],
                            self.pbf[:, half * 1024:half * 1024 + 128]))(h, qc, b, half), waits=[mt])
                        self.pbf_free[half] = mc
                        ot_marks.append(mc)
                    self.av_free = [mos[0], mos[1]]
                slot_cons[(hg, hi)] = mmax(*head_cons)
                cons += head_cons
            grp_cons[hg] = mmax(*cons)
        ot_ready = mmax(*ot_marks)

        self.S.tag = base_tag + ":proj"
        Wo = W['o']
        for ech in range(4):
            wf = [self.wocol_free[ech % 2]]
            if ech < 2:
                wf.append(grp_cons[2 + ech])
                wf.append(grp_cons[3])
            mw = self.G((lambda ech, Wo: lambda e: e.dma_start(
                out=self.wocol[ech % 2][:],
                in_=bass.AP(Wo, ech * 512, [[D, 128], [128 * D, 16], [1, 512]])))(ech, Wo),
                waits=wf, stream=f'g_wo{ech % 2}')
            mmlast = None
            for qt in range(8):
                bank = qt % 4
                wt = [mw, ot_ready, self.bank_free[bank]]
                if a:
                    wt.append(self.const_ready)
                mm = None
                for dt in range(16):
                    stop = (dt == 15) and not a
                    mm = self.T((lambda ech, qt, dt, bank, stop: lambda e: e.matmul(
                        self.pbk(bank), self.qt_ot(dt)[:, qt * 128:(qt + 1) * 128],
                        self.wocol[ech % 2][:, dt * 512:(dt + 1) * 512],
                        start=(dt == 0), stop=stop))(ech, qt, dt, bank, stop),
                        waits=wt if dt == 0 else ())
                if a:
                    mm = self.T((lambda ech, bank: lambda e: e.matmul(
                        self.pbk(bank), self.ones_row[:, 0:128],
                        self.bo_row[:, ech * 512:(ech + 1) * 512],
                        start=False, stop=True))(ech, bank), waits=())
                mmlast = mm
                mx = self.V((lambda qt, ech, bank: lambda e: e.tensor_add(
                    self.xs[qt][:, ech * 512:(ech + 1) * 512],
                    self.xs[qt][:, ech * 512:(ech + 1) * 512],
                    self.pbk(bank)))(qt, ech, bank), waits=[mm, self.x_marks[qt]])
                self.bank_free[bank] = mx
                self.x_marks[qt] = mx
            self.wocol_free[ech % 2] = mmlast
        self.phase_end = self.S.barrier_marks()

    def ffn(self):
        S = self.S
        S.begin_phase(mmax(self.phase_end))
        sg_free = None
        for qc in range(2):
            hm = [None] * 64
            hsum_stop = [None] * 4

            def emit_hsum(it):
                for qt in range(4):
                    j, col0 = qt // 2, (qt % 2) * 256
                    wt2 = [hm[it]]
                    if it == 0:
                        wt2 += [self.av_free[j], self.const_ready]
                    hsum_stop[qt] = self.T((lambda it, qt, j, col0: lambda e: e.matmul(
                        self.pbav[j][:, col0: col0 + 1],
                        self.h_tile(it)[:, qt * 128:(qt + 1) * 128],
                        self.ones_col[:],
                        start=(it == 0 and qt % 2 == 0),
                        stop=(it == 63 and qt % 2 == 1)))(it, qt, j, col0), waits=wt2)

            first = (self.ffn_pass == 0 and qc == 0)
            base_tag = self.S.tag.split(":gu")[0].split(":dn")[0]
            self.S.tag = base_tag + f":gu{qc}"
            for ip in range(32):
                buf = ip % 2
                if first:
                    mgl = self.G((lambda buf, ip: lambda e: e.dma_start(
                        out=self.gbuf[buf][:],
                        in_=bass.AP(self.g_wt, ip * 256, [[I_FF, 128], [128 * I_FF, 16], [1, 256]])))(buf, ip),
                        waits=[self.gbuf_free[buf]], stream=f'g_gb{buf}')
                    mul_ = self.G((lambda buf, ip: lambda e: e.dma_start(
                        out=self.ubuf[buf][:],
                        in_=bass.AP(self.u_wt, ip * 256, [[I_FF, 128], [128 * I_FF, 16], [1, 256]])))(buf, ip),
                        waits=[self.ubuf_free[buf]], stream=f'g_ub{buf}')
                    sg1 = self.Y((lambda buf, ip: lambda e: e.dma_start(
                        out=bass.AP(self.gw_bf, ip * 524288, [[4096, 128], [1, 4096]]),
                        in_=self.gbuf[buf][:]))(buf, ip), waits=[mgl], stream=f'y_cg{buf}')
                    sg2 = self.Y((lambda buf, ip: lambda e: e.dma_start(
                        out=bass.AP(self.uw_bf, ip * 524288, [[4096, 128], [1, 4096]]),
                        in_=self.ubuf[buf][:]))(buf, ip), waits=[mul_], stream=f'y_cu{buf}')
                    self.gu_conv[ip] = (sg1, sg2)
                else:
                    mgl = self.G((lambda buf, ip: lambda e: e.dma_start(
                        out=self.gbuf[buf][:],
                        in_=bass.AP(self.gw_bf, ip * 524288, [[4096, 128], [1, 4096]])))(buf, ip),
                        waits=[self.gbuf_free[buf], self.gu_conv[ip][0]], stream=f'g_gb{buf}')
                    mul_ = self.G((lambda buf, ip: lambda e: e.dma_start(
                        out=self.ubuf[buf][:],
                        in_=bass.AP(self.uw_bf, ip * 524288, [[4096, 128], [1, 4096]])))(buf, ip),
                        waits=[self.ubuf_free[buf], self.gu_conv[ip][1]], stream=f'g_ub{buf}')
                glast = ulast = None
                for k in range(2):
                    it = ip * 2 + k
                    bg = (it % 2) * 2
                    bu = bg + 1
                    wt = [mgl, self.xln_ready, self.bank_free[bg]]
                    for dt in range(16):
                        glast = self.T((lambda buf, dt, k, qc, bg: lambda e: e.matmul(
                            self.pbk(bg),
                            self.gbuf[buf][:, dt * 256 + k * 128: dt * 256 + (k + 1) * 128],
                            self.xlnT(False, dt)[:, qc * 512:(qc + 1) * 512],
                            start=(dt == 0), stop=(dt == 15)))(buf, dt, k, qc, bg),
                            waits=wt if dt == 0 else ())
                    wt = [mul_, self.xln_ready, self.bank_free[bu]]
                    for dt in range(16):
                        ulast = self.T((lambda buf, dt, k, qc, bu: lambda e: e.matmul(
                            self.pbk(bu),
                            self.ubuf[buf][:, dt * 256 + k * 128: dt * 256 + (k + 1) * 128],
                            self.xlnT(False, dt)[:, qc * 512:(qc + 1) * 512],
                            start=(dt == 0), stop=(dt == 15)))(buf, dt, k, qc, bu),
                            waits=wt if dt == 0 else ())
                    msg = self.Sc((lambda it, bg: lambda e: e.activation(
                        self.sg_sb[:], self.pbk(bg), AF.Silu,
                        scale=self.gsc_sb[:, it:it + 1]))(it, bg), waits=[glast, sg_free])
                    self.bank_free[bg] = msg
                    mh = self.V((lambda it, bu: lambda e: e.scalar_tensor_tensor(
                        self.h_tile(it), self.pbk(bu), self.usc_sb[:, it:it + 1],
                        self.sg_sb[:], ALU.mult, ALU.mult))(it, bu), waits=[ulast, msg])
                    self.bank_free[bu] = mh
                    sg_free = mh
                    hm[it] = mh
                    if it >= 1:
                        emit_hsum(it - 1)
                self.gbuf_free[buf] = mmax(glast, self.gu_conv[ip][0] if first else None)
                self.ubuf_free[buf] = mmax(ulast, self.gu_conv[ip][1] if first else None)
            emit_hsum(63)
            hneg_m = []
            for qt in range(4):
                j, col0 = qt // 2, (qt % 2) * 256
                hneg_m.append(self.Sc((lambda qc, qt, j, col0: lambda e: e.mul(
                    self.hneg[:, qc * 4 + qt: qc * 4 + qt + 1],
                    self.pbav[j][:, col0: col0 + 1], -128.0))(qc, qt, j, col0),
                    waits=[hsum_stop[2 * j + 1]]))
            self.av_free = [hneg_m[1], hneg_m[3]]
            mhneg = mmax(*hneg_m)
            h_ready = mmax(*hm)
            self.S.tag = base_tag + f":dn{qc}"
            mdsc = self.G(lambda e: e.dma_start(out=self.dsc_sb[:], in_=self.dsc_in[:, :]),
                          waits=[h_ready, self.ubuf_free[0]], stream='g_dsc')
            wdbuf_free = [mmax(self.gbuf_free[0], h_ready), mmax(self.gbuf_free[1], h_ready)]
            t1_free = self.ubuf_free[1]
            t1b_free = self.ubuf_free[1]
            for ech in range(4):
                stops = [None] * 4
                for ic in range(8):
                    buf = ic % 2
                    if first:
                        mwl = self.G((lambda buf, ic, ech: lambda e: e.dma_start(
                            out=self.wdbuf[buf][:],
                            in_=bass.AP(self.d_wt, (ic * 8 * 128) * D + ech * 512,
                                        [[D, 128], [128 * D, 8], [1, 512]])))(buf, ic, ech),
                            waits=[wdbuf_free[buf]], stream=f'g_wd{buf}')
                        scv = self.Y((lambda buf, ic, ech: lambda e: e.dma_start(
                            out=bass.AP(self.dw_bf, (ech * 8 + ic) * 524288,
                                        [[4096, 128], [1, 4096]]),
                            in_=self.wdbuf[buf][:]))(buf, ic, ech), waits=[mwl], stream=f'y_cd{buf}')
                        self.wd_conv[(ech, ic)] = scv
                    else:
                        mwl = self.G((lambda buf, ic, ech: lambda e: e.dma_start(
                            out=self.wdbuf[buf][:],
                            in_=bass.AP(self.dw_bf, (ech * 8 + ic) * 524288,
                                        [[4096, 128], [1, 4096]])))(buf, ic, ech),
                            waits=[wdbuf_free[buf], self.wd_conv[(ech, ic)]], stream=f'g_wd{buf}')
                    for i8 in range(8):
                        it = ic * 8 + i8
                        for qt in range(4):
                            if it == 0:
                                wt = [mwl, h_ready, mhneg, self.bank_free[qt]]
                            elif i8 == 0:
                                wt = [mwl]
                            else:
                                wt = ()
                            stops[qt] = self.T((lambda buf, it, i8, qt: lambda e: e.matmul(
                                self.pbk(qt), self.h_tile(it)[:, qt * 128:(qt + 1) * 128],
                                self.wdbuf[buf][:, i8 * 512:(i8 + 1) * 512],
                                start=(it == 0), stop=(it == 63)))(buf, it, i8, qt),
                                waits=wt)
                    wdbuf_free[buf] = mmax(stops[3], self.wd_conv.get((ech, ic)) if first else None)
                for qt in range(4):
                    rt = qc * 4 + qt
                    m1 = self.Sc((lambda qt, qc: lambda e: e.add(
                        self.t1_sb[:], self.pbk(qt),
                        self.hneg[:, qc * 4 + qt: qc * 4 + qt + 1]))(qt, qc),
                        waits=[stops[qt], mhneg, t1_free])
                    self.bank_free[qt] = m1
                    m2 = self.V((lambda ech: lambda e: e.tensor_mul(
                        self.t1b_sb[:], self.t1_sb[:],
                        self.dsc_sb[:, ech * 512:(ech + 1) * 512]))(ech),
                        waits=[m1, mdsc, t1b_free])
                    t1_free = m2
                    m3 = self.V((lambda rt, ech: lambda e: e.tensor_add(
                        self.xs[rt][:, ech * 512:(ech + 1) * 512],
                        self.xs[rt][:, ech * 512:(ech + 1) * 512],
                        self.t1b_sb[:]))(rt, ech), waits=[m2, self.x_marks[rt]])
                    t1b_free = m3
                    self.x_marks[rt] = m3
            self.gbuf_free = [wdbuf_free[0], wdbuf_free[1]]
            self.ubuf_free = [mmax(t1_free), mmax(t1_free, t1b_free)]
        self.ffn_pass += 1
        self.phase_end = self.S.barrier_marks()

    def finish(self):
        self.S.begin_phase(mmax(self.phase_end))
        for r in range(8):
            self.Y((lambda r: lambda e: e.dma_start(
                out=self.out_t[r], in_=self.xs[r][:]))(r), waits=[self.x_marks[r]])
        self.S.emit_wait_only('Y', [self.S.mark('y_x')])

    def realize(self):
        nc, S = self.nc, self.S
        import contextlib
        with contextlib.ExitStack() as st:
            sems = {s: st.enter_context(nc.semaphore(s)) for s in ALL_SEMS}
            blk = st.enter_context(nc.Block())
            attr = {'T': 'tensor', 'V': 'vector', 'S': 'scalar', 'Y': 'sync', 'G': 'gpsimd'}
            for ename in ENGINES:
                def runner(eng, ename=ename):
                    hiwater = {}
                    for (e2, waits, builder, inc) in S.ops:
                        if e2 != ename:
                            continue
                        for (s, v) in waits:
                            if hiwater.get(s, -1) >= v:
                                continue
                            hiwater[s] = v
                            eng.wait_ge(sems[s], v)
                        if builder is None:
                            continue
                        ins = builder(eng)
                        if inc is not None:
                            ins.then_inc(sems[inc[0]], inc[1])
                getattr(blk, attr[ename])(runner)
        return nc


def build(n_sub=4):
    b = Builder(n_sub)
    b.init_loads()
    steps = [('ln_a',), ('attn', 'a'), ('ln_f',), ('ffn',), ('ln_a',), ('attn', 'b'),
             ('ln_f',), ('ffn',)][: 2 * n_sub]
    for i, st in enumerate(steps):
        b.S.tag = f"{i}:{st[0]}" + (st[1] if len(st) > 1 else "")
        if st[0] == 'ln_a':
            b.ln(True)
        elif st[0] == 'ln_f':
            b.ln(False)
        elif st[0] == 'attn':
            b.attn(st[1])
        else:
            b.ffn()
    b.finish()
    return b.realize()


def host_consts(P):
    d = {}
    d["ident"] = np.eye(128, dtype=BF)
    kp = np.arange(128)[:, None]
    qp = np.arange(128)[None, :]
    am = np.zeros((2, 128, 128), np.float32)
    for rk in range(2):
        am[rk] = np.where((2 * kp + rk) <= (2 * qp + P), 0.0, NEG)
    d["amask"] = am.astype(BF)
    slopes = np.repeat([2.0 ** (-0.5 * (i + 1)) for i in range(4)], 4)[:16]

    def hilo(x):
        hi = x.astype(BF).astype(np.float32)
        lo = (x - hi).astype(BF).astype(np.float32)
        return hi, lo

    u2t = np.zeros((6, 2 * 16 * 128), np.float32)
    for rk in range(2):
        for h in range(16):
            c = (rk * 16 + h) * 128
            u1h, u1l = hilo(slopes[h] * (2.0 * np.arange(128, dtype=np.float64) + rk))
            s2h, s2l = hilo(np.full(128, slopes[h], np.float64))
            u2t[0, c:c + 128] = u1h
            u2t[1, c:c + 128] = u1l
            u2t[2, c:c + 128] = s2h
            u2t[3, c:c + 128] = s2h
            u2t[4, c:c + 128] = s2l
            u2t[5, c:c + 128] = s2l
    d["b_u2"] = u2t.astype(BF)
    v2 = np.zeros((6, 2 * 512), np.float32)
    v2[0, :] = 1.0
    v2[1, :] = 1.0
    for qc in range(2):
        q = np.arange(512)
        w2 = -(256.0 * (4 * qc + q // 128) + 2.0 * (q % 128) + P)
        wh, wl = hilo(w2.astype(np.float64))
        v2[2, qc * 512:(qc + 1) * 512] = wh
        v2[3, qc * 512:(qc + 1) * 512] = wl
        v2[4, qc * 512:(qc + 1) * 512] = wh
        v2[5, qc * 512:(qc + 1) * 512] = wl
    d["b_v2"] = v2.astype(BF)
    bb = np.zeros((128, 128), np.float32)
    for h in range(16):
        for gb in range(8):
            bb[:, h * 8 + gb] = -SHIFT_B + slopes[h] * 256.0 * gb
    d["b_bias"] = bb
    return d


def host_weights(z):
    d = {}
    d["a_wq"] = np.ascontiguousarray(z["L0_attn_a_Wq"].T)
    d["a_wk"] = np.ascontiguousarray(z["L0_attn_a_Wk"].T)
    d["a_wv"] = np.ascontiguousarray(z["L0_attn_a_Wv"].T)
    d["a_wo"] = np.ascontiguousarray(z["L0_attn_a_Wo"].T)
    d["a_bq"] = z["L0_attn_a_bq"]
    d["a_bk"] = z["L0_attn_a_bk"]
    d["a_bv"] = z["L0_attn_a_bv"]
    d["a_bo"] = z["L0_attn_a_bo"]
    d["b_wq"] = z["L0_attn_b_Wq"]
    d["b_wk"] = z["L0_attn_b_Wk"]
    d["b_wv"] = z["L0_attn_b_Wv"]
    d["b_wo"] = z["L0_attn_b_Wo"]
    d["g_wt"] = np.ascontiguousarray(z["L0_ffn_gate_w"].T)
    d["u_wt"] = np.ascontiguousarray(z["L0_ffn_up_w"].T)
    d["d_wt"] = np.ascontiguousarray(z["L0_ffn_down_w"].T)
    d["gsc"] = z["L0_ffn_gate_scale"]
    d["usc"] = z["L0_ffn_up_scale"]
    d["dsc_b"] = np.ascontiguousarray(np.broadcast_to(z["L0_ffn_down_scale"], (128, D)))
    return d


_NC_CACHE = {}


def run(inputs, n_sub=4):
    if n_sub not in _NC_CACHE:
        _NC_CACHE[n_sub] = build(n_sub)
    nc = _NC_CACHE[n_sub]
    X = inputs["L0_attn_a_X"]
    w = host_weights(inputs)
    consts = [host_consts(0), host_consts(1)]
    in_maps = []
    for c in range(8):
        b, P = c // 2, c % 2
        d = dict(w)
        d.update(consts[P])
        d["x_in"] = np.ascontiguousarray(X[b].reshape(8, 128, 2, D)[:, :, P])
        in_maps.append(d)
    res = run_bass_kernel_spmd(nc, in_maps, core_ids=list(range(8)))
    out = np.empty((4, 2048, D), np.float32)
    ov = out.reshape(4, 8, 128, 2, D)
    for c in range(8):
        b, P = c // 2, c % 2
        ov[b, :, :, P] = res.results[c]["out"]
    return out


def kernel(**inputs):
    return run(inputs, n_sub=4)


# version 09 (final)
# speedup vs baseline: 1.0999x; 1.0093x over previous
import numpy as np
import ml_dtypes
import concourse.bass as bass
import concourse.mybir as mybir
from concourse.bass_utils import run_bass_kernel_spmd

F32 = mybir.dt.float32
BF16 = mybir.dt.bfloat16
I32 = mybir.dt.int32
AF = mybir.ActivationFunctionType
ALU = mybir.AluOpType
AX = mybir.AxisListType
BF = ml_dtypes.bfloat16

D = 2048
I_FF = 8192
NEG = -1.0e30
SHIFT_A = 24.0
SHIFT_B = 12.0
SC_A = float(1.0 / np.sqrt(np.sqrt(128.0)))
SC_B = float(1.0 / np.sqrt(128.0))
SLOPES4 = [2.0 ** (-0.5 * (i + 1)) for i in range(4)]
KB = 1024

ENGINES = ('G', 'Y', 'T', 'V', 'S')
SEM_OF = dict(T='td', V='vd', S='sd', Y='y_x', G='g_cst')
DMA_SEMS = ('y_x', 'y_k', 'y_v', 'y_g0', 'y_g1',
            'y_cg0', 'y_cg1', 'y_cu0', 'y_cu1', 'y_cd0', 'y_cd1',
            'y_s0', 'y_s1', 'y_s2', 'y_s3',
            'g_cst', 'g_dsc', 'g_wc0', 'g_wc1', 'g_wo0', 'g_wo1',
            'g_gb0', 'g_gb1', 'g_ub0', 'g_ub1', 'g_wd0', 'g_wd1')
ALL_SEMS = ('td', 'vd', 'sd', 'cc') + DMA_SEMS


class Sched:
    def __init__(self):
        self.ops = []
        self.cnt = {s: 0 for s in ALL_SEMS}
        self.gate = []
        self.gated = set(ENGINES)
        self.tag = "init"
        self.tags = []

    def begin_phase(self, marks):
        self.gate = [m for m in marks if m is not None and m[1] > 0]
        self.gated = set()

    @staticmethod
    def _flat(waits):
        out = []
        for w in waits:
            if w is None:
                continue
            if isinstance(w, list):
                out.extend(x for x in w if x is not None and x[1] > 0)
            elif w[1] > 0:
                out.append(w)
        return out

    def emit(self, eng, builder, waits=(), sem=None, amt=None):
        waits = self._flat(waits)
        if eng not in self.gated:
            waits = self.gate + waits
            self.gated.add(eng)
        if sem is None:
            sem = SEM_OF[eng]
        if amt is None:
            amt = 16 if sem[:2] in ('y_', 'g_') else 1
        self.ops.append((eng, waits, builder, (sem, amt)))
        self.tags.append(self.tag)
        self.cnt[sem] += amt
        return (sem, self.cnt[sem])

    def emit_wait_only(self, eng, waits):
        self.ops.append((eng, self._flat(waits), None, None))
        self.tags.append(self.tag)

    def mark(self, sem):
        return (sem, self.cnt[sem])

    def barrier_marks(self):
        return [(s, v) for s, v in self.cnt.items() if v > 0 and s != 'cc']


def mmax(*marks):
    best = {}
    for m in marks:
        if m is None:
            continue
        items = m if isinstance(m, list) else [m]
        for it in items:
            if it is None:
                continue
            s, v = it
            best[s] = max(best.get(s, 0), v)
    return [(s, v) for s, v in best.items() if v > 0]


class Builder:
    def __init__(self, n_sub=4):
        nc = bass.Bass("TRN2", target_bir_lowering=False)
        self.nc = nc
        self.n_sub = n_sub
        self.S = Sched()
        di = lambda n, s, d: nc.dram_tensor(n, s, d, kind="ExternalInput")
        self.x_in = di("x_in", [8, 128, D], F32)
        self.out_t = nc.dram_tensor("out", [8, 128, D], F32, kind="ExternalOutput")
        self.a_w = dict(q=di("a_wq", [D, D], F32), k=di("a_wk", [D, D], F32),
                        v=di("a_wv", [D, D], F32), o=di("a_wo", [D, D], F32))
        self.b_w = dict(q=di("b_wq", [D, D], F32), k=di("b_wk", [D, 512], F32),
                        v=di("b_wv", [D, 512], F32), o=di("b_wo", [D, D], F32))
        self.a_b = dict(q=di("a_bq", [D], F32), k=di("a_bk", [D], F32),
                        v=di("a_bv", [D], F32), o=di("a_bo", [D], F32))
        self.g_wt = di("g_wt", [D, I_FF], I32)
        self.u_wt = di("u_wt", [D, I_FF], I32)
        self.d_wt = di("d_wt", [I_FF, D], I32)
        self.gsc_in = di("gsc", [I_FF], F32)
        self.usc_in = di("usc", [I_FF], F32)
        self.dsc_in = di("dsc_b", [128, D], F32)
        self.ident_in = di("ident", [128, 128], BF16)
        self.amask_in = di("amask", [2, 128, 128], BF16)
        self.u2_in = di("b_u2", [6, 2 * 16 * 128], BF16)
        self.v2_in = di("b_v2", [6, 2 * 512], BF16)
        self.bbias_in = di("b_bias", [128, 128], F32)
        self.kvb = {16: nc.dram_tensor("kvb_a", [2, 16, 131072], BF16),
                    4: nc.dram_tensor("kvb_b", [2, 4, 131072], BF16)}
        self.kvg = {16: nc.dram_tensor("kvg_a", [2, 2, 16, 131072], BF16),
                    4: nc.dram_tensor("kvg_b", [2, 2, 4, 131072], BF16)}
        self.gw_bf = nc.dram_tensor("gw_bf", [D * I_FF], BF16)
        self.uw_bf = nc.dram_tensor("uw_bf", [D * I_FF], BF16)
        self.dw_bf = nc.dram_tensor("dw_bf", [I_FF * D], BF16)
        self.ffn_pass = 0
        self.gu_conv = [None] * 32
        self.wd_conv = {}

        total = 200 * KB
        base0, _end = nc.bump_sbuf(total)
        base0 = ((base0 + 511) // 512) * 512
        self.cur = base0
        def sal(name, shape, dt, at=None, align=512):
            esz = 2 if dt == BF16 else 4
            nbytes = int(np.prod(shape[1:])) * esz
            if at is None:
                at = self.cur
                self.cur = at + ((nbytes + align - 1) // align) * align
            return nc.alloc_sbuf_tensor_at(name, shape, dt, offset=at)
        self.xs = [sal(f"x{r}", [128, D], F32) for r in range(8)]
        self.slot1 = sal("slot1", [128, 16 * KB], BF16)
        self.slot2 = sal("slot2", [128, 16 * KB], BF16)
        self.ident = sal("ident_sb", [128, 128], BF16)
        self.st_sum = sal("st_sum", [128, 128], F32)
        self.st_m2 = sal("st_m2", [128, 128], F32)
        self.st_istd = sal("st_istd", [128, 128], F32)
        self.st_nmi = sal("st_nmi", [128, 128], F32)
        self.st_t1 = sal("st_t1", [128, 128], F32)
        self.st_t2 = sal("st_t2", [128, 128], F32)
        self.st_t3 = sal("st_t3", [128, 128], F32)
        self.junk = sal("junk", [128, 128], BF16, align=32)
        self.bqs = sal("bqs", [128, 16], F32, align=32)
        self.bks = sal("bks", [128, 16], F32, align=32)
        self.bvs = sal("bvs", [128, 16], F32, align=32)
        self.gsc_sb = sal("gsc_sb", [128, 64], F32, align=32)
        self.usc_sb = sal("usc_sb", [128, 64], F32, align=32)
        self.ones_col = sal("ones_col", [128, 1], BF16, align=32)
        self.ones_row = sal("ones_row", [1, 128], BF16, align=32)
        self.hneg = sal("hneg", [128, 8], F32, align=32)
        self.rd_sb = sal("rd_sb", [128, 4], F32, align=32)
        self.fc_sb = sal("fc_sb", [128, 2], F32, align=32)
        self.cur = ((self.cur + 511) // 512) * 512
        A = self.cur

        self.ksb = sal("ksb", [128, 4 * D], BF16, at=A)
        self.vsb = sal("vsb", [128, 4 * 16 * 129], BF16, at=A + 16 * KB)
        p1 = A + 33 * KB
        self.pt_sb = sal("pt_sb", [128, 8 * 512], BF16, at=p1)
        self.ktmp = sal("ktmp", [128, KB], BF16, at=p1 + 8 * KB)
        self.vtmp = sal("vtmp", [128, KB], BF16, at=p1 + 10 * KB)
        self.vttmp = sal("vttmp", [128, KB], BF16, at=p1 + 12 * KB)
        self.osb = sal("osb", [128, 512], BF16, at=p1 + 14 * KB)
        self.wcol = [sal(f"wcol{i}", [128, 16 * 128], BF16, at=p1 + 15 * KB + i * 4 * KB)
                     for i in range(2)]
        self.u2_sb = sal("u2_sb", [6, 2 * 16 * 128], BF16, at=p1 + 15 * KB)
        self.v2_sb = sal("v2_sb", [6, 2 * 512], BF16, at=p1 + 23 * KB)
        self.bbias_sb = sal("bbias_sb", [128, 128], F32, at=p1 + 25 * KB)
        self.amask_sb = sal("amask_sb", [128, 2 * 128], BF16, at=p1 + 26 * KB)
        self.bo_row = sal("bo_row", [1, D], BF16, at=p1 + 27 * KB)
        attn_end = p1 + 31 * KB
        self.wocol = [sal(f"wocol{i}", [128, 16 * 512], BF16, at=A + i * 16 * KB) for i in range(2)]
        self.xsq_a = sal("xsq_a", [128, D], F32, at=A)
        self.xlntm_a = sal("xlntm_a", [128, D], BF16, at=A + 8 * KB)

        self.h_extra = sal("h_extra", [128, 16 * KB], BF16, at=A)
        f0 = A + 32 * KB
        self.gbuf = [sal(f"gbuf{i}", [128, 16 * 256], BF16, at=f0 + i * 8 * KB) for i in range(2)]
        self.ubuf = [sal(f"ubuf{i}", [128, 16 * 256], BF16, at=f0 + 16 * KB + i * 8 * KB) for i in range(2)]
        self.wdbuf = [sal(f"wdbuf{i}", [128, 8 * 512], BF16, at=f0 + i * 8 * KB) for i in range(2)]
        self.dsc_sb = sal("dsc_sb", [128, D], F32, at=f0 + 16 * KB)
        self.t1_sb = sal("t1_sb", [128, 512], F32, at=f0 + 24 * KB)
        self.t1b_sb = sal("t1b_sb", [128, 512], F32, at=f0 + 26 * KB + 512)
        f1 = f0 + 32 * KB
        self.sg_sb = sal("sg_sb", [128, 512], BF16, at=f1)
        ffn_end = f1 + KB
        self.xsq_f = sal("xsq_f", [128, D], F32, at=f0)
        self.xlntm_f = sal("xlntm_f", [128, D], BF16, at=f0 + 8 * KB)

        assert max(attn_end, ffn_end) <= base0 + total, (attn_end, ffn_end, base0 + total)
        assert base0 + total <= nc.sbuf_top, (base0, total, nc.sbuf_top)

        self.pb01 = nc.alloc_psum_tensor("pb01", [128, 1024], F32)
        self.pb23 = nc.alloc_psum_tensor("pb23", [128, 1024], F32)
        self.pbav = [nc.alloc_psum_tensor(f"pbav{i}", [128, 512], F32) for i in range(2)]
        self.pbf = nc.alloc_psum_tensor("pbf", [128, 2048], BF16)

        self.bank_free = [None] * 4
        self.av_free = [None, None]
        self.pbf_free = [None, None]
        self.wcol_free = [None, None]
        self.wocol_free = [None, None]
        self.gbuf_free = [None, None]
        self.ubuf_free = [None, None]
        self.pt_free = [None] * 8
        self.x_marks = [None] * 8
        self.xln_ready = None
        self.stats_free = None
        self.phase_end = []

    def pbk(self, i, c0=0, c1=512):
        t = self.pb01 if i < 2 else self.pb23
        base = (i % 2) * 512
        return t[:, base + c0: base + c1]

    def T(self, b, waits=()): return self.S.emit('T', b, waits)
    def V(self, b, waits=()): return self.S.emit('V', b, waits)
    def Sc(self, b, waits=()): return self.S.emit('S', b, waits)
    def Y(self, b, waits=(), stream='y_x'): return self.S.emit('Y', b, waits, sem=stream)
    def G(self, b, waits=(), stream='g_cst'): return self.S.emit('G', b, waits, sem=stream)

    def init_loads(self):
        for r in range(8):
            self.Y((lambda r: lambda e: e.dma_start(
                out=self.xs[r][:], in_=self.x_in[r]))(r))
        for r in range(8):
            self.x_marks[r] = self.S.mark('y_x')
        mi = self.G(lambda e: e.dma_start(out=self.ident[:], in_=self.ident_in[:, :]))
        marks = [mi]
        for dst, src in ((self.bqs, self.a_b['q']), (self.bks, self.a_b['k']),
                         (self.bvs, self.a_b['v'])):
            marks.append(self.G((lambda dst, src: lambda e: e.dma_start(
                out=dst[:], in_=bass.AP(src, 0, [[1, 128], [128, 16]]),
                allow_slow_non_contiguous=True))(dst, src)))
        marks.append(self.G(lambda e: e.dma_start(
            out=self.gsc_sb[:], in_=bass.AP(self.gsc_in, 0, [[1, 128], [128, 64]]),
            allow_slow_non_contiguous=True)))
        marks.append(self.G(lambda e: e.dma_start(
            out=self.usc_sb[:], in_=bass.AP(self.usc_in, 0, [[1, 128], [128, 64]]),
            allow_slow_non_contiguous=True)))
        marks.append(self.G(lambda e: e.dma_start(
            out=self.bo_row[:], in_=bass.AP(self.a_b['o'], 0, [[D, 1], [1, D]]))))
        mb = mmax(*marks)
        m1 = self.V(lambda e: e.tensor_scalar_mul(self.bqs[:], self.bqs[:], SC_A), waits=[mb])
        m2 = self.V(lambda e: e.tensor_scalar_mul(self.bks[:], self.bks[:], SC_A), waits=[mb])
        m3 = self.V(lambda e: e.memset(self.ones_col[:], 1.0))
        m4 = self.V(lambda e: e.memset(self.ones_row[:], 1.0))
        m5 = self.V(lambda e: e.memset(self.fc_sb[:, 0:1], -SHIFT_A))
        m6 = self.V(lambda e: e.memset(self.fc_sb[:, 1:2], -SHIFT_B))
        mz = [self.V((lambda t: lambda e: e.memset(t[:], 0.0))(t))
              for t in (self.st_sum, self.st_m2, self.st_istd,
                        self.st_nmi, self.st_t1, self.st_t2, self.st_t3)]
        self.const_ready = mmax(mb, m1, m2, m3, m4, m5, m6, *mz)
        self.ident_ready = mi
        self.phase_end = self.S.barrier_marks()

    def ln(self, for_attn):
        S = self.S
        xsq = self.xsq_a if for_attn else self.xsq_f
        xlntm = self.xlntm_a if for_attn else self.xlntm_f
        slot = self.slot1 if for_attn else self.slot2
        red = []
        prev_v = mmax(self.const_ready, self.stats_free)
        for r in range(8):
            prev_v = self.V((lambda r: lambda e: e.reduce_sum(
                self.st_sum[:, r:r + 1], self.xs[r][:], AX.X))(r),
                waits=[self.x_marks[r], prev_v])
            red.append(prev_v)
        S.begin_phase(mmax(self.phase_end))
        prev_s = None
        for r in range(8):
            prev_s = self.Sc((lambda r, xsq: lambda e: e.activation(
                xsq[:], self.xs[r][:], AF.Square, accum_out=self.st_m2[:, r:r + 1]))(r, xsq),
                waits=[self.x_marks[r]] + ([prev_s] if prev_s else []))
            red.append(prev_s)
        mred = mmax(*red)
        self.V(lambda e: e.tensor_copy(self.junk[:], self.ident[:]), waits=[mred, self.ident_ready])
        mA = self.V(lambda e: e.tensor_scalar_mul(self.st_t1[:], self.st_sum[:], 1.0 / D), waits=[mred])
        self.stats_free = mA
        mB = self.V(lambda e: e.tensor_scalar_mul(self.st_m2[:], self.st_m2[:], 1.0 / D), waits=[mred])
        mC = self.V(lambda e: e.tensor_mul(self.st_t2[:], self.st_t1[:], self.st_t1[:]), waits=[mA])
        mD = self.V(lambda e: e.tensor_sub(self.st_t3[:], self.st_m2[:], self.st_t2[:]), waits=[mB, mC])
        mD2 = self.V(lambda e: e.tensor_scalar_add(self.st_t3[:], self.st_t3[:], 1e-5), waits=[mD])
        mE = self.Sc(lambda e: e.activation(self.st_t2[:], self.st_t3[:], AF.Ln), waits=[mD2])
        mF = self.Sc(lambda e: e.activation(self.st_istd[:], self.st_t2[:], AF.Exp, scale=-0.5), waits=[mE])
        mG = self.V(lambda e: e.scalar_tensor_tensor(
            self.st_nmi[:], self.st_t1[:], -1.0, self.st_istd[:], ALU.mult, ALU.mult), waits=[mF])
        cps = []
        half0 = None
        last_tr = [None, None]
        maff = None
        for r in range(8):
            wa = [mG] + [m for m in last_tr if m] + ([maff] if maff else [])
            maff = self.V((lambda r, xlntm: lambda e: e.tensor_scalar(
                xlntm[:], self.xs[r][:], self.st_istd[:, r:r + 1],
                self.st_nmi[:, r:r + 1], ALU.mult, ALU.add))(r, xlntm), waits=wa)
            for dt in range(16):
                half = dt % 2
                wt = [maff, self.ident_ready, self.pbf_free[half]]
                mt = self.T((lambda dt, half, xlntm: lambda e: e.transpose(
                    self.pbf[:, half * 1024:half * 1024 + 128],
                    xlntm[:, dt * 128:(dt + 1) * 128], self.ident[:]))(dt, half, xlntm), waits=wt)
                if dt >= 14:
                    last_tr[half] = mt
                mc = self.V((lambda r, dt, half, slot: lambda e: e.tensor_copy(
                    slot[:, dt * KB + r * 128: dt * KB + (r + 1) * 128],
                    self.pbf[:, half * 1024:half * 1024 + 128]))(r, dt, half, slot), waits=[mt])
                self.pbf_free[half] = mc
                cps.append(mc)
            if r == 3:
                half0 = mmax(*cps)
        self.xln_ready = mmax(*cps)
        self.xln_half = [half0, self.xln_ready]
        self.phase_end = self.S.barrier_marks()

    def xlnT(self, for_attn, dt):
        slot = self.slot1 if for_attn else self.slot2
        return slot[:, dt * KB:(dt + 1) * KB]

    def qt_ot(self, h):
        return self.slot2[:, h * KB:(h + 1) * KB]

    def h_tile(self, it):
        if it < 32:
            return self.slot1[:, it * 512:(it + 1) * 512]
        return self.h_extra[:, (it - 32) * 512:(it - 31) * 512]

    def _proj16(self, wh, E, et, buf, dest_cb, scale, bias_col, for_attn=True):
        mload = self.G((lambda buf, wh, E, et: lambda e: e.dma_start(
            out=self.wcol[buf][:],
            in_=bass.AP(wh, et * 128, [[E, 128], [128 * E, 16], [1, 128]])))(buf, wh, E, et),
            waits=[self.wcol_free[buf]], stream=f'g_wc{buf}')
        mm = None
        evs = []
        for qc in range(2):
            bank = qc
            wt = [mload, self.xln_half[qc], self.bank_free[bank]]
            for dt in range(16):
                mm = self.T((lambda buf, dt, bank, qc: lambda e: e.matmul(
                    self.pbk(bank), self.wcol[buf][:, dt * 128:(dt + 1) * 128],
                    self.xlnT(True, dt)[:, qc * 512:(qc + 1) * 512],
                    start=(dt == 0), stop=(dt == 15)))(buf, dt, bank, qc),
                    waits=wt if dt == 0 else ())
            me = dest_cb(qc, bank, mm)
            self.bank_free[bank] = me
            evs.append(me)
        self.wcol_free[buf] = mm
        return evs

    def attn(self, variant):
        S = self.S
        a = variant == 'a'
        nKV = 16 if a else 4
        W = self.a_w if a else self.b_w
        kvb, kvg = self.kvb[nKV], self.kvg[nKV]
        E_kv = D if a else 512
        qscale = SC_A if a else SC_B
        kscale = SC_A if a else 1.0
        shift = SHIFT_A if a else SHIFT_B

        S.begin_phase(mmax(self.phase_end))
        mconst = self.G(lambda e: e.dma_start(
            out=self.amask_sb[:],
            in_=bass.AP(self.amask_in, 0, [[128, 128], [16384, 2], [1, 128]])))
        if not a:
            mconst = mmax(mconst, self.G(lambda e: e.dma_start(
                out=self.bbias_sb[:], in_=self.bbias_in[:, :])))
        m_ones = self.V(lambda e: e.memset(
            bass.AP(self.vsb, 128, [[4 * 16 * 129, 128], [129, 64], [1, 1]]), 1.0))

        base_tag = self.S.tag
        self.S.tag = base_tag + ":kv"
        wi = 0
        store_marks = []
        ktmp_free = None
        vttmp_free = None
        for kv in range(nKV):
            def k_dest(qc, bank, mm, kv=kv):
                if a:
                    bld = (lambda bank, qc, kv: lambda e: e.tensor_scalar(
                        self.ktmp[:, qc * 512:(qc + 1) * 512], self.pbk(bank), kscale,
                        self.bks[:, kv:kv + 1], ALU.mult, ALU.add))(bank, qc, kv)
                else:
                    bld = (lambda bank, qc: lambda e: e.tensor_scalar(
                        self.ktmp[:, qc * 512:(qc + 1) * 512], self.pbk(bank), kscale,
                        None, ALU.mult))(bank, qc)
                return self.V(bld, waits=[mm, ktmp_free, self.const_ready])
            evk = self._proj16(W['k'], E_kv, kv, wi % 2, k_dest, kscale, kv)
            wi += 1
            ms = self.Y((lambda kv: lambda e: e.dma_start(
                out=bass.AP(kvb, kv * 131072, [[1024, 128], [1, 1024]]),
                in_=self.ktmp[:]))(kv), waits=evk, stream='y_k')
            ktmp_free = ms
            store_marks.append(ms)

            def v_dest(qc, bank, mm, kv=kv):
                if a:
                    bld = (lambda bank, qc, kv: lambda e: e.tensor_scalar(
                        self.vtmp[:, qc * 512:(qc + 1) * 512], self.pbk(bank), 1.0,
                        self.bvs[:, kv:kv + 1], ALU.mult, ALU.add))(bank, qc, kv)
                else:
                    bld = (lambda bank, qc: lambda e: e.tensor_copy(
                        self.vtmp[:, qc * 512:(qc + 1) * 512], self.pbk(bank)))(bank, qc)
                return self.V(bld, waits=[mm, self.const_ready])
            evv = self._proj16(W['v'], E_kv, kv, wi % 2, v_dest, 1.0, kv)
            wi += 1
            vt = []
            for j in range(8):
                half = j % 2
                wt = [mmax(*evv), self.pbf_free[half], self.ident_ready]
                mt = self.T((lambda j, half: lambda e: e.transpose(
                    self.pbf[:, half * 1024:half * 1024 + 128],
                    self.vtmp[:, j * 128:(j + 1) * 128], self.ident[:]))(j, half), waits=wt)
                mc = self.V((lambda j, half: lambda e: e.tensor_copy(
                    self.vttmp[:, j * 128:(j + 1) * 128],
                    self.pbf[:, half * 1024:half * 1024 + 128]))(j, half),
                    waits=[mt, vttmp_free])
                self.pbf_free[half] = mc
                vt.append(mc)
            ms = self.Y((lambda kv, nKV: lambda e: e.dma_start(
                out=bass.AP(kvb, (nKV + kv) * 131072, [[128, 128], [16384, 8], [1, 128]]),
                in_=self.vttmp[:].rearrange("p (j d) -> p j d", j=8)))(kv, nKV),
                waits=[mmax(*vt)], stream='y_v')
            vttmp_free = ms
            store_marks.append(ms)

        mcc = S.emit('G', (lambda kvb, kvg: lambda e: e.collective_compute(
            "AllGather", ALU.bypass, replica_groups=[[0, 1], [2, 3], [4, 5], [6, 7]],
            ins=[kvb.ap()], outs=[kvg.ap()]))(kvb, kvg),
            waits=[mmax(*store_marks)], sem='cc', amt=1)

        self.S.tag = base_tag + ":q"
        qev = []
        for et in range(16):
            def q_dest(qc, bank, mm, et=et):
                if a:
                    bld = (lambda bank, qc, et: lambda e: e.tensor_scalar(
                        self.qt_ot(et)[:, qc * 512:(qc + 1) * 512], self.pbk(bank), qscale,
                        self.bqs[:, et:et + 1], ALU.mult, ALU.add))(bank, qc, et)
                else:
                    bld = (lambda bank, qc, et: lambda e: e.tensor_scalar(
                        self.qt_ot(et)[:, qc * 512:(qc + 1) * 512], self.pbk(bank), qscale,
                        None, ALU.mult))(bank, qc, et)
                return self.V(bld, waits=[mm, self.const_ready])
            qev += self._proj16(W['q'], D, et, wi % 2, q_dest, qscale, et)
            wi += 1
        q_ready = mmax(*qev)
        if not a:
            self.G(lambda e: e.dma_start(out=self.u2_sb[:], in_=self.u2_in[:, :]),
                   waits=[self.wcol_free[0], self.wcol_free[1]])
            self.G(lambda e: e.dma_start(out=self.v2_sb[:], in_=self.v2_in[:, :]),
                   waits=())
            alibi_ready = self.S.mark('g_cst')
            mconst = alibi_ready
            self.wcol_free = [None, None]
        else:
            alibi_ready = None

        grp_cons = {}
        slot_cons = {}
        ot_marks = []
        self.S.tag = base_tag + ":heads"
        for hg in range(4):
            par = hg % 2
            RNK = 2 * nKV * 131072
            gl_slot = [None] * 4
            if a:
                for s in range(4):
                    wl = [mcc, m_ones, slot_cons.get((hg - 1, s))]
                    ms1 = self.Y((lambda s, hg: lambda e: e.dma_start(
                        out=bass.AP(self.ksb, s * 2048, [[4 * D, 128], [1024, 2], [1, 1024]]),
                        in_=bass.AP(kvg, 0 * RNK + (4 * hg + s) * 131072,
                                    [[1024, 128], [RNK, 2], [1, 1024]])))(s, hg),
                        waits=wl, stream=f'y_s{s}')
                    ms2 = [self.Y((lambda s, hg, r: lambda e: e.dma_start(
                        out=bass.AP(self.vsb, s * 16 * 129 + r * 8 * 129,
                                    [[4 * 16 * 129, 128], [129, 8], [1, 128]]),
                        in_=bass.AP(kvg, r * RNK + (nKV + 4 * hg + s) * 131072,
                                    [[128, 128], [16384, 8], [1, 128]])))(s, hg, r),
                        waits=wl, stream=f'y_s{s}') for r in range(2)]
                    gl_slot[s] = mmax(ms1, *ms2)
            else:
                wl = [mcc, m_ones, grp_cons.get(hg - 2)]
                lms = []
                for r in range(2):
                    lms.append(self.Y((lambda par, r, hg: lambda e: e.dma_start(
                        out=bass.AP(self.ksb, par * 2048 + r * 1024, [[4 * D, 128], [1, 1024]]),
                        in_=bass.AP(kvg, r * RNK + hg * 131072, [[1024, 128], [1, 1024]])))(par, r, hg),
                        waits=wl, stream=f'y_g{hg % 2}'))
                    lms.append(self.Y((lambda par, r, hg: lambda e: e.dma_start(
                        out=bass.AP(self.vsb, par * 16 * 129 + r * 8 * 129,
                                    [[4 * 16 * 129, 128], [129, 8], [1, 128]]),
                        in_=bass.AP(kvg, r * RNK + (nKV + hg) * 131072,
                                    [[128, 128], [16384, 8], [1, 128]])))(par, r, hg),
                        waits=wl, stream=f'y_g{hg % 2}'))
                gl_slot = [mmax(*lms)] * 4
            cons = []
            for hi in range(4):
                h = 4 * hg + hi
                kcol = hi * 2048 if a else par * 2048
                vbase = hi * 16 * 129 if a else par * 16 * 129
                gl = gl_slot[hi]
                head_cons = []
                for qc in range(2):
                    issued = []
                    for g in range(4 * qc + 4):
                        issued += [g, g + 8]
                    nI = len(issued)
                    nP = nI // 2
                    gbs = [t % 8 for t in issued]
                    mex = {}
                    av_sched = [(idx, b) for idx in range(nI)
                                for b in range(max(0, gbs[idx] - 4 * qc), 4)]
                    bank_first = {}
                    bank_last = {}
                    for pos, (idx, b) in enumerate(av_sched):
                        j = b // 2
                        if j not in bank_first:
                            bank_first[j] = (idx, b)
                        bank_last[j] = (idx, b)
                    av_stop_bank = [None, None]

                    def do_scores_pair(p):
                        bp = p % 2
                        tens = self.pb01 if bp == 0 else self.pb23
                        mprevs = []
                        t = issued[2 * p]
                        gb = t % 8
                        b0 = max(0, gb - 4 * qc)
                        nq = 512 - b0 * 128
                        for k in (0, 1):
                            idx = 2 * p + k
                            t = issued[idx]
                            rk = t // 8
                            bank = 2 * bp + k
                            wt = [gl, q_ready, self.bank_free[bank]]
                            if not a:
                                wt.append(alibi_ready)
                            mm = self.T((lambda qc, t, bank, kcol, h, b0, nq: lambda e: e.matmul(
                                self.pbk(bank, b0 * 128, b0 * 128 + nq),
                                self.ksb[:, kcol + (t // 8) * 1024 + (t % 8) * 128:
                                         kcol + (t // 8) * 1024 + (t % 8) * 128 + 128],
                                self.qt_ot(h)[:, qc * 512 + b0 * 128:(qc + 1) * 512],
                                start=True, stop=a))(qc, t, bank, kcol, h, b0, nq), waits=wt)
                            if not a:
                                ucol = (rk * 16 + h) * 128
                                mm = self.T((lambda qc, bank, ucol, b0, nq: lambda e: e.matmul(
                                    self.pbk(bank, b0 * 128, b0 * 128 + nq),
                                    self.u2_sb[:, ucol: ucol + 128],
                                    self.v2_sb[:, qc * 512 + b0 * 128:(qc + 1) * 512],
                                    start=False, stop=True))(qc, bank, ucol, b0, nq), waits=())
                            head_cons.append(mm)
                            mprev = mm
                            b_blk = gb - 4 * qc
                            if 0 <= b_blk <= 3:
                                mprev = self.V((lambda bank, rk, b_blk: lambda e: e.tensor_add(
                                    self.pbk(bank, b_blk * 128, (b_blk + 1) * 128),
                                    self.pbk(bank, b_blk * 128, (b_blk + 1) * 128),
                                    self.amask_sb[:, rk * 128:(rk + 1) * 128]))(bank, rk, b_blk),
                                    waits=[mm, mconst])
                            mprevs.append(mprev)
                        psA = (2 * p) % 8
                        if a:
                            bias_ap = self.fc_sb[:, 0:1]
                        else:
                            bias_ap = self.bbias_sb[:, h * 8 + gb: h * 8 + gb + 1]
                        m = self.Sc((lambda tens, bp, psA, bias_ap, b0, nq: lambda e: e.activation(
                            bass.AP(self.pt_sb, psA * 512 + b0 * 128, [[4096, 128], [512, 2], [1, nq]]),
                            bass.AP(tens, b0 * 128, [[1024, 128], [512, 2], [1, nq]]),
                            AF.Exp, bias=bias_ap))(tens, bp, psA, bias_ap, b0, nq),
                            waits=mprevs + [self.pt_free[psA], self.pt_free[psA + 1],
                                            self.const_ready, mconst])
                        self.bank_free[2 * bp] = m
                        self.bank_free[2 * bp + 1] = m
                        mex[p] = m

                    def do_av_pair(p):
                        for k in (0, 1):
                            idx = 2 * p + k
                            t = issued[idx]
                            gb = t % 8
                            ps = idx % 8
                            last_av = None
                            for b in range(max(0, gb - 4 * qc), 4):
                                j = b // 2
                                col0 = (b % 2) * 256
                                first = bank_first[j] == (idx, b)
                                last = bank_last[j] == (idx, b)
                                wt = [mex[p]]
                                if first:
                                    wt.append(self.av_free[j])
                                last_av = self.T((lambda j, col0, ps, vbase, t, first, last, b:
                                                  lambda e: e.matmul(
                                    self.pbav[j][:, col0: col0 + 129],
                                    self.pt_sb[:, ps * 512 + b * 128: ps * 512 + (b + 1) * 128],
                                    self.vsb[:, vbase + t * 129: vbase + (t + 1) * 129],
                                    start=first, stop=last))(j, col0, ps, vbase, t, first, last, b),
                                    waits=wt)
                                if last:
                                    av_stop_bank[j] = last_av
                            self.pt_free[ps] = last_av
                            head_cons.append(last_av)

                    for p in range(nP):
                        do_scores_pair(p)
                        if p >= 2:
                            do_av_pair(p - 2)
                    if nP >= 2:
                        do_av_pair(nP - 2)
                    do_av_pair(nP - 1)
                    for b in range(4):
                        j, col0 = b // 2, (b % 2) * 256
                        self.V((lambda b, j, col0: lambda e: e.reciprocal(
                            self.rd_sb[:, b:b + 1], self.pbav[j][:, col0 + 128: col0 + 129]))(b, j, col0),
                            waits=[av_stop_bank[j]])
                    mj = self.V(lambda e: e.tensor_copy(self.junk[:], self.ident[:]),
                                waits=[self.ident_ready])
                    mos = [None, None]
                    for b in range(4):
                        j, col0 = b // 2, (b % 2) * 256
                        mo = self.V((lambda b, j, col0: lambda e: e.tensor_scalar(
                            self.osb[:, b * 128:(b + 1) * 128], self.pbav[j][:, col0: col0 + 128],
                            self.rd_sb[:, b:b + 1], None, ALU.mult))(b, j, col0), waits=[mj])
                        mos[j] = mo
                        half = b % 2
                        mt = self.T((lambda b, half: lambda e: e.transpose(
                            self.pbf[:, half * 1024:half * 1024 + 128],
                            self.osb[:, b * 128:(b + 1) * 128], self.ident[:]))(b, half),
                            waits=[mo, self.ident_ready, self.pbf_free[half]])
                        mc = self.V((lambda h, qc, b, half: lambda e: e.tensor_copy(
                            self.qt_ot(h)[:, qc * 512 + b * 128: qc * 512 + (b + 1) * 128],
                            self.pbf[:, half * 1024:half * 1024 + 128]))(h, qc, b, half), waits=[mt])
                        self.pbf_free[half] = mc
                        ot_marks.append(mc)
                    self.av_free = [mos[0], mos[1]]
                slot_cons[(hg, hi)] = mmax(*head_cons)
                cons += head_cons
            grp_cons[hg] = mmax(*cons)
        ot_ready = mmax(*ot_marks)

        self.S.tag = base_tag + ":proj"
        Wo = W['o']
        for ech in range(4):
            wf = [self.wocol_free[ech % 2]]
            if ech < 2:
                wf.append(grp_cons[2 + ech])
                wf.append(grp_cons[3])
            mw = self.G((lambda ech, Wo: lambda e: e.dma_start(
                out=self.wocol[ech % 2][:],
                in_=bass.AP(Wo, ech * 512, [[D, 128], [128 * D, 16], [1, 512]])))(ech, Wo),
                waits=wf, stream=f'g_wo{ech % 2}')
            mmlast = None
            for qt in range(8):
                bank = qt % 4
                wt = [mw, ot_ready, self.bank_free[bank]]
                if a:
                    wt.append(self.const_ready)
                mm = None
                for dt in range(16):
                    stop = (dt == 15) and not a
                    mm = self.T((lambda ech, qt, dt, bank, stop: lambda e: e.matmul(
                        self.pbk(bank), self.qt_ot(dt)[:, qt * 128:(qt + 1) * 128],
                        self.wocol[ech % 2][:, dt * 512:(dt + 1) * 512],
                        start=(dt == 0), stop=stop))(ech, qt, dt, bank, stop),
                        waits=wt if dt == 0 else ())
                if a:
                    mm = self.T((lambda ech, bank: lambda e: e.matmul(
                        self.pbk(bank), self.ones_row[:, 0:128],
                        self.bo_row[:, ech * 512:(ech + 1) * 512],
                        start=False, stop=True))(ech, bank), waits=())
                mmlast = mm
                mx = self.V((lambda qt, ech, bank: lambda e: e.tensor_add(
                    self.xs[qt][:, ech * 512:(ech + 1) * 512],
                    self.xs[qt][:, ech * 512:(ech + 1) * 512],
                    self.pbk(bank)))(qt, ech, bank), waits=[mm, self.x_marks[qt]])
                self.bank_free[bank] = mx
                self.x_marks[qt] = mx
            self.wocol_free[ech % 2] = mmlast
        self.phase_end = self.S.barrier_marks()

    def ffn(self):
        S = self.S
        S.begin_phase(mmax(self.phase_end))
        sg_free = None
        for qc in range(2):
            hm = [None] * 64
            hsum_stop = [None] * 4

            def emit_hsum(it):
                for qt in range(4):
                    j, col0 = qt // 2, (qt % 2) * 256
                    wt2 = [hm[it]]
                    if it == 0:
                        wt2 += [self.av_free[j], self.const_ready]
                    hsum_stop[qt] = self.T((lambda it, qt, j, col0: lambda e: e.matmul(
                        self.pbav[j][:, col0: col0 + 1],
                        self.h_tile(it)[:, qt * 128:(qt + 1) * 128],
                        self.ones_col[:],
                        start=(it == 0 and qt % 2 == 0),
                        stop=(it == 63 and qt % 2 == 1)))(it, qt, j, col0), waits=wt2)

            first = (self.ffn_pass == 0 and qc == 0)
            base_tag = self.S.tag.split(":gu")[0].split(":dn")[0]
            self.S.tag = base_tag + f":gu{qc}"
            for ip in range(32):
                buf = ip % 2
                if first:
                    mgl = self.G((lambda buf, ip: lambda e: e.dma_start(
                        out=self.gbuf[buf][:],
                        in_=bass.AP(self.g_wt, ip * 256, [[I_FF, 128], [128 * I_FF, 16], [1, 256]])))(buf, ip),
                        waits=[self.gbuf_free[buf]], stream=f'g_gb{buf}')
                    mul_ = self.G((lambda buf, ip: lambda e: e.dma_start(
                        out=self.ubuf[buf][:],
                        in_=bass.AP(self.u_wt, ip * 256, [[I_FF, 128], [128 * I_FF, 16], [1, 256]])))(buf, ip),
                        waits=[self.ubuf_free[buf]], stream=f'g_ub{buf}')
                    sg1 = self.Y((lambda buf, ip: lambda e: e.dma_start(
                        out=bass.AP(self.gw_bf, ip * 524288, [[4096, 128], [1, 4096]]),
                        in_=self.gbuf[buf][:]))(buf, ip), waits=[mgl], stream=f'y_cg{buf}')
                    sg2 = self.Y((lambda buf, ip: lambda e: e.dma_start(
                        out=bass.AP(self.uw_bf, ip * 524288, [[4096, 128], [1, 4096]]),
                        in_=self.ubuf[buf][:]))(buf, ip), waits=[mul_], stream=f'y_cu{buf}')
                    self.gu_conv[ip] = (sg1, sg2)
                else:
                    mgl = self.G((lambda buf, ip: lambda e: e.dma_start(
                        out=self.gbuf[buf][:],
                        in_=bass.AP(self.gw_bf, ip * 524288, [[4096, 128], [1, 4096]])))(buf, ip),
                        waits=[self.gbuf_free[buf], self.gu_conv[ip][0]], stream=f'g_gb{buf}')
                    mul_ = self.G((lambda buf, ip: lambda e: e.dma_start(
                        out=self.ubuf[buf][:],
                        in_=bass.AP(self.uw_bf, ip * 524288, [[4096, 128], [1, 4096]])))(buf, ip),
                        waits=[self.ubuf_free[buf], self.gu_conv[ip][1]], stream=f'g_ub{buf}')
                glast = ulast = None
                for k in range(2):
                    it = ip * 2 + k
                    bg = (it % 2) * 2
                    bu = bg + 1
                    wt = [mgl, self.xln_half[qc], self.bank_free[bg]]
                    for dt in range(16):
                        glast = self.T((lambda buf, dt, k, qc, bg: lambda e: e.matmul(
                            self.pbk(bg),
                            self.gbuf[buf][:, dt * 256 + k * 128: dt * 256 + (k + 1) * 128],
                            self.xlnT(False, dt)[:, qc * 512:(qc + 1) * 512],
                            start=(dt == 0), stop=(dt == 15)))(buf, dt, k, qc, bg),
                            waits=wt if dt == 0 else ())
                    wt = [mul_, self.xln_half[qc], self.bank_free[bu]]
                    for dt in range(16):
                        ulast = self.T((lambda buf, dt, k, qc, bu: lambda e: e.matmul(
                            self.pbk(bu),
                            self.ubuf[buf][:, dt * 256 + k * 128: dt * 256 + (k + 1) * 128],
                            self.xlnT(False, dt)[:, qc * 512:(qc + 1) * 512],
                            start=(dt == 0), stop=(dt == 15)))(buf, dt, k, qc, bu),
                            waits=wt if dt == 0 else ())
                    msg = self.Sc((lambda it, bg: lambda e: e.activation(
                        self.sg_sb[:], self.pbk(bg), AF.Silu,
                        scale=self.gsc_sb[:, it:it + 1]))(it, bg), waits=[glast, sg_free])
                    self.bank_free[bg] = msg
                    mh = self.V((lambda it, bu: lambda e: e.scalar_tensor_tensor(
                        self.h_tile(it), self.pbk(bu), self.usc_sb[:, it:it + 1],
                        self.sg_sb[:], ALU.mult, ALU.mult))(it, bu), waits=[ulast, msg])
                    self.bank_free[bu] = mh
                    sg_free = mh
                    hm[it] = mh
                    if it >= 1:
                        emit_hsum(it - 1)
                self.gbuf_free[buf] = mmax(glast, self.gu_conv[ip][0] if first else None)
                self.ubuf_free[buf] = mmax(ulast, self.gu_conv[ip][1] if first else None)
            emit_hsum(63)
            hneg_m = []
            for qt in range(4):
                j, col0 = qt // 2, (qt % 2) * 256
                hneg_m.append(self.Sc((lambda qc, qt, j, col0: lambda e: e.mul(
                    self.hneg[:, qc * 4 + qt: qc * 4 + qt + 1],
                    self.pbav[j][:, col0: col0 + 1], -128.0))(qc, qt, j, col0),
                    waits=[hsum_stop[2 * j + 1]]))
            self.av_free = [hneg_m[1], hneg_m[3]]
            mhneg = mmax(*hneg_m)
            h_ready = mmax(*hm)
            self.S.tag = base_tag + f":dn{qc}"
            mdsc = self.G(lambda e: e.dma_start(out=self.dsc_sb[:], in_=self.dsc_in[:, :]),
                          waits=[h_ready, self.ubuf_free[0]], stream='g_dsc')
            wdbuf_free = [mmax(self.gbuf_free[0], h_ready), mmax(self.gbuf_free[1], h_ready)]
            t1_free = self.ubuf_free[1]
            t1b_free = self.ubuf_free[1]
            for ech in range(4):
                stops = [None] * 4
                for ic in range(8):
                    buf = ic % 2
                    if first:
                        mwl = self.G((lambda buf, ic, ech: lambda e: e.dma_start(
                            out=self.wdbuf[buf][:],
                            in_=bass.AP(self.d_wt, (ic * 8 * 128) * D + ech * 512,
                                        [[D, 128], [128 * D, 8], [1, 512]])))(buf, ic, ech),
                            waits=[wdbuf_free[buf]], stream=f'g_wd{buf}')
                        scv = self.Y((lambda buf, ic, ech: lambda e: e.dma_start(
                            out=bass.AP(self.dw_bf, (ech * 8 + ic) * 524288,
                                        [[4096, 128], [1, 4096]]),
                            in_=self.wdbuf[buf][:]))(buf, ic, ech), waits=[mwl], stream=f'y_cd{buf}')
                        self.wd_conv[(ech, ic)] = scv
                    else:
                        mwl = self.G((lambda buf, ic, ech: lambda e: e.dma_start(
                            out=self.wdbuf[buf][:],
                            in_=bass.AP(self.dw_bf, (ech * 8 + ic) * 524288,
                                        [[4096, 128], [1, 4096]])))(buf, ic, ech),
                            waits=[wdbuf_free[buf], self.wd_conv[(ech, ic)]], stream=f'g_wd{buf}')
                    for i8 in range(8):
                        it = ic * 8 + i8
                        for qt in range(4):
                            if it == 0:
                                wt = [mwl, h_ready, mhneg, self.bank_free[qt]]
                            elif i8 == 0:
                                wt = [mwl]
                            else:
                                wt = ()
                            stops[qt] = self.T((lambda buf, it, i8, qt: lambda e: e.matmul(
                                self.pbk(qt), self.h_tile(it)[:, qt * 128:(qt + 1) * 128],
                                self.wdbuf[buf][:, i8 * 512:(i8 + 1) * 512],
                                start=(it == 0), stop=(it == 63)))(buf, it, i8, qt),
                                waits=wt)
                    wdbuf_free[buf] = mmax(stops[3], self.wd_conv.get((ech, ic)) if first else None)
                for qt in range(4):
                    rt = qc * 4 + qt
                    m1 = self.Sc((lambda qt, qc: lambda e: e.add(
                        self.t1_sb[:], self.pbk(qt),
                        self.hneg[:, qc * 4 + qt: qc * 4 + qt + 1]))(qt, qc),
                        waits=[stops[qt], mhneg, t1_free])
                    self.bank_free[qt] = m1
                    m2 = self.V((lambda ech: lambda e: e.tensor_mul(
                        self.t1b_sb[:], self.t1_sb[:],
                        self.dsc_sb[:, ech * 512:(ech + 1) * 512]))(ech),
                        waits=[m1, mdsc, t1b_free])
                    t1_free = m2
                    m3 = self.V((lambda rt, ech: lambda e: e.tensor_add(
                        self.xs[rt][:, ech * 512:(ech + 1) * 512],
                        self.xs[rt][:, ech * 512:(ech + 1) * 512],
                        self.t1b_sb[:]))(rt, ech), waits=[m2, self.x_marks[rt]])
                    t1b_free = m3
                    self.x_marks[rt] = m3
            self.gbuf_free = [wdbuf_free[0], wdbuf_free[1]]
            self.ubuf_free = [mmax(t1_free), mmax(t1_free, t1b_free)]
        self.ffn_pass += 1
        self.phase_end = self.S.barrier_marks()

    def finish(self):
        self.S.begin_phase(mmax(self.phase_end))
        for r in range(8):
            self.Y((lambda r: lambda e: e.dma_start(
                out=self.out_t[r], in_=self.xs[r][:]))(r), waits=[self.x_marks[r]])
        self.S.emit_wait_only('Y', [self.S.mark('y_x')])

    def realize(self):
        nc, S = self.nc, self.S
        import contextlib
        with contextlib.ExitStack() as st:
            sems = {s: st.enter_context(nc.semaphore(s)) for s in ALL_SEMS}
            blk = st.enter_context(nc.Block())
            attr = {'T': 'tensor', 'V': 'vector', 'S': 'scalar', 'Y': 'sync', 'G': 'gpsimd'}
            for ename in ENGINES:
                def runner(eng, ename=ename):
                    hiwater = {}
                    for (e2, waits, builder, inc) in S.ops:
                        if e2 != ename:
                            continue
                        for (s, v) in waits:
                            if hiwater.get(s, -1) >= v:
                                continue
                            hiwater[s] = v
                            eng.wait_ge(sems[s], v)
                        if builder is None:
                            continue
                        ins = builder(eng)
                        if inc is not None:
                            ins.then_inc(sems[inc[0]], inc[1])
                getattr(blk, attr[ename])(runner)
        return nc


def build(n_sub=4):
    b = Builder(n_sub)
    b.init_loads()
    steps = [('ln_a',), ('attn', 'a'), ('ln_f',), ('ffn',), ('ln_a',), ('attn', 'b'),
             ('ln_f',), ('ffn',)][: 2 * n_sub]
    for i, st in enumerate(steps):
        b.S.tag = f"{i}:{st[0]}" + (st[1] if len(st) > 1 else "")
        if st[0] == 'ln_a':
            b.ln(True)
        elif st[0] == 'ln_f':
            b.ln(False)
        elif st[0] == 'attn':
            b.attn(st[1])
        else:
            b.ffn()
    b.finish()
    return b.realize()


def host_consts(P):
    d = {}
    d["ident"] = np.eye(128, dtype=BF)
    kp = np.arange(128)[:, None]
    qp = np.arange(128)[None, :]
    am = np.zeros((2, 128, 128), np.float32)
    for rk in range(2):
        am[rk] = np.where((2 * kp + rk) <= (2 * qp + P), 0.0, NEG)
    d["amask"] = am.astype(BF)
    slopes = np.repeat([2.0 ** (-0.5 * (i + 1)) for i in range(4)], 4)[:16]

    def hilo(x):
        hi = x.astype(BF).astype(np.float32)
        lo = (x - hi).astype(BF).astype(np.float32)
        return hi, lo

    u2t = np.zeros((6, 2 * 16 * 128), np.float32)
    for rk in range(2):
        for h in range(16):
            c = (rk * 16 + h) * 128
            u1h, u1l = hilo(slopes[h] * (2.0 * np.arange(128, dtype=np.float64) + rk))
            s2h, s2l = hilo(np.full(128, slopes[h], np.float64))
            u2t[0, c:c + 128] = u1h
            u2t[1, c:c + 128] = u1l
            u2t[2, c:c + 128] = s2h
            u2t[3, c:c + 128] = s2h
            u2t[4, c:c + 128] = s2l
            u2t[5, c:c + 128] = s2l
    d["b_u2"] = u2t.astype(BF)
    v2 = np.zeros((6, 2 * 512), np.float32)
    v2[0, :] = 1.0
    v2[1, :] = 1.0
    for qc in range(2):
        q = np.arange(512)
        w2 = -(256.0 * (4 * qc + q // 128) + 2.0 * (q % 128) + P)
        wh, wl = hilo(w2.astype(np.float64))
        v2[2, qc * 512:(qc + 1) * 512] = wh
        v2[3, qc * 512:(qc + 1) * 512] = wl
        v2[4, qc * 512:(qc + 1) * 512] = wh
        v2[5, qc * 512:(qc + 1) * 512] = wl
    d["b_v2"] = v2.astype(BF)
    bb = np.zeros((128, 128), np.float32)
    for h in range(16):
        for gb in range(8):
            bb[:, h * 8 + gb] = -SHIFT_B + slopes[h] * 256.0 * gb
    d["b_bias"] = bb
    return d


def host_weights(z):
    d = {}
    d["a_wq"] = np.ascontiguousarray(z["L0_attn_a_Wq"].T)
    d["a_wk"] = np.ascontiguousarray(z["L0_attn_a_Wk"].T)
    d["a_wv"] = np.ascontiguousarray(z["L0_attn_a_Wv"].T)
    d["a_wo"] = np.ascontiguousarray(z["L0_attn_a_Wo"].T)
    d["a_bq"] = z["L0_attn_a_bq"]
    d["a_bk"] = z["L0_attn_a_bk"]
    d["a_bv"] = z["L0_attn_a_bv"]
    d["a_bo"] = z["L0_attn_a_bo"]
    d["b_wq"] = z["L0_attn_b_Wq"]
    d["b_wk"] = z["L0_attn_b_Wk"]
    d["b_wv"] = z["L0_attn_b_Wv"]
    d["b_wo"] = z["L0_attn_b_Wo"]
    d["g_wt"] = np.ascontiguousarray(z["L0_ffn_gate_w"].T)
    d["u_wt"] = np.ascontiguousarray(z["L0_ffn_up_w"].T)
    d["d_wt"] = np.ascontiguousarray(z["L0_ffn_down_w"].T)
    d["gsc"] = z["L0_ffn_gate_scale"]
    d["usc"] = z["L0_ffn_up_scale"]
    d["dsc_b"] = np.ascontiguousarray(np.broadcast_to(z["L0_ffn_down_scale"], (128, D)))
    return d


_NC_CACHE = {}


def run(inputs, n_sub=4):
    if n_sub not in _NC_CACHE:
        _NC_CACHE[n_sub] = build(n_sub)
    nc = _NC_CACHE[n_sub]
    X = inputs["L0_attn_a_X"]
    w = host_weights(inputs)
    consts = [host_consts(0), host_consts(1)]
    in_maps = []
    for c in range(8):
        b, P = c // 2, c % 2
        d = dict(w)
        d.update(consts[P])
        d["x_in"] = np.ascontiguousarray(X[b].reshape(8, 128, 2, D)[:, :, P])
        in_maps.append(d)
    res = run_bass_kernel_spmd(nc, in_maps, core_ids=list(range(8)))
    out = np.empty((4, 2048, D), np.float32)
    ov = out.reshape(4, 8, 128, 2, D)
    for c in range(8):
        b, P = c // 2, c % 2
        ov[b, :, :, P] = res.results[c]["out"]
    return out


def kernel(**inputs):
    return run(inputs, n_sub=4)
